# version 10
# baseline (speedup 1.0000x reference)
import numpy as np
import ml_dtypes
import concourse.bass as bass
import concourse.mybir as mybir
from concourse.bass_utils import run_bass_kernel_spmd
from contextlib import ExitStack

F32 = mybir.dt.float32
BF16 = mybir.dt.bfloat16
I32 = mybir.dt.int32
AF = mybir.ActivationFunctionType
ALU = mybir.AluOpType

B, T, C, HID = 4, 8192, 1024, 3072
TCA = 512
NCA = T // TCA
OWN = T // 2
TCB = 256
NCB = OWN // TCB
NCC = C // 128
NDB = C // 128
NHB = HID // 128
EPS = 1e-5
NEWTON_ITERS = 8
SEMS = ["lds", "ldp0", "ldp1", "ldx0", "ldx1", "ldxn0", "ldxn1", "ldy", "ldxo", "ldwro",
        "ldffnw", "ldos", "stp0", "stp1", "sty", "stx", "sto0", "sto1",
        "std", "v", "s", "t"]
DEBUG_TAPS = False


class Ctx:
    def __init__(self, nc, M):
        self.nc = nc
        self.M = M
        self.newM = {}
        self.cnt = {k: 0 for k in SEMS}
        self.sem = {}
        self.pass1 = not M

    def mark(self, key, semname):
        self.newM[key] = self.cnt[semname]

    def bump(self, inst, semname, n=1):
        inst.then_inc(self.sem[semname], n)
        self.cnt[semname] += n

    def wait(self, eng, key, semname):
        if self.pass1:
            return
        val = self.M.get(key)
        assert val is not None, f"missing milestone {key}"
        if val > 0:
            eng.wait_ge(self.sem[semname], val)

    def tap(self, g, name, src_ap, shape, dtype):
        t = self.nc.dram_tensor("tap_" + name, shape, dtype, kind="ExternalOutput").ap()
        self.bump(g.dma_start(out=t, in_=src_ap), "std", 16)
        self.mark("tap_" + name, "std")


def build(M):
    nc = bass.Bass("TRN2", target_bir_lowering=False, detect_race_conditions=False)
    nc.cache_partition_id()
    cx = Ctx(nc, M)

    xall = nc.dram_tensor("x_all", [T, C], F32, kind="ExternalInput").ap()
    xown = nc.dram_tensor("x_own", [OWN, C], F32, kind="ExternalInput").ap()
    wkT = nc.dram_tensor("wkT", [C, C], F32, kind="ExternalInput").ap()
    wvT = nc.dram_tensor("wvT", [C, C], F32, kind="ExternalInput").ap()
    wrT = nc.dram_tensor("wrT", [C, C], F32, kind="ExternalInput").ap()
    woT = nc.dram_tensor("woT", [C, C], F32, kind="ExternalInput").ap()
    wgT = nc.dram_tensor("wgT", [C, HID], I32, kind="ExternalInput").ap()
    wuT = nc.dram_tensor("wuT", [C, HID], I32, kind="ExternalInput").ap()
    wdT = nc.dram_tensor("wdT", [HID, C], I32, kind="ExternalInput").ap()
    tmk_i = nc.dram_tensor("tmk", [128, 8], F32, kind="ExternalInput").ap()
    tmv_i = nc.dram_tensor("tmv", [128, 8], F32, kind="ExternalInput").ap()
    tmr_i = nc.dram_tensor("tmr", [128, 8], F32, kind="ExternalInput").ap()
    td_i = nc.dram_tensor("td", [128, 8], F32, kind="ExternalInput").ap()
    tf_i = nc.dram_tensor("tf", [128, 8], F32, kind="ExternalInput").ap()
    ident_i = nc.dram_tensor("ident", [128, 128], BF16, kind="ExternalInput").ap()

    out_own = nc.dram_tensor("out_own", [OWN, C], F32, kind="ExternalOutput").ap()

    xnT_hbm = nc.dram_tensor("xnT_hbm", [C, T + 1], BF16, kind="Internal").ap()
    y_hbm = nc.dram_tensor("y_hbm", [C, T], BF16, kind="Internal").ap()
    own_xn = nc.dram_tensor("own_xn", [C, OWN + 1], BF16, kind="Internal").ap()
    own_y = nc.dram_tensor("own_y", [C, OWN], BF16, kind="Internal").ap()
    wr_hbm = nc.dram_tensor("wr_hbm", [C, C], BF16, kind="Internal").ap()
    wo_hbm = nc.dram_tensor("wo_hbm", [C, C], BF16, kind="Internal").ap()

    for semname in SEMS:
        cx.sem[semname] = nc.alloc_semaphore("sem_" + semname)

    identb = nc.alloc_sbuf_tensor("identb", [128, 128], BF16)
    tmk = nc.alloc_sbuf_tensor("tmk_t", [128, 8], F32)
    tmv = nc.alloc_sbuf_tensor("tmv_t", [128, 8], F32)
    tmr = nc.alloc_sbuf_tensor("tmr_t", [128, 8], F32)
    tdt = nc.alloc_sbuf_tensor("td_t", [128, 8], F32)
    tft = nc.alloc_sbuf_tensor("tf_t", [128, 8], F32)
    eu = nc.alloc_sbuf_tensor("eu_t", [128, 8], F32)
    ddec = nc.alloc_sbuf_tensor("ddec_t", [128, 8], F32)
    e1n = nc.alloc_sbuf_tensor("e1n_t", [128, 8], F32)
    c1p5 = nc.alloc_sbuf_tensor("c1p5", [128, 64], F32)


    sA = ExitStack()
    ptr_banks = [sA.enter_context(nc.psum_tensor(f"ptr{i}", [128, 1024], BF16)) for i in range(4)]
    pk_banks = [sA.enter_context(nc.psum_tensor(f"pk{i}", [128, 512], F32)) for i in range(2)]
    pv_banks = [sA.enter_context(nc.psum_tensor(f"pv{i}", [128, 512], F32)) for i in range(2)]
    wkTb = sA.enter_context(nc.sbuf_tensor("wkTb", [128, 8, 1024], BF16))
    wvTb = sA.enter_context(nc.sbuf_tensor("wvTb", [128, 8, 1024], BF16))
    dfull = sA.enter_context(nc.sbuf_tensor("dfull", [128, 8, TCA], F32))
    carryPv = sA.enter_context(nc.sbuf_tensor("carryPv", [128, 8], F32))
    carryQv = sA.enter_context(nc.sbuf_tensor("carryQv", [128, 8], F32))
    wtmpP = sA.enter_context(nc.sbuf_tensor("wtmpP", [128, 256], F32))
    wtmpQ = sA.enter_context(nc.sbuf_tensor("wtmpQ", [128, 256], F32))
    onesrow = sA.enter_context(nc.sbuf_tensor("onesrow", [128, TCA], F32))
    sNs = sA.enter_context(nc.sbuf_tensor("sNs", [128, 64], F32))
    bNs = sA.enter_context(nc.sbuf_tensor("bNs", [128, 64], F32))

    sP = ExitStack()
    wsrcf = [sP.enter_context(nc.sbuf_tensor(f"wsrcf{i}", [128, 4, 1024], F32)) for i in range(2)]
    wcvtb = [sP.enter_context(nc.sbuf_tensor(f"wcvtb{i}", [128, 4, 1024], BF16)) for i in range(2)]


    pieces = []
    for w_src, dst_res, tag in [(wkT, wkTb, "wk"), (wvT, wvTb, "wv")]:
        for ph in range(2):
            src = w_src[ph * 512:(ph + 1) * 512, :].rearrange("(a p) d -> p a d", p=128)
            pieces.append(("f2res", src, (dst_res, ph), 127.0, f"{tag}{ph}"))
    for w_src, w_dst, scale, tag in [(wrT, wr_hbm, 0.5 * 127.0, "wr"), (woT, wo_hbm, 0.5, "wo")]:
        for ph in range(2):
            src = w_src[ph * 512:(ph + 1) * 512, :].rearrange("(a p) d -> p a d", p=128)
            dst = w_dst[ph * 512:(ph + 1) * 512, :].rearrange("(a p) d -> p a d", p=128)
            pieces.append(("f2hbm", src, dst, scale, f"{tag}{ph}"))

    with nc.Block() as blk:
        @blk.sync
        def _(sync):
            for t_in, t_sb in [(tmk_i, tmk), (tmv_i, tmv), (tmr_i, tmr),
                               (td_i, tdt), (tf_i, tft), (ident_i, identb)]:
                cx.bump(sync.dma_start(out=t_sb[:], in_=t_in), "lds", 16)
            cx.mark("ld_small", "lds")
            for i, (kind, src, dst, scale, tag) in enumerate(pieces):
                par = i % 2
                buf = wsrcf[par]
                if i >= 2:
                    cx.wait(sync, f"pcvt{i-2}", "v")
                cx.bump(sync.dma_start(out=buf[:], in_=src), f"ldp{par}", 16)
                cx.mark(f"pld{i}", f"ldp{par}")

        @blk.scalar
        def _(scalar):
            cx.wait(scalar, "ld_small", "lds")
            cx.bump(scalar.activation(eu[:], tft[:], AF.Exp), "s", 1)
            cx.mark("eu", "s")
            cx.bump(scalar.activation(ddec[:], tdt[:], AF.Exp), "s", 1)
            cx.mark("etd", "s")
            cx.wait(scalar, "e1n", "v")
            cx.bump(scalar.activation(ddec[:], e1n[:], AF.Exp), "s", 1)
            cx.mark("ddec", "s")

        @blk.vector
        def _(vector):
            vector.memset(c1p5[:], 1.5)
            cx.wait(vector, "etd", "s")
            cx.bump(vector.tensor_scalar(e1n[:], ddec[:], -1.0, None, ALU.mult), "v", 1)
            cx.mark("e1n", "v")
            cx.wait(vector, "ddec", "s")
            vector.memset(onesrow[:], 1.0)
            for ccN in range(8):
                inst = vector.tensor_scalar(dfull[:, ccN, :], onesrow[:],
                                            ddec[:, ccN:ccN + 1], None, ALU.mult)
            cx.bump(inst, "v", 1)
            cx.mark("dfull", "v")
            last_cvt_user = {}
            for i, (kind, src, dst, scale, tag) in enumerate(pieces):
                par = i % 2
                cx.wait(vector, f"pld{i}", f"ldp{par}")
                if kind == "f2res":
                    dst_res, ph = dst
                    inst = vector.tensor_scalar(dst_res[:, ph * 4:(ph + 1) * 4, :], wsrcf[par][:],
                                                scale, None, ALU.mult)
                else:
                    cbuf = ("wcvtb", par)
                    prev = last_cvt_user.get(cbuf)
                    if prev is not None:
                        cx.wait(vector, f"pst{prev}", f"stp{prev % 2}")
                    last_cvt_user[cbuf] = i
                    inst = vector.tensor_scalar(wcvtb[par][:], wsrcf[par][:], scale, None, ALU.mult)
                cx.bump(inst, "v", 1)
                cx.mark(f"pcvt{i}", "v")
            cx.mark("prep_v_done", "v")

        @blk.gpsimd
        def _(g):
            for i, (kind, src, dst, scale, tag) in enumerate(pieces):
                if kind == "f2res":
                    continue
                par = i % 2
                cx.wait(g, f"pcvt{i}", "v")
                cx.bump(g.dma_start(out=dst, in_=wcvtb[par][:]), f"stp{par}", 16)
                cx.mark(f"pst{i}", f"stp{par}")
            cx.mark("prep_st0_done", "stp0")
            cx.mark("prep_st1_done", "stp1")

    sP.close()

    xf2 = [sA.enter_context(nc.sbuf_tensor(f"xf{i}", [128, 4, 1024], F32)) for i in range(2)]
    sqscr = sA.enter_context(nc.sbuf_tensor("sqscr", [128, 1024], BF16))
    ssq = sA.enter_context(nc.sbuf_tensor("ssq", [128, 64], F32))
    sm = sA.enter_context(nc.sbuf_tensor("sm", [128, 64], F32))
    meanT = sA.enter_context(nc.sbuf_tensor("meanT", [128, 64], F32))
    varT = sA.enter_context(nc.sbuf_tensor("varT", [128, 64], F32))
    sNb = [sA.enter_context(nc.sbuf_tensor(f"sN{i}", [128, 64], F32)) for i in range(2)]
    bN = sA.enter_context(nc.sbuf_tensor("bN", [128, 64], F32))
    nt1 = sA.enter_context(nc.sbuf_tensor("nt1", [128, 64], F32))
    nt2 = sA.enter_context(nc.sbuf_tensor("nt2", [128, 64], F32))
    vtmp = sA.enter_context(nc.sbuf_tensor("vtmp", [128, 64], F32))
    xn_tok = sA.enter_context(nc.sbuf_tensor("xn_tok", [128, 4, 1024], BF16))
    xnT = sA.enter_context(nc.sbuf_tensor("xnT", [128, 8, TCA + 1], BF16))
    dxx = sA.enter_context(nc.sbuf_tensor("dxx", [128, TCA], BF16))
    xkT = sA.enter_context(nc.sbuf_tensor("xkT", [128, 8, TCA], BF16))
    xvT = sA.enter_context(nc.sbuf_tensor("xvT", [128, 8, TCA], BF16))
    ekf = sA.enter_context(nc.sbuf_tensor("ekf", [128, 4, TCA + 2], F32))
    vvf = sA.enter_context(nc.sbuf_tensor("vvf", [128, 4, TCA], F32))
    ekvf = sA.enter_context(nc.sbuf_tensor("ekvf", [128, 4, TCA + 2], F32))
    Pt = sA.enter_context(nc.sbuf_tensor("Pt", [128, 4, TCA], F32))
    Qt = sA.enter_context(nc.sbuf_tensor("Qt", [128, 4, TCA], F32))
    numt = sA.enter_context(nc.sbuf_tensor("numt", [128, 4, TCA], F32))
    dent = sA.enter_context(nc.sbuf_tensor("dent", [128, 4, TCA], F32))
    yb = sA.enter_context(nc.sbuf_tensor("yb", [128, 8, TCA], BF16))

    with nc.Block() as blk:
        @blk.sync
        def _(sync):
            for j in range(NCA):
                if j == 0:
                    cx.wait(sync, "prep_v_done", "v")
                    cx.wait(sync, "prep_st0_done", "stp0")
                    cx.wait(sync, "prep_st1_done", "stp1")
                elif j >= 2:
                    cx.wait(sync, f"lnap{j-2}", "v")
                    cx.wait(sync, f"sq{j-2}", "s")
                    if DEBUG_TAPS and j == 2:
                        cx.wait(sync, "tapsA_done", "std")
                src = xall[j * TCA:(j + 1) * TCA, :].rearrange("(a p) c -> p a c", p=128)
                cx.bump(sync.dma_start(out=xf2[j % 2][:], in_=src), f"ldx{j % 2}", 16)
                cx.mark(f"ldx{j}", f"ldx{j % 2}")

        @blk.scalar
        def _(scalar):
            def emit_sq_relay(j):
                cx.wait(scalar, f"ldx{j}", f"ldx{j % 2}")
                cx.wait(scalar, f"stats{j-1}" if j > 0 else "statsinit", "v")
                for a in range(4):
                    scalar.activation(sqscr[:], xf2[j % 2][:, a, :], AF.Square,
                                      accum_out=ssq[:, a:a + 1])
                    inst = scalar.activation(sqscr[:], xf2[j % 2][:, a, :], AF.Copy,
                                             accum_out=sm[:, a:a + 1])
                cx.bump(inst, "s", 1)
                cx.mark(f"sq{j}", "s")

            def emit_relay(j):
                cx.wait(scalar, f"stats{j}", "v")
                scalar.copy(sNs[:], sNb[(NEWTON_ITERS + 1) % 2][:])
                inst = scalar.copy(bNs[:], bN[:])
                cx.bump(inst, "s", 1)
                cx.mark(f"lnrelay{j}", "s")

            emit_sq_relay(0)
            emit_relay(0)
            emit_sq_relay(1)
            for j in range(NCA):
                for ccN in range(8):
                    cx.wait(scalar, f"tr{j}_{ccN}", "t")
                    cx.wait(scalar, f"ghost{j}", "v")
                    if j > 0 and ccN == 0:
                        cx.wait(scalar, f"xnwr{j-1}", "stx")
                    inst = scalar.copy(xnT[:, ccN, 1:TCA + 1], ptr_banks[ccN % 4][:, 0:TCA])
                    cx.bump(inst, "s", 1)
                    cx.mark(f"trevac{j}_{ccN}", "s")
                for half in range(2):
                    for db in range(half * 4, half * 4 + 4):
                        cx.wait(scalar, f"kv{j}_{db}", "t")
                        if db % 4 == 0:
                            if db == 0:
                                if j > 0:
                                    cx.wait(scalar, f"scan{j-1}_1", "v")
                            else:
                                cx.wait(scalar, f"scan{j}_0", "v")
                        jj = db % 4
                        scalar.activation(ekf[:, jj, 1:257], pk_banks[db % 2][:, 0:256], AF.Exp)
                        cx.bump(scalar.activation(ekf[:, jj, 258:TCA + 2], pk_banks[db % 2][:, 256:TCA], AF.Exp), "s", 1)
                        cx.bump(scalar.copy(vvf[:, jj, :], pv_banks[db % 2][:]), "s", 1)
                        cx.mark(f"ekevac{j}_{db}", "s")
                    if half == 0 and j + 1 < NCA:
                        emit_relay(j + 1)
                    cx.wait(scalar, f"den{j}_{half}", "v")
                    for db in range(half * 4, half * 4 + 4):
                        jj = db % 4
                        inst = scalar.activation(vvf[:, jj, :], dent[:, jj, :], AF.Ln)
                    cx.bump(inst, "s", 1)
                    cx.mark(f"lnd{j}_{half}", "s")
                    cx.wait(scalar, f"negd{j}_{half}", "v")
                    for db in range(half * 4, half * 4 + 4):
                        jj = db % 4
                        inst = scalar.activation(vvf[:, jj, :], dent[:, jj, :], AF.Exp)
                    cx.bump(inst, "s", 1)
                    cx.mark(f"expd{j}_{half}", "s")
                if j + 2 < NCA:
                    emit_sq_relay(j + 2)

        @blk.vector
        def _(vector):
            def emit_stats_lnap(j):
                cx.wait(vector, f"sq{j}", "s")
                vector.tensor_scalar(meanT[:], sm[:], 1.0 / C, None, ALU.mult)
                vector.tensor_scalar(vtmp[:], ssq[:], 1.0 / C, None, ALU.mult)
                vector.tensor_mul(nt1[:], meanT[:], meanT[:])
                vector.tensor_sub(nt2[:], vtmp[:], nt1[:])
                vector.tensor_scalar(varT[:], nt2[:], EPS, None, ALU.add)
                vector.memset(sNb[0][:], 1.0)
                cur = 0
                for _ in range(NEWTON_ITERS):
                    vector.tensor_mul(nt1[:], sNb[cur][:], sNb[cur][:])
                    vector.tensor_mul(nt2[:], nt1[:], varT[:])
                    vector.scalar_tensor_tensor(nt1[:], nt2[:], -0.5, c1p5[:], ALU.mult, ALU.add)
                    vector.tensor_mul(sNb[1 - cur][:], sNb[cur][:], nt1[:])
                    cur = 1 - cur
                vector.tensor_scalar(sNb[1 - cur][:], sNb[cur][:], 1.0 / 127.0, None, ALU.mult)
                sN = sNb[1 - cur]
                inst = vector.tensor_mul(bN[:], meanT[:], sN[:])
                cx.bump(inst, "v", 1)
                cx.mark(f"stats{j}", "v")
                cx.wait(vector, f"lnrelay{j}", "s")
                if j > 0:
                    cx.wait(vector, f"tr{j-1}_7", "t")
                for a in range(4):
                    inst = vector.tensor_scalar(xn_tok[:, a, :], xf2[j % 2][:, a, :],
                                                sNs[:, a:a + 1], bNs[:, a:a + 1],
                                                ALU.mult, ALU.subtract)
                cx.bump(inst, "v", 1)
                cx.mark(f"lnap{j}", "v")

            for j in range(NCA):
                if j == 0:
                    cx.wait(vector, "prep_st0_done", "stp0")
                    cx.wait(vector, "prep_st1_done", "stp1")
                    vector.memset(sm[:], 1.0)
                    vector.memset(carryPv[:], 0.0)
                    vector.memset(carryQv[:], 0.0)
                    inst = vector.memset(ssq[:], 1.0)
                    cx.bump(inst, "v", 1)
                    cx.mark("statsinit", "v")
                    emit_stats_lnap(0)
                    for ccN in range(8):
                        inst = vector.memset(xnT[:, ccN, 0:1], 0.0)
                    cx.bump(inst, "v", 1)
                    cx.mark("ghost0", "v")
                cx.wait(vector, f"trevac{j}_7", "s")
                if j > 0:
                    cx.wait(vector, f"kv{j-1}_7", "t")
                for ccN in range(8):
                    vector.tensor_sub(dxx[:], xnT[:, ccN, 1:TCA + 1], xnT[:, ccN, 0:TCA])
                    vector.scalar_tensor_tensor(xkT[:, ccN, :], dxx[:], tmk[:, ccN:ccN + 1],
                                                xnT[:, ccN, 0:TCA], ALU.mult, ALU.add)
                    inst = vector.scalar_tensor_tensor(xvT[:, ccN, :], dxx[:], tmv[:, ccN:ccN + 1],
                                                       xnT[:, ccN, 0:TCA], ALU.mult, ALU.add)
                cx.bump(inst, "v", 1)
                cx.mark(f"mix{j}", "v")
                if j + 1 < NCA:
                    emit_stats_lnap(j + 1)
                    for ccN in range(8):
                        inst = vector.tensor_copy(xnT[:, ccN, 0:1], xnT[:, ccN, TCA:TCA + 1])
                    cx.bump(inst, "v", 1)
                    cx.mark(f"ghost{j+1}", "v")
                for half in range(2):
                    h4 = half * 4
                    if DEBUG_TAPS and j == 0 and half == 1:
                        cx.wait(vector, "tapsA_half0", "std")
                    for db in range(h4, h4 + 4):
                        cx.wait(vector, f"ekevac{j}_{db}", "s")
                    if half == 0 and j > 0:
                        cx.wait(vector, f"ywr{j-1}", "sty")
                    vector.tensor_copy(ekvf[:, 0:4, 0:1], carryPv[:, h4:h4 + 4])
                    vector.tensor_copy(ekf[:, 0:4, 0:1], carryQv[:, h4:h4 + 4])
                    for db in range(h4, h4 + 4):
                        jj = db % 4
                        vector.tensor_mul(ekvf[:, jj, 1:257], ekf[:, jj, 1:257], vvf[:, jj, 0:256])
                        vector.tensor_mul(ekvf[:, jj, 258:TCA + 2], ekf[:, jj, 258:TCA + 2], vvf[:, jj, 256:TCA])
                    for db in range(h4, h4 + 4):
                        jj = db % 4
                        vector.tensor_tensor_scan(Pt[:, jj, 0:256], dfull[:, db, 0:256],
                                                  ekvf[:, jj, 0:256], 0.0, ALU.mult, ALU.add)
                        vector.tensor_tensor_scan(Qt[:, jj, 0:256], dfull[:, db, 0:256],
                                                  ekf[:, jj, 0:256], 0.0, ALU.mult, ALU.add)
                    for db in range(h4, h4 + 4):
                        jj = db % 4
                        vector.tensor_mul(wtmpP[:, jj * 64:(jj + 1) * 64], Pt[:, jj, 192:256], dfull[:, db, 0:64])
                        vector.tensor_mul(wtmpQ[:, jj * 64:(jj + 1) * 64], Qt[:, jj, 192:256], dfull[:, db, 0:64])
                    for db in range(h4, h4 + 4):
                        jj = db % 4
                        vector.tensor_add(ekvf[:, jj, 257:258], wtmpP[:, jj * 64 + 63:jj * 64 + 64], ekvf[:, jj, 256:257])
                        vector.tensor_add(ekf[:, jj, 257:258], wtmpQ[:, jj * 64 + 63:jj * 64 + 64], ekf[:, jj, 256:257])
                    for db in range(h4, h4 + 4):
                        jj = db % 4
                        vector.tensor_tensor_scan(Pt[:, jj, 256:TCA], dfull[:, db, 0:256],
                                                  ekvf[:, jj, 257:TCA + 1], 0.0, ALU.mult, ALU.add)
                        vector.tensor_tensor_scan(Qt[:, jj, 256:TCA], dfull[:, db, 0:256],
                                                  ekf[:, jj, 257:TCA + 1], 0.0, ALU.mult, ALU.add)
                    for db in range(h4, h4 + 4):
                        jj = db % 4
                        vector.scalar_tensor_tensor(numt[:, jj, 0:256], ekvf[:, jj, 1:257],
                                                    eu[:, db:db + 1], Pt[:, jj, 0:256],
                                                    ALU.mult, ALU.add)
                        vector.scalar_tensor_tensor(numt[:, jj, 256:TCA], ekvf[:, jj, 258:TCA + 2],
                                                    eu[:, db:db + 1], Pt[:, jj, 256:TCA],
                                                    ALU.mult, ALU.add)
                        vector.scalar_tensor_tensor(dent[:, jj, 0:256], ekf[:, jj, 1:257],
                                                    eu[:, db:db + 1], Qt[:, jj, 0:256],
                                                    ALU.mult, ALU.add)
                        cx.last_den_inst = vector.scalar_tensor_tensor(
                            dent[:, jj, 256:TCA], ekf[:, jj, 258:TCA + 2],
                            eu[:, db:db + 1], Qt[:, jj, 256:TCA],
                            ALU.mult, ALU.add)
                        inst = None
                    cx.bump(cx.last_den_inst, "v", 1)
                    cx.mark(f"den{j}_{half}", "v")
                    for db in range(h4, h4 + 4):
                        jj = db % 4
                        vector.tensor_mul(wtmpP[:, jj * 64:(jj + 1) * 64], Pt[:, jj, 448:TCA], dfull[:, db, 0:64])
                        vector.tensor_mul(wtmpQ[:, jj * 64:(jj + 1) * 64], Qt[:, jj, 448:TCA], dfull[:, db, 0:64])
                    for db in range(h4, h4 + 4):
                        jj = db % 4
                        vector.tensor_add(carryPv[:, db:db + 1], wtmpP[:, jj * 64 + 63:jj * 64 + 64], ekvf[:, jj, TCA + 1:TCA + 2])
                        vector.tensor_add(carryQv[:, db:db + 1], wtmpQ[:, jj * 64 + 63:jj * 64 + 64], ekf[:, jj, TCA + 1:TCA + 2])
                    cx.wait(vector, f"lnd{j}_{half}", "s")
                    for db in range(h4, h4 + 4):
                        jj = db % 4
                        inst = vector.tensor_scalar(dent[:, jj, :], vvf[:, jj, :], -1.0, None, ALU.mult)
                    cx.bump(inst, "v", 1)
                    cx.mark(f"negd{j}_{half}", "v")
                    cx.wait(vector, f"expd{j}_{half}", "s")
                    for db in range(h4, h4 + 4):
                        jj = db % 4
                        inst = vector.tensor_mul(yb[:, db, :], numt[:, jj, :], vvf[:, jj, :])
                    cx.bump(inst, "v", 1)
                    cx.mark(f"scan{j}_{half}", "v")

        @blk.tensor
        def _(tensor):
            def emit_tr(j, ccs):
                for ccN in ccs:
                    if j > 0 or ccN >= 4:
                        jw, cw = (j, ccN - 4) if ccN >= 4 else (j - 1, ccN + 4)
                        cx.wait(tensor, f"trevac{jw}_{cw}", "s")
                    for a in range(4):
                        inst = tensor.transpose(ptr_banks[ccN % 4][:, a * 128:(a + 1) * 128],
                                                xn_tok[:, a, ccN * 128:(ccN + 1) * 128], identb[:])
                    cx.bump(inst, "t", 1)
                    cx.mark(f"tr{j}_{ccN}", "t")

            def emit_kv(j, dbs):
                for db in dbs:
                    if j > 0 or db >= 2:
                        jw, dw = (j, db - 2) if db >= 2 else (j - 1, db + 6)
                        cx.wait(tensor, f"ekevac{jw}_{dw}", "s")
                    for ccN in range(8):
                        tensor.matmul(pk_banks[db % 2][:], wkTb[:, ccN, db * 128:(db + 1) * 128],
                                      xkT[:, ccN, :], start=(ccN == 0), stop=(ccN == 7))
                    for ccN in range(8):
                        inst = tensor.matmul(pv_banks[db % 2][:], wvTb[:, ccN, db * 128:(db + 1) * 128],
                                             xvT[:, ccN, :], start=(ccN == 0), stop=(ccN == 7))
                    cx.bump(inst, "t", 1)
                    cx.mark(f"kv{j}_{db}", "t")

            cx.wait(tensor, "lnap0", "v")
            emit_tr(0, range(8))
            for j in range(NCA):
                cx.wait(tensor, f"mix{j}", "v")
                emit_kv(j, range(0, 4))
                if j + 1 < NCA:
                    cx.wait(tensor, f"lnap{j+1}", "v")
                    emit_tr(j + 1, range(0, 4))
                emit_kv(j, range(4, 8))
                if j + 1 < NCA:
                    emit_tr(j + 1, range(4, 8))

        @blk.gpsimd
        def _(g):
            for j in range(NCA):
                if DEBUG_TAPS and j == 0:
                    cx.wait(g, "scan0_0", "v")
                    cx.tap(g, "xf", xf2[0][:], [128, 4, 1024], F32)
                    cx.tap(g, "sm", sm[:], [128, 64], F32)
                    cx.tap(g, "ssq", ssq[:], [128, 64], F32)
                    cx.tap(g, "varT", varT[:], [128, 64], F32)
                    cx.tap(g, "meanT", meanT[:], [128, 64], F32)
                    cx.tap(g, "vtmp", vtmp[:], [128, 64], F32)
                    cx.tap(g, "nt1", nt1[:], [128, 64], F32)
                    cx.tap(g, "nt2", nt2[:], [128, 64], F32)
                    cx.tap(g, "xn_tok", xn_tok[:], [128, 4, 1024], BF16)
                    cx.tap(g, "xkT", xkT[:], [128, 8, TCA], BF16)
                    cx.tap(g, "xvT", xvT[:], [128, 8, TCA], BF16)
                    cx.tap(g, "sN", sNb[(NEWTON_ITERS + 1) % 2][:], [128, 64], F32)
                    cx.tap(g, "bN", bN[:], [128, 64], F32)
                    cx.tap(g, "ekf", ekf[:], [128, 4, TCA + 2], F32)
                    cx.tap(g, "ekvf", ekvf[:], [128, 4, TCA + 2], F32)
                    cx.tap(g, "Pt", Pt[:], [128, 4, TCA], F32)
                    cx.tap(g, "Qt", Qt[:], [128, 4, TCA], F32)
                    cx.tap(g, "numt", numt[:], [128, 4, TCA], F32)
                    cx.tap(g, "dent", dent[:], [128, 4, TCA], F32)
                    cx.mark("tapsA_half0", "std")
                cx.wait(g, f"scan{j}_1", "v")
                if DEBUG_TAPS and j == 0:
                    cx.tap(g, "yb", yb[:], [128, 8, TCA], BF16)
                dst = y_hbm[:, j * TCA:(j + 1) * TCA].rearrange("(cc p) t -> p cc t", p=128)
                cx.bump(g.dma_start(out=dst, in_=yb[:]), "sty", 16)
                cx.mark(f"ywr{j}", "sty")
                cx.wait(g, f"trevac{j}_7", "s")
                if j == 0:
                    cx.wait(g, "ghost0", "v")
                    dst2 = xnT_hbm[:, 0:TCA + 1].rearrange("(cc p) t -> p cc t", p=128)
                    cx.bump(g.dma_start(out=dst2, in_=xnT[:]), "stx", 16)
                else:
                    dst2 = xnT_hbm[:, 1 + j * TCA:1 + (j + 1) * TCA].rearrange("(cc p) t -> p cc t", p=128)
                    cx.bump(g.dma_start(out=dst2, in_=xnT[:, :, 1:TCA + 1]), "stx", 16)
                cx.mark(f"xnwr{j}", "stx")
                if DEBUG_TAPS and j == 0:
                    cx.tap(g, "xnT", xnT[:], [128, 8, TCA + 1], BF16)
                    cx.mark("tapsA_done", "std")

    cx.newM["A_v_done"] = cx.cnt["v"]
    cx.newM["A_s_done"] = cx.cnt["s"]
    cx.newM["A_t_done"] = cx.cnt["t"]
    cx.newM["A_sty_done"] = cx.cnt["sty"]
    cx.newM["A_stx_done"] = cx.cnt["stx"]
    sA.close()

    sB = ExitStack()
    pr_banks = [sB.enter_context(nc.psum_tensor(f"pr{i}", [128, 512], F32)) for i in range(2)]
    pg_banks = [sB.enter_context(nc.psum_tensor(f"pg{i}", [128, 512], F32)) for i in range(2)]
    pu_banks = [sB.enter_context(nc.psum_tensor(f"pu{i}", [128, 512], F32)) for i in range(2)]
    po_banks = [sB.enter_context(nc.psum_tensor(f"po{i}", [128, 512], F32)) for i in range(2)]
    wgTb = sB.enter_context(nc.sbuf_tensor("wgTb", [128, 8, HID], BF16))
    wuTb = sB.enter_context(nc.sbuf_tensor("wuTb", [128, 8, HID], BF16))
    wdb = sB.enter_context(nc.sbuf_tensor("wdb", [128, 24, 1024], BF16))
    wrob = sB.enter_context(nc.sbuf_tensor("wrob", [128, 8, 1024], BF16))
    xnTob = [sB.enter_context(nc.sbuf_tensor(f"xnTo{i}", [128, 8, TCB + 1], BF16)) for i in range(2)]
    dxxB = sB.enter_context(nc.sbuf_tensor("dxxB", [128, TCB], BF16))
    xrT = sB.enter_context(nc.sbuf_tensor("xrT", [128, 8, TCB], BF16))
    srT = sB.enter_context(nc.sbuf_tensor("srT", [128, 8, TCB], BF16))
    yo = sB.enter_context(nc.sbuf_tensor("yo", [128, 8, TCB], BF16))
    sgb = sB.enter_context(nc.sbuf_tensor("sgb", [128, TCB], BF16))
    ubb = sB.enter_context(nc.sbuf_tensor("ubb", [128, TCB], BF16))
    hact = sB.enter_context(nc.sbuf_tensor("hact", [128, 24, TCB], BF16))
    xo = sB.enter_context(nc.sbuf_tensor("xo", [128, 1024], F32))
    outf = sB.enter_context(nc.sbuf_tensor("outf", [128, 1024], F32))
    outt = sB.enter_context(nc.sbuf_tensor("outt", [128, 1024], F32))

    with nc.Block() as blk:
        @blk.gpsimd
        def _(g):
            cx.wait(g, "A_v_done", "v")
            cx.wait(g, "A_s_done", "s")
            cx.wait(g, "A_t_done", "t")
            cx.bump(g.dma_start(out=wgTb[:], in_=wgT.rearrange("(cc p) h -> p cc h", p=128)), "ldffnw", 16)
            cx.bump(g.dma_start(out=wuTb[:], in_=wuT.rearrange("(cc p) h -> p cc h", p=128)), "ldffnw", 16)
            cx.bump(g.dma_start(out=wdb[:], in_=wdT.rearrange("(hb p) d -> p hb d", p=128)), "ldffnw", 16)
            cx.mark("ldffnw", "ldffnw")

        @blk.sync
        def _(sync):
            cx.wait(sync, "A_v_done", "v")
            cx.wait(sync, "A_s_done", "s")
            cx.wait(sync, "A_t_done", "t")
            cx.wait(sync, "A_sty_done", "sty")
            cx.wait(sync, "A_stx_done", "stx")

            HW_ = OWN // 2
            if cx.pass1:
                cx.cnt["ldos"] += 128
                cx.cnt["ldos2"] = cx.cnt.get("ldos2", 0)
            else:
                pid = sync.partition_id()
                hval = sync.compute_val(pid % 2)
                c0 = sync.compute_val(hval == 0)
                c1 = sync.compute_val(hval == 1)
                cx.bump(sync.dma_start(out=own_xn[:, 0:HW_ + 1], in_=xnT_hbm[:, 0:HW_ + 1], cond=c0), "ldos", 16)
                cx.bump(sync.dma_start(out=own_xn[:, 0:HW_ + 1], in_=xnT_hbm[:, OWN:OWN + HW_ + 1], cond=c1), "ldos", 16)
                cx.bump(sync.dma_start(out=own_y[:, 0:HW_], in_=y_hbm[:, 0:HW_], cond=c0), "ldos", 16)
                cx.bump(sync.dma_start(out=own_y[:, 0:HW_], in_=y_hbm[:, OWN:OWN + HW_], cond=c1), "ldos", 16)
                cx.bump(sync.dma_start(out=own_xn[:, HW_ + 1:OWN + 1], in_=xnT_hbm[:, HW_ + 1:OWN + 1], cond=c0), "ldos", 16)
                cx.bump(sync.dma_start(out=own_xn[:, HW_ + 1:OWN + 1], in_=xnT_hbm[:, OWN + HW_ + 1:T + 1], cond=c1), "ldos", 16)
                cx.bump(sync.dma_start(out=own_y[:, HW_:OWN], in_=y_hbm[:, HW_:OWN], cond=c0), "ldos", 16)
                cx.bump(sync.dma_start(out=own_y[:, HW_:OWN], in_=y_hbm[:, OWN + HW_:T], cond=c1), "ldos", 16)
            cx.mark("ownstage", "ldos")

    with nc.Block() as blk:
        @blk.sync
        def _(sync):
            for s_ in range(NCB):
                if s_ == 0:
                    if not cx.pass1:
                        sync.wait_ge(cx.sem["ldos"], 64)
                elif s_ == NCB // 2:
                    cx.wait(sync, "ownstage", "ldos")
                elif s_ >= 2:
                    cx.wait(sync, f"bmix{s_-2}", "v")
                    cx.wait(sync, f"gu{s_-2}_{NHB-1}", "t")
                if DEBUG_TAPS and s_ == 1:
                    cx.wait(sync, "tapsB_done", "std")
                t0 = s_ * TCB
                src = own_xn[:, t0:t0 + TCB + 1].rearrange("(cc p) t -> p cc t", p=128)
                cx.bump(sync.dma_start(out=xnTob[s_ % 2][:], in_=src), f"ldxn{s_ % 2}", 16)
                cx.mark(f"ldxn{s_}", f"ldxn{s_ % 2}")
                if s_ > 0:
                    cx.wait(sync, f"smul{s_-1}", "v")
                srcy = own_y[:, t0:t0 + TCB].rearrange("(cc p) t -> p cc t", p=128)
                cx.bump(sync.dma_start(out=yo[:], in_=srcy), "ldy", 16)
                cx.mark(f"ldy{s_}", "ldy")
                if s_ > 0:
                    cx.wait(sync, f"oadd{s_-1}_1", "v")
                cx.bump(sync.dma_start(out=xo[:], in_=xown[t0:t0 + 128, :]), "ldxo", 16)
                cx.mark(f"ldxo{s_}_0", "ldxo")
                if s_ > 0:
                    cx.wait(sync, f"out{s_-1}_1_1", "t")
                cx.bump(sync.dma_start(out=wrob[:], in_=wr_hbm.rearrange("(cc p) d -> p cc d", p=128)), "ldwro", 16)
                cx.mark(f"ldwr{s_}", "ldwro")
                cx.wait(sync, f"rmm{s_}_7", "t")
                if DEBUG_TAPS and s_ == 0:
                    cx.wait(sync, "tap_wrob", "std")
                cx.bump(sync.dma_start(out=wrob[:], in_=wo_hbm.rearrange("(cc p) d -> p cc d", p=128)), "ldwro", 16)
                cx.mark(f"ldwo{s_}", "ldwro")
                cx.wait(sync, f"oadd{s_}_0", "v")
                cx.bump(sync.dma_start(out=xo[:], in_=xown[t0 + 128:t0 + 256, :]), "ldxo", 16)
                cx.mark(f"ldxo{s_}_1", "ldxo")

        @blk.vector
        def _(vector):
            for s_ in range(NCB):
                cx.wait(vector, f"ldxn{s_}", f"ldxn{s_ % 2}")
                if s_ > 0:
                    cx.wait(vector, f"rmm{s_-1}_7", "t")
                xnTo = xnTob[s_ % 2]
                for ccN in range(8):
                    vector.tensor_sub(dxxB[:], xnTo[:, ccN, 1:TCB + 1], xnTo[:, ccN, 0:TCB])
                    inst = vector.scalar_tensor_tensor(xrT[:, ccN, :], dxxB[:], tmr[:, ccN:ccN + 1],
                                                       xnTo[:, ccN, 0:TCB], ALU.mult, ALU.add)
                cx.bump(inst, "v", 1)
                cx.mark(f"bmix{s_}", "v")
                cx.wait(vector, f"revac{s_}_7", "s")
                cx.wait(vector, f"ldy{s_}", "ldy")
                if DEBUG_TAPS and s_ == 0:
                    cx.wait(vector, "tapB_sr", "std")
                vector.tensor_scalar(xrT[:], srT[:], 1.0, None, ALU.add)
                inst = vector.tensor_mul(srT[:], yo[:], xrT[:])
                cx.bump(inst, "v", 1)
                cx.mark(f"smul{s_}", "v")
                for hb in range(NHB):
                    cx.wait(vector, f"guevac{s_}_{hb}", "s")
                    if hb == 0 and s_ > 0:
                        cx.wait(vector, f"out{s_-1}_1_1", "t")
                    inst = vector.scalar_tensor_tensor(hact[:, hb, :], sgb[:], 1.0 / 127.0,
                                                       ubb[:], ALU.mult, ALU.mult)
                    cx.bump(inst, "v", 1)
                    cx.mark(f"hm{s_}_{hb}", "v")
                for tb in range(2):
                    cx.wait(vector, f"oevac{s_}_{tb}_1", "s")
                    cx.wait(vector, f"ldxo{s_}_{tb}", "ldxo")
                    prev = (s_, tb - 1) if tb > 0 else (s_ - 1, 1)
                    if prev[0] >= 0:
                        cx.wait(vector, f"owr{prev[0]}_{prev[1]}", f"sto{prev[1]}")
                    inst = vector.tensor_add(outt[:], outf[:], xo[:])
                    cx.bump(inst, "v", 1)
                    cx.mark(f"oadd{s_}_{tb}", "v")

        @blk.tensor
        def _(tensor):
            for s_ in range(NCB):
                cx.wait(tensor, f"bmix{s_}", "v")
                cx.wait(tensor, f"ldwr{s_}", "ldwro")
                for db in range(8):
                    kprev = (f"revac{s_}_{db-2}" if db >= 2 else
                             (f"revac{s_-1}_{6+db}" if s_ > 0 else None))
                    if kprev is not None:
                        cx.wait(tensor, kprev, "s")
                    for ccN in range(8):
                        inst = tensor.matmul(pr_banks[db % 2][:, 0:TCB], wrob[:, ccN, db * 128:(db + 1) * 128],
                                             xrT[:, ccN, :], start=(ccN == 0), stop=(ccN == 7))
                    cx.bump(inst, "t", 1)
                    cx.mark(f"rmm{s_}_{db}", "t")
                xnTo = xnTob[s_ % 2]
                cx.wait(tensor, "ldffnw", "ldffnw")
                for hb in range(NHB):
                    kprev = (f"guevac{s_}_{hb-2}" if hb >= 2 else
                             (f"guevac{s_-1}_{NHB-2+hb}" if s_ > 0 else None))
                    if kprev is not None:
                        cx.wait(tensor, kprev, "s")
                    for ccN in range(8):
                        tensor.matmul(pg_banks[hb % 2][:, 0:TCB], wgTb[:, ccN, hb * 128:(hb + 1) * 128],
                                      xnTo[:, ccN, 1:TCB + 1], start=(ccN == 0), stop=(ccN == 7))
                    for ccN in range(8):
                        inst = tensor.matmul(pu_banks[hb % 2][:, 0:TCB], wuTb[:, ccN, hb * 128:(hb + 1) * 128],
                                             xnTo[:, ccN, 1:TCB + 1], start=(ccN == 0), stop=(ccN == 7))
                    cx.bump(inst, "t", 1)
                    cx.mark(f"gu{s_}_{hb}", "t")
                cx.wait(tensor, f"smul{s_}", "v")
                cx.wait(tensor, f"hm{s_}_{NHB-1}", "v")
                cx.wait(tensor, f"ldwo{s_}", "ldwro")
                pob = po_banks
                for tb in range(2):
                    for dh in range(2):
                        idx = (tb * 2 + dh) % 2
                        if s_ > 0 or tb > 0:
                            kw = (f"oevac{s_}_{tb-1}_{dh}" if tb > 0
                                  else f"oevac{s_-1}_1_{dh}")
                            cx.wait(tensor, kw, "s")
                        for ccN in range(8):
                            tensor.matmul(pob[idx][:], srT[:, ccN, tb * 128:(tb + 1) * 128],
                                          wrob[:, ccN, dh * 512:(dh + 1) * 512],
                                          start=(ccN == 0), stop=False)
                        for hb in range(NHB):
                            inst = tensor.matmul(pob[idx][:], hact[:, hb, tb * 128:(tb + 1) * 128],
                                                 wdb[:, hb, dh * 512:(dh + 1) * 512],
                                                 start=False, stop=(hb == NHB - 1))
                        cx.bump(inst, "t", 1)
                        cx.mark(f"out{s_}_{tb}_{dh}", "t")

        @blk.scalar
        def _(scalar):
            for s_ in range(NCB):
                for db in range(8):
                    cx.wait(scalar, f"rmm{s_}_{db}", "t")
                    if db == 0 and s_ > 0:
                        cx.wait(scalar, f"out{s_-1}_1_1", "t")
                    inst = scalar.activation(srT[:, db, :], pr_banks[db % 2][:, 0:TCB], AF.Tanh)
                    cx.bump(inst, "s", 1)
                    cx.mark(f"revac{s_}_{db}", "s")
                for hb in range(NHB):
                    cx.wait(scalar, f"gu{s_}_{hb}", "t")
                    if hb > 0:
                        cx.wait(scalar, f"hm{s_}_{hb-1}", "v")
                    elif s_ > 0:
                        cx.wait(scalar, f"hm{s_-1}_{NHB-1}", "v")
                    scalar.activation(sgb[:], pg_banks[hb % 2][:, 0:TCB], AF.Silu)
                    inst = scalar.copy(ubb[:], pu_banks[hb % 2][:, 0:TCB])
                    cx.bump(inst, "s", 1)
                    cx.mark(f"guevac{s_}_{hb}", "s")
                for tb in range(2):
                    for dh in range(2):
                        idx = (tb * 2 + dh) % 2
                        cx.wait(scalar, f"out{s_}_{tb}_{dh}", "t")
                        if s_ > 0 or tb > 0:
                            kw = f"oadd{s_}_{tb-1}" if tb > 0 else f"oadd{s_-1}_1"
                            cx.wait(scalar, kw, "v")
                        inst = scalar.copy(outf[:, dh * 512:(dh + 1) * 512], po_banks[idx][:])
                        cx.bump(inst, "s", 1)
                        cx.mark(f"oevac{s_}_{tb}_{dh}", "s")

        @blk.gpsimd
        def _(g):
            for s_ in range(NCB):
                if DEBUG_TAPS and s_ == 0:
                    cx.wait(g, "ldxn0", "ldxn0")
                    cx.tap(g, "xnTo", xnTob[0][:], [128, 8, TCB + 1], BF16)
                    cx.wait(g, "ldffnw", "ldffnw")
                    cx.tap(g, "wgTb", wgTb[:, :, 0:128], [128, 8, 128], BF16)
                    cx.wait(g, "ldwr0", "ldwro")
                    cx.tap(g, "wrob", wrob[:, :, 0:128], [128, 8, 128], BF16)
                    cx.mark("tap_wrob", "std")
                    cx.wait(g, "bmix0", "v")
                    cx.tap(g, "xrT", xrT[:], [128, 8, TCB], BF16)
                    cx.wait(g, "revac0_7", "s")
                    cx.tap(g, "srT", srT[:], [128, 8, TCB], BF16)
                    cx.mark("tapB_sr", "std")
                    cx.wait(g, "smul0", "v")
                    cx.tap(g, "yos", srT[:], [128, 8, TCB], BF16)
                    cx.wait(g, f"hm0_{NHB-1}", "v")
                    cx.tap(g, "hact", hact[:], [128, 24, TCB], BF16)
                    cx.mark("tapsB_done", "std")
                for tb in range(2):
                    cx.wait(g, f"oadd{s_}_{tb}", "v")
                    dst = out_own[s_ * TCB + tb * 128: s_ * TCB + (tb + 1) * 128, :]
                    cx.bump(g.dma_start(out=dst, in_=outt[:]), f"sto{tb}", 16)
                    cx.mark(f"owr{s_}_{tb}", f"sto{tb}")

    sB.close()

    totals = {k: cx.cnt[k] for k in ["sto0", "sto1", "sty", "stx", "stp0", "stp1", "std"]}
    with nc.Block() as blk:
        @blk.gpsimd
        def _(g):
            for k, v in totals.items():
                if v:
                    g.wait_ge(cx.sem[k], v)

        @blk.sync
        def _(sync):
            for k, v in totals.items():
                if v:
                    sync.wait_ge(cx.sem[k], v)

    return nc, cx.newM


_nc_cache = {}


def _get_program():
    if "nc" not in _nc_cache:
        _, M1 = build({})
        nc, M2 = build(M1)
        assert M1 == M2, "milestone mismatch between emission passes"
        _nc_cache["nc"] = nc
    return _nc_cache["nc"]


def kernel(L0_attn_x, L0_attn_time_decay, L0_attn_time_first,
           L0_attn_time_mix_k, L0_attn_time_mix_v, L0_attn_time_mix_r,
           L0_attn_W_key, L0_attn_W_value, L0_attn_W_receptance, L0_attn_W_output,
           L0_ffn_w_gate, L0_ffn_w_up, L0_ffn_w_down):
    x = np.ascontiguousarray(np.asarray(L0_attn_x, dtype=np.float32))
    wk = np.ascontiguousarray(np.asarray(L0_attn_W_key, np.float32).T)
    wv = np.ascontiguousarray(np.asarray(L0_attn_W_value, np.float32).T)
    wr = np.ascontiguousarray(np.asarray(L0_attn_W_receptance, np.float32).T)
    wo = np.ascontiguousarray(np.asarray(L0_attn_W_output, np.float32).T)
    wg = np.ascontiguousarray(np.asarray(L0_ffn_w_gate, np.int32).T)
    wu = np.ascontiguousarray(np.asarray(L0_ffn_w_up, np.int32).T)
    wd = np.ascontiguousarray(np.asarray(L0_ffn_w_down, np.int32).T)

    def perm(v):
        return np.ascontiguousarray(np.asarray(v, np.float32).reshape(8, 128).T)

    tmk = perm(L0_attn_time_mix_k)
    tmv = perm(L0_attn_time_mix_v)
    tmr = perm(L0_attn_time_mix_r)
    td = perm(L0_attn_time_decay)
    tf = perm(L0_attn_time_first)
    ident = np.eye(128, dtype=ml_dtypes.bfloat16)

    nc = _get_program()
    maps = []
    for core in range(8):
        b, h = core // 2, core % 2
        maps.append({
            "x_all": np.ascontiguousarray(x[b]),
            "x_own": np.ascontiguousarray(x[b, h * OWN:(h + 1) * OWN]),
            "wkT": wk, "wvT": wv, "wrT": wr, "woT": wo,
            "wgT": wg, "wuT": wu, "wdT": wd,
            "tmk": tmk, "tmv": tmv, "tmr": tmr, "td": td, "tf": tf,
            "ident": ident,
        })
    res = run_bass_kernel_spmd(nc, maps, core_ids=list(range(8)))
    out = np.empty((B, T, C), np.float32)
    for core in range(8):
        b, h = core // 2, core % 2
        out[b, h * OWN:(h + 1) * OWN] = np.asarray(res.results[core]["out_own"])
    return out


# version 11
# speedup vs baseline: 1.1590x; 1.1590x over previous
import numpy as np
import ml_dtypes
import concourse.bass as bass
import concourse.mybir as mybir
from concourse.bass_utils import run_bass_kernel_spmd
from contextlib import ExitStack

F32 = mybir.dt.float32
BF16 = mybir.dt.bfloat16
I32 = mybir.dt.int32
AF = mybir.ActivationFunctionType
ALU = mybir.AluOpType

B, T, C, HID = 4, 8192, 1024, 3072
TCA = 512
NCA = T // TCA
OWN = T // 2
CH = C // 2
NDBL = CH // 128
TCB = 256
NCB = OWN // TCB
NCC = C // 128
NHB = HID // 128
EPS = 1e-5
NEWTON_ITERS = 8
PAIRS = [[0, 1], [2, 3], [4, 5], [6, 7]]
SEMS = ["lds", "ldp0", "ldp1", "ldx0", "ldx1", "ldxn0", "ldxn1", "ldy", "ldxo", "ldwro",
        "ldffnw", "ldos", "stp0", "stp1", "sty", "stx", "sto0", "sto1",
        "std", "cl", "v", "s", "t"]
DEBUG_TAPS = False


class Ctx:
    def __init__(self, nc, M):
        self.nc = nc
        self.M = M
        self.newM = {}
        self.cnt = {k: 0 for k in SEMS}
        self.sem = {}
        self.pass1 = not M

    def mark(self, key, semname):
        self.newM[key] = self.cnt[semname]

    def bump(self, inst, semname, n=1):
        inst.then_inc(self.sem[semname], n)
        self.cnt[semname] += n

    def wait(self, eng, key, semname):
        if self.pass1:
            return
        val = self.M.get(key)
        assert val is not None, f"missing milestone {key}"
        if val > 0:
            eng.wait_ge(self.sem[semname], val)

    def tap(self, g, name, src_ap, shape, dtype):
        t = self.nc.dram_tensor("tap_" + name, shape, dtype, kind="ExternalOutput").ap()
        self.bump(g.dma_start(out=t, in_=src_ap), "std", 16)
        self.mark("tap_" + name, "std")


def build(M):
    nc = bass.Bass("TRN2", target_bir_lowering=False, detect_race_conditions=False)
    nc.cache_partition_id()
    cx = Ctx(nc, M)

    xall = nc.dram_tensor("x_all", [T, C], F32, kind="ExternalInput").ap()
    xown = nc.dram_tensor("x_own", [OWN, C], F32, kind="ExternalInput").ap()
    wkT = nc.dram_tensor("wkT", [C, CH], F32, kind="ExternalInput").ap()
    wvT = nc.dram_tensor("wvT", [C, CH], F32, kind="ExternalInput").ap()
    wrT = nc.dram_tensor("wrT", [C, C], F32, kind="ExternalInput").ap()
    woT = nc.dram_tensor("woT", [C, C], F32, kind="ExternalInput").ap()
    wgT = nc.dram_tensor("wgT", [C, HID], I32, kind="ExternalInput").ap()
    wuT = nc.dram_tensor("wuT", [C, HID], I32, kind="ExternalInput").ap()
    wdT = nc.dram_tensor("wdT", [HID, C], I32, kind="ExternalInput").ap()
    tmk_i = nc.dram_tensor("tmk", [128, 8], F32, kind="ExternalInput").ap()
    tmv_i = nc.dram_tensor("tmv", [128, 8], F32, kind="ExternalInput").ap()
    tmr_i = nc.dram_tensor("tmr", [128, 8], F32, kind="ExternalInput").ap()
    td_i = nc.dram_tensor("td", [128, NDBL], F32, kind="ExternalInput").ap()
    tf_i = nc.dram_tensor("tf", [128, NDBL], F32, kind="ExternalInput").ap()
    ident_i = nc.dram_tensor("ident", [128, 128], BF16, kind="ExternalInput").ap()

    out_own = nc.dram_tensor("out_own", [OWN, C], F32, kind="ExternalOutput").ap()

    xnT_hbm = nc.dram_tensor("xnT_hbm", [C, T + 1], BF16, kind="Internal").ap()
    cin_all = nc.dram_tensor("cin_all", [NCA * CH, TCA], BF16, kind="Internal").ap()
    cout_all = nc.dram_tensor("cout_all", [NCA * C, TCA], BF16, kind="Internal").ap()
    own_xn = nc.dram_tensor("own_xn", [C, OWN + 1], BF16, kind="Internal").ap()
    own_y = nc.dram_tensor("own_y", [C, OWN], BF16, kind="Internal").ap()
    wr_hbm = nc.dram_tensor("wr_hbm", [C, C], BF16, kind="Internal").ap()
    wo_hbm = nc.dram_tensor("wo_hbm", [C, C], BF16, kind="Internal").ap()

    for semname in SEMS:
        cx.sem[semname] = nc.alloc_semaphore("sem_" + semname)

    identb = nc.alloc_sbuf_tensor("identb", [128, 128], BF16)
    tmk = nc.alloc_sbuf_tensor("tmk_t", [128, 8], F32)
    tmv = nc.alloc_sbuf_tensor("tmv_t", [128, 8], F32)
    tmr = nc.alloc_sbuf_tensor("tmr_t", [128, 8], F32)
    tdt = nc.alloc_sbuf_tensor("td_t", [128, NDBL], F32)
    tft = nc.alloc_sbuf_tensor("tf_t", [128, NDBL], F32)
    eu = nc.alloc_sbuf_tensor("eu_t", [128, NDBL], F32)
    ddec = nc.alloc_sbuf_tensor("ddec_t", [128, NDBL], F32)
    e1n = nc.alloc_sbuf_tensor("e1n_t", [128, NDBL], F32)
    c1p5 = nc.alloc_sbuf_tensor("c1p5", [128, 64], F32)


    sA = ExitStack()
    ptr_banks = [sA.enter_context(nc.psum_tensor(f"ptr{i}", [128, 1024], BF16)) for i in range(4)]
    pk_banks = [sA.enter_context(nc.psum_tensor(f"pk{i}", [128, 512], F32)) for i in range(2)]
    pv_banks = [sA.enter_context(nc.psum_tensor(f"pv{i}", [128, 512], F32)) for i in range(2)]
    wkTb = sA.enter_context(nc.sbuf_tensor("wkTb", [128, 8, CH], BF16))
    wvTb = sA.enter_context(nc.sbuf_tensor("wvTb", [128, 8, CH], BF16))
    dfull = sA.enter_context(nc.sbuf_tensor("dfull", [128, NDBL, TCA], F32))
    carryPv = sA.enter_context(nc.sbuf_tensor("carryPv", [128, 8], F32))
    carryQv = sA.enter_context(nc.sbuf_tensor("carryQv", [128, 8], F32))
    wtmpP = sA.enter_context(nc.sbuf_tensor("wtmpP", [128, 256], F32))
    wtmpQ = sA.enter_context(nc.sbuf_tensor("wtmpQ", [128, 256], F32))
    onesrow = sA.enter_context(nc.sbuf_tensor("onesrow", [128, TCA], F32))
    sNs = sA.enter_context(nc.sbuf_tensor("sNs", [128, 64], F32))
    bNs = sA.enter_context(nc.sbuf_tensor("bNs", [128, 64], F32))

    sP = ExitStack()
    wsrcf = [sP.enter_context(nc.sbuf_tensor(f"wsrcf{i}", [128, 4, 1024], F32)) for i in range(2)]
    wcvtb = [sP.enter_context(nc.sbuf_tensor(f"wcvtb{i}", [128, 4, 1024], BF16)) for i in range(2)]


    pieces = []
    for w_src, dst_res, tag in [(wkT, wkTb, "wk"), (wvT, wvTb, "wv")]:
        for ph in range(2):
            src = w_src[ph * 512:(ph + 1) * 512, :].rearrange("(a p) d -> p a d", p=128)
            pieces.append(("f2res", src, (dst_res, ph), 127.0, f"{tag}{ph}"))
    for w_src, w_dst, scale, tag in [(wrT, wr_hbm, 0.5 * 127.0, "wr"), (woT, wo_hbm, 0.5, "wo")]:
        for ph in range(2):
            src = w_src[ph * 512:(ph + 1) * 512, :].rearrange("(a p) d -> p a d", p=128)
            dst = w_dst[ph * 512:(ph + 1) * 512, :].rearrange("(a p) d -> p a d", p=128)
            pieces.append(("f2hbm", src, dst, scale, f"{tag}{ph}"))

    with nc.Block() as blk:
        @blk.sync
        def _(sync):
            for t_in, t_sb in [(tmk_i, tmk), (tmv_i, tmv), (tmr_i, tmr),
                               (td_i, tdt), (tf_i, tft), (ident_i, identb)]:
                cx.bump(sync.dma_start(out=t_sb[:], in_=t_in), "lds", 16)
            cx.mark("ld_small", "lds")
            for i, (kind, src, dst, scale, tag) in enumerate(pieces):
                par = i % 2
                buf = wsrcf[par][:, :, 0:CH] if kind == "f2res" else wsrcf[par][:]
                if i >= 2:
                    cx.wait(sync, f"pcvt{i-2}", "v")
                cx.bump(sync.dma_start(out=buf, in_=src), f"ldp{par}", 16)
                cx.mark(f"pld{i}", f"ldp{par}")

        @blk.scalar
        def _(scalar):
            cx.wait(scalar, "ld_small", "lds")
            cx.bump(scalar.activation(eu[:], tft[:], AF.Exp), "s", 1)
            cx.mark("eu", "s")
            cx.bump(scalar.activation(ddec[:], tdt[:], AF.Exp), "s", 1)
            cx.mark("etd", "s")
            cx.wait(scalar, "e1n", "v")
            cx.bump(scalar.activation(ddec[:], e1n[:], AF.Exp), "s", 1)
            cx.mark("ddec", "s")

        @blk.vector
        def _(vector):
            vector.memset(c1p5[:], 1.5)
            cx.wait(vector, "etd", "s")
            cx.bump(vector.tensor_scalar(e1n[:], ddec[:], -1.0, None, ALU.mult), "v", 1)
            cx.mark("e1n", "v")
            cx.wait(vector, "ddec", "s")
            vector.memset(onesrow[:], 1.0)
            for ccN in range(NDBL):
                inst = vector.tensor_scalar(dfull[:, ccN, :], onesrow[:],
                                            ddec[:, ccN:ccN + 1], None, ALU.mult)
            cx.bump(inst, "v", 1)
            cx.mark("dfull", "v")
            last_cvt_user = {}
            for i, (kind, src, dst, scale, tag) in enumerate(pieces):
                par = i % 2
                cx.wait(vector, f"pld{i}", f"ldp{par}")
                if kind == "f2res":
                    dst_res, ph = dst
                    inst = vector.tensor_scalar(dst_res[:, ph * 4:(ph + 1) * 4, :],
                                                wsrcf[par][:, :, 0:CH],
                                                scale, None, ALU.mult)
                else:
                    cbuf = ("wcvtb", par)
                    prev = last_cvt_user.get(cbuf)
                    if prev is not None:
                        cx.wait(vector, f"pst{prev}", f"stp{prev % 2}")
                    last_cvt_user[cbuf] = i
                    inst = vector.tensor_scalar(wcvtb[par][:], wsrcf[par][:], scale, None, ALU.mult)
                cx.bump(inst, "v", 1)
                cx.mark(f"pcvt{i}", "v")
            cx.mark("prep_v_done", "v")

        @blk.gpsimd
        def _(g):
            for i, (kind, src, dst, scale, tag) in enumerate(pieces):
                if kind == "f2res":
                    continue
                par = i % 2
                cx.wait(g, f"pcvt{i}", "v")
                cx.bump(g.dma_start(out=dst, in_=wcvtb[par][:]), f"stp{par}", 16)
                cx.mark(f"pst{i}", f"stp{par}")
            cx.mark("prep_st0_done", "stp0")
            cx.mark("prep_st1_done", "stp1")

    sP.close()

    xf2 = [sA.enter_context(nc.sbuf_tensor(f"xf{i}", [128, 4, 1024], F32)) for i in range(2)]
    sqscr = sA.enter_context(nc.sbuf_tensor("sqscr", [128, 1024], BF16))
    ssq = sA.enter_context(nc.sbuf_tensor("ssq", [128, 64], F32))
    sm = sA.enter_context(nc.sbuf_tensor("sm", [128, 64], F32))
    meanT = sA.enter_context(nc.sbuf_tensor("meanT", [128, 64], F32))
    varT = sA.enter_context(nc.sbuf_tensor("varT", [128, 64], F32))
    sNb = [sA.enter_context(nc.sbuf_tensor(f"sN{i}", [128, 64], F32)) for i in range(2)]
    bN = sA.enter_context(nc.sbuf_tensor("bN", [128, 64], F32))
    nt1 = sA.enter_context(nc.sbuf_tensor("nt1", [128, 64], F32))
    nt2 = sA.enter_context(nc.sbuf_tensor("nt2", [128, 64], F32))
    vtmp = sA.enter_context(nc.sbuf_tensor("vtmp", [128, 64], F32))
    xn_tok = sA.enter_context(nc.sbuf_tensor("xn_tok", [128, 4, 1024], BF16))
    xnT = sA.enter_context(nc.sbuf_tensor("xnT", [128, 8, TCA + 1], BF16))
    dxx = sA.enter_context(nc.sbuf_tensor("dxx", [128, TCA], BF16))
    xkT = sA.enter_context(nc.sbuf_tensor("xkT", [128, 8, TCA], BF16))
    xvT = sA.enter_context(nc.sbuf_tensor("xvT", [128, 8, TCA], BF16))
    ekf = sA.enter_context(nc.sbuf_tensor("ekf", [128, 4, TCA + 2], F32))
    vvf = sA.enter_context(nc.sbuf_tensor("vvf", [128, 4, TCA], F32))
    ekvf = sA.enter_context(nc.sbuf_tensor("ekvf", [128, 4, TCA + 2], F32))
    Pt = sA.enter_context(nc.sbuf_tensor("Pt", [128, 4, TCA], F32))
    Qt = sA.enter_context(nc.sbuf_tensor("Qt", [128, 4, TCA], F32))
    numt = sA.enter_context(nc.sbuf_tensor("numt", [128, 4, TCA], F32))
    dent = sA.enter_context(nc.sbuf_tensor("dent", [128, 4, TCA], F32))
    yb = sA.enter_context(nc.sbuf_tensor("yb", [128, NDBL, TCA], BF16))

    with nc.Block() as blk:
        @blk.sync
        def _(sync):
            for j in range(NCA):
                if j == 0:
                    cx.wait(sync, "prep_v_done", "v")
                    cx.wait(sync, "prep_st0_done", "stp0")
                    cx.wait(sync, "prep_st1_done", "stp1")
                elif j >= 2:
                    cx.wait(sync, f"lnap{j-2}", "v")
                    cx.wait(sync, f"sq{j-2}", "s")
                    if DEBUG_TAPS and j == 2:
                        cx.wait(sync, "tapsA_done", "std")
                src = xall[j * TCA:(j + 1) * TCA, :].rearrange("(a p) c -> p a c", p=128)
                cx.bump(sync.dma_start(out=xf2[j % 2][:], in_=src), f"ldx{j % 2}", 16)
                cx.mark(f"ldx{j}", f"ldx{j % 2}")

        @blk.scalar
        def _(scalar):
            def emit_sq_relay(j):
                cx.wait(scalar, f"ldx{j}", f"ldx{j % 2}")
                cx.wait(scalar, f"stats{j-1}" if j > 0 else "statsinit", "v")
                for a in range(4):
                    scalar.activation(sqscr[:], xf2[j % 2][:, a, :], AF.Square,
                                      accum_out=ssq[:, a:a + 1])
                    inst = scalar.activation(sqscr[:], xf2[j % 2][:, a, :], AF.Copy,
                                             accum_out=sm[:, a:a + 1])
                cx.bump(inst, "s", 1)
                cx.mark(f"sq{j}", "s")

            def emit_relay(j):
                cx.wait(scalar, f"stats{j}", "v")
                scalar.copy(sNs[:], sNb[(NEWTON_ITERS + 1) % 2][:])
                inst = scalar.copy(bNs[:], bN[:])
                cx.bump(inst, "s", 1)
                cx.mark(f"lnrelay{j}", "s")

            emit_sq_relay(0)
            emit_relay(0)
            emit_sq_relay(1)
            for j in range(NCA):
                for ccN in range(8):
                    cx.wait(scalar, f"tr{j}_{ccN}", "t")
                    cx.wait(scalar, f"ghost{j}", "v")
                    if j > 0 and ccN == 0:
                        cx.wait(scalar, f"xnwr{j-1}", "stx")
                    inst = scalar.copy(xnT[:, ccN, 1:TCA + 1], ptr_banks[ccN % 4][:, 0:TCA])
                    cx.bump(inst, "s", 1)
                    cx.mark(f"trevac{j}_{ccN}", "s")
                for db in range(NDBL):
                    cx.wait(scalar, f"kv{j}_{db}", "t")
                    if db == 0 and j > 0:
                        cx.wait(scalar, f"scan{j-1}_0", "v")
                    scalar.activation(ekf[:, db, 1:257], pk_banks[db % 2][:, 0:256], AF.Exp)
                    cx.bump(scalar.activation(ekf[:, db, 258:TCA + 2], pk_banks[db % 2][:, 256:TCA], AF.Exp), "s", 1)
                    cx.bump(scalar.copy(vvf[:, db, :], pv_banks[db % 2][:]), "s", 1)
                    cx.mark(f"ekevac{j}_{db}", "s")
                    if db == 1 and j + 1 < NCA:
                        emit_relay(j + 1)
                cx.wait(scalar, f"den{j}_0", "v")
                for db in range(NDBL):
                    inst = scalar.activation(vvf[:, db, :], dent[:, db, :], AF.Ln)
                cx.bump(inst, "s", 1)
                cx.mark(f"lnd{j}_0", "s")
                cx.wait(scalar, f"negd{j}_0", "v")
                for db in range(NDBL):
                    inst = scalar.activation(vvf[:, db, :], dent[:, db, :], AF.Exp)
                cx.bump(inst, "s", 1)
                cx.mark(f"expd{j}_0", "s")
                if j + 2 < NCA:
                    emit_sq_relay(j + 2)

        @blk.vector
        def _(vector):
            def emit_stats_lnap(j):
                cx.wait(vector, f"sq{j}", "s")
                vector.tensor_scalar(meanT[:], sm[:], 1.0 / C, None, ALU.mult)
                vector.tensor_scalar(vtmp[:], ssq[:], 1.0 / C, None, ALU.mult)
                vector.tensor_mul(nt1[:], meanT[:], meanT[:])
                vector.tensor_sub(nt2[:], vtmp[:], nt1[:])
                vector.tensor_scalar(varT[:], nt2[:], EPS, None, ALU.add)
                vector.memset(sNb[0][:], 1.0)
                cur = 0
                for _ in range(NEWTON_ITERS):
                    vector.tensor_mul(nt1[:], sNb[cur][:], sNb[cur][:])
                    vector.tensor_mul(nt2[:], nt1[:], varT[:])
                    vector.scalar_tensor_tensor(nt1[:], nt2[:], -0.5, c1p5[:], ALU.mult, ALU.add)
                    vector.tensor_mul(sNb[1 - cur][:], sNb[cur][:], nt1[:])
                    cur = 1 - cur
                vector.tensor_scalar(sNb[1 - cur][:], sNb[cur][:], 1.0 / 127.0, None, ALU.mult)
                sN = sNb[1 - cur]
                inst = vector.tensor_mul(bN[:], meanT[:], sN[:])
                cx.bump(inst, "v", 1)
                cx.mark(f"stats{j}", "v")
                cx.wait(vector, f"lnrelay{j}", "s")
                if j > 0:
                    cx.wait(vector, f"tr{j-1}_7", "t")
                for a in range(4):
                    inst = vector.tensor_scalar(xn_tok[:, a, :], xf2[j % 2][:, a, :],
                                                sNs[:, a:a + 1], bNs[:, a:a + 1],
                                                ALU.mult, ALU.subtract)
                cx.bump(inst, "v", 1)
                cx.mark(f"lnap{j}", "v")

            for j in range(NCA):
                if j == 0:
                    cx.wait(vector, "prep_st0_done", "stp0")
                    cx.wait(vector, "prep_st1_done", "stp1")
                    vector.memset(sm[:], 1.0)
                    vector.memset(carryPv[:], 0.0)
                    vector.memset(carryQv[:], 0.0)
                    inst = vector.memset(ssq[:], 1.0)
                    cx.bump(inst, "v", 1)
                    cx.mark("statsinit", "v")
                    emit_stats_lnap(0)
                    for ccN in range(8):
                        inst = vector.memset(xnT[:, ccN, 0:1], 0.0)
                    cx.bump(inst, "v", 1)
                    cx.mark("ghost0", "v")
                cx.wait(vector, f"trevac{j}_7", "s")
                if j > 0:
                    cx.wait(vector, f"kv{j-1}_{NDBL-1}", "t")
                for ccN in range(8):
                    vector.tensor_sub(dxx[:], xnT[:, ccN, 1:TCA + 1], xnT[:, ccN, 0:TCA])
                    vector.scalar_tensor_tensor(xkT[:, ccN, :], dxx[:], tmk[:, ccN:ccN + 1],
                                                xnT[:, ccN, 0:TCA], ALU.mult, ALU.add)
                    inst = vector.scalar_tensor_tensor(xvT[:, ccN, :], dxx[:], tmv[:, ccN:ccN + 1],
                                                       xnT[:, ccN, 0:TCA], ALU.mult, ALU.add)
                cx.bump(inst, "v", 1)
                cx.mark(f"mix{j}", "v")
                if j + 1 < NCA:
                    emit_stats_lnap(j + 1)
                    for ccN in range(8):
                        inst = vector.tensor_copy(xnT[:, ccN, 0:1], xnT[:, ccN, TCA:TCA + 1])
                    cx.bump(inst, "v", 1)
                    cx.mark(f"ghost{j+1}", "v")
                for db in range(NDBL):
                    cx.wait(vector, f"ekevac{j}_{db}", "s")
                if j > 0:
                    cx.wait(vector, f"ywr{j-1}", "sty")
                vector.tensor_copy(ekvf[:, 0:4, 0:1], carryPv[:, 0:4])
                vector.tensor_copy(ekf[:, 0:4, 0:1], carryQv[:, 0:4])
                for db in range(NDBL):
                    vector.tensor_mul(ekvf[:, db, 1:257], ekf[:, db, 1:257], vvf[:, db, 0:256])
                    vector.tensor_mul(ekvf[:, db, 258:TCA + 2], ekf[:, db, 258:TCA + 2], vvf[:, db, 256:TCA])
                for db in range(NDBL):
                    vector.tensor_tensor_scan(Pt[:, db, 0:256], dfull[:, db, 0:256],
                                              ekvf[:, db, 0:256], 0.0, ALU.mult, ALU.add)
                    vector.tensor_tensor_scan(Qt[:, db, 0:256], dfull[:, db, 0:256],
                                              ekf[:, db, 0:256], 0.0, ALU.mult, ALU.add)
                for db in range(NDBL):
                    vector.tensor_mul(wtmpP[:, db * 64:(db + 1) * 64], Pt[:, db, 192:256], dfull[:, db, 0:64])
                    vector.tensor_mul(wtmpQ[:, db * 64:(db + 1) * 64], Qt[:, db, 192:256], dfull[:, db, 0:64])
                for db in range(NDBL):
                    vector.tensor_add(ekvf[:, db, 257:258], wtmpP[:, db * 64 + 63:db * 64 + 64], ekvf[:, db, 256:257])
                    vector.tensor_add(ekf[:, db, 257:258], wtmpQ[:, db * 64 + 63:db * 64 + 64], ekf[:, db, 256:257])
                for db in range(NDBL):
                    vector.tensor_tensor_scan(Pt[:, db, 256:TCA], dfull[:, db, 0:256],
                                              ekvf[:, db, 257:TCA + 1], 0.0, ALU.mult, ALU.add)
                    vector.tensor_tensor_scan(Qt[:, db, 256:TCA], dfull[:, db, 0:256],
                                              ekf[:, db, 257:TCA + 1], 0.0, ALU.mult, ALU.add)
                for db in range(NDBL):
                    vector.scalar_tensor_tensor(numt[:, db, 0:256], ekvf[:, db, 1:257],
                                                eu[:, db:db + 1], Pt[:, db, 0:256],
                                                ALU.mult, ALU.add)
                    vector.scalar_tensor_tensor(numt[:, db, 256:TCA], ekvf[:, db, 258:TCA + 2],
                                                eu[:, db:db + 1], Pt[:, db, 256:TCA],
                                                ALU.mult, ALU.add)
                    vector.scalar_tensor_tensor(dent[:, db, 0:256], ekf[:, db, 1:257],
                                                eu[:, db:db + 1], Qt[:, db, 0:256],
                                                ALU.mult, ALU.add)
                    cx.last_den_inst = vector.scalar_tensor_tensor(
                        dent[:, db, 256:TCA], ekf[:, db, 258:TCA + 2],
                        eu[:, db:db + 1], Qt[:, db, 256:TCA],
                        ALU.mult, ALU.add)
                cx.bump(cx.last_den_inst, "v", 1)
                cx.mark(f"den{j}_0", "v")
                for db in range(NDBL):
                    vector.tensor_mul(wtmpP[:, db * 64:(db + 1) * 64], Pt[:, db, 448:TCA], dfull[:, db, 0:64])
                    vector.tensor_mul(wtmpQ[:, db * 64:(db + 1) * 64], Qt[:, db, 448:TCA], dfull[:, db, 0:64])
                for db in range(NDBL):
                    vector.tensor_add(carryPv[:, db:db + 1], wtmpP[:, db * 64 + 63:db * 64 + 64], ekvf[:, db, TCA + 1:TCA + 2])
                    vector.tensor_add(carryQv[:, db:db + 1], wtmpQ[:, db * 64 + 63:db * 64 + 64], ekf[:, db, TCA + 1:TCA + 2])
                cx.wait(vector, f"lnd{j}_0", "s")
                for db in range(NDBL):
                    inst = vector.tensor_scalar(dent[:, db, :], vvf[:, db, :], -1.0, None, ALU.mult)
                cx.bump(inst, "v", 1)
                cx.mark(f"negd{j}_0", "v")
                cx.wait(vector, f"expd{j}_0", "s")
                for db in range(NDBL):
                    inst = vector.tensor_mul(yb[:, db, :], numt[:, db, :], vvf[:, db, :])
                cx.bump(inst, "v", 1)
                cx.mark(f"scan{j}_0", "v")

        @blk.tensor
        def _(tensor):
            def emit_tr(j, ccs):
                for ccN in ccs:
                    if j > 0 or ccN >= 4:
                        jw, cw = (j, ccN - 4) if ccN >= 4 else (j - 1, ccN + 4)
                        cx.wait(tensor, f"trevac{jw}_{cw}", "s")
                    for a in range(4):
                        inst = tensor.transpose(ptr_banks[ccN % 4][:, a * 128:(a + 1) * 128],
                                                xn_tok[:, a, ccN * 128:(ccN + 1) * 128], identb[:])
                    cx.bump(inst, "t", 1)
                    cx.mark(f"tr{j}_{ccN}", "t")

            def emit_kv(j, dbs):
                for db in dbs:
                    if j > 0 or db >= 2:
                        jw, dw = (j, db - 2) if db >= 2 else (j - 1, db + 2)
                        cx.wait(tensor, f"ekevac{jw}_{dw}", "s")
                    for ccN in range(8):
                        tensor.matmul(pk_banks[db % 2][:], wkTb[:, ccN, db * 128:(db + 1) * 128],
                                      xkT[:, ccN, :], start=(ccN == 0), stop=(ccN == 7))
                    for ccN in range(8):
                        inst = tensor.matmul(pv_banks[db % 2][:], wvTb[:, ccN, db * 128:(db + 1) * 128],
                                             xvT[:, ccN, :], start=(ccN == 0), stop=(ccN == 7))
                    cx.bump(inst, "t", 1)
                    cx.mark(f"kv{j}_{db}", "t")

            cx.wait(tensor, "lnap0", "v")
            emit_tr(0, range(8))
            for j in range(NCA):
                cx.wait(tensor, f"mix{j}", "v")
                emit_kv(j, range(0, 2))
                if j + 1 < NCA:
                    cx.wait(tensor, f"lnap{j+1}", "v")
                    emit_tr(j + 1, range(0, 4))
                emit_kv(j, range(2, 4))
                if j + 1 < NCA:
                    emit_tr(j + 1, range(4, 8))

        @blk.gpsimd
        def _(g):
            for j in range(NCA):
                if DEBUG_TAPS and j == 0:
                    cx.wait(g, "scan0_0", "v")
                    cx.tap(g, "xf", xf2[0][:], [128, 4, 1024], F32)
                    cx.tap(g, "sm", sm[:], [128, 64], F32)
                    cx.tap(g, "ssq", ssq[:], [128, 64], F32)
                    cx.tap(g, "varT", varT[:], [128, 64], F32)
                    cx.tap(g, "meanT", meanT[:], [128, 64], F32)
                    cx.tap(g, "vtmp", vtmp[:], [128, 64], F32)
                    cx.tap(g, "nt1", nt1[:], [128, 64], F32)
                    cx.tap(g, "nt2", nt2[:], [128, 64], F32)
                    cx.tap(g, "xn_tok", xn_tok[:], [128, 4, 1024], BF16)
                    cx.tap(g, "xkT", xkT[:], [128, 8, TCA], BF16)
                    cx.tap(g, "xvT", xvT[:], [128, 8, TCA], BF16)
                    cx.tap(g, "sN", sNb[(NEWTON_ITERS + 1) % 2][:], [128, 64], F32)
                    cx.tap(g, "bN", bN[:], [128, 64], F32)
                    cx.tap(g, "ekf", ekf[:], [128, 4, TCA + 2], F32)
                    cx.tap(g, "ekvf", ekvf[:], [128, 4, TCA + 2], F32)
                    cx.tap(g, "Pt", Pt[:], [128, 4, TCA], F32)
                    cx.tap(g, "Qt", Qt[:], [128, 4, TCA], F32)
                    cx.tap(g, "numt", numt[:], [128, 4, TCA], F32)
                    cx.tap(g, "dent", dent[:], [128, 4, TCA], F32)
                    cx.mark("tapsA_half0", "std")
                cx.wait(g, f"scan{j}_0", "v")
                if DEBUG_TAPS and j == 0:
                    cx.tap(g, "yb", yb[:], [128, NDBL, TCA], BF16)
                dst = cin_all[j * CH:(j + 1) * CH, :].rearrange("(cc p) t -> p cc t", p=128)
                cx.bump(g.dma_start(out=dst, in_=yb[:]), "sty", 16)
                cx.mark(f"ywr{j}", "sty")
                cx.wait(g, f"ywr{j}", "sty")
                cx.bump(g.collective_compute(
                    "AllGather", ALU.bypass,
                    ins=[cin_all[j * CH:(j + 1) * CH, :]],
                    outs=[cout_all[j * C:(j + 1) * C, :]],
                    replica_groups=PAIRS), "cl", 1)
                cx.mark(f"cl{j}", "cl")
                cx.wait(g, f"trevac{j}_7", "s")
                if j == 0:
                    cx.wait(g, "ghost0", "v")
                    dst2 = xnT_hbm[:, 0:TCA + 1].rearrange("(cc p) t -> p cc t", p=128)
                    cx.bump(g.dma_start(out=dst2, in_=xnT[:]), "stx", 16)
                else:
                    dst2 = xnT_hbm[:, 1 + j * TCA:1 + (j + 1) * TCA].rearrange("(cc p) t -> p cc t", p=128)
                    cx.bump(g.dma_start(out=dst2, in_=xnT[:, :, 1:TCA + 1]), "stx", 16)
                cx.mark(f"xnwr{j}", "stx")
                if DEBUG_TAPS and j == 0:
                    cx.tap(g, "xnT", xnT[:], [128, 8, TCA + 1], BF16)
                    cx.mark("tapsA_done", "std")

    cx.newM["A_v_done"] = cx.cnt["v"]
    cx.newM["A_s_done"] = cx.cnt["s"]
    cx.newM["A_t_done"] = cx.cnt["t"]
    cx.newM["A_sty_done"] = cx.cnt["sty"]
    cx.newM["A_stx_done"] = cx.cnt["stx"]
    cx.newM["A_cl_done"] = cx.cnt["cl"]
    sA.close()

    sB = ExitStack()
    pr_banks = [sB.enter_context(nc.psum_tensor(f"pr{i}", [128, 512], F32)) for i in range(2)]
    pg_banks = [sB.enter_context(nc.psum_tensor(f"pg{i}", [128, 512], F32)) for i in range(2)]
    pu_banks = [sB.enter_context(nc.psum_tensor(f"pu{i}", [128, 512], F32)) for i in range(2)]
    po_banks = [sB.enter_context(nc.psum_tensor(f"po{i}", [128, 512], F32)) for i in range(2)]
    wgTb = sB.enter_context(nc.sbuf_tensor("wgTb", [128, 8, HID], BF16))
    wuTb = sB.enter_context(nc.sbuf_tensor("wuTb", [128, 8, HID], BF16))
    wdb = sB.enter_context(nc.sbuf_tensor("wdb", [128, 24, 1024], BF16))
    wrob = sB.enter_context(nc.sbuf_tensor("wrob", [128, 8, 1024], BF16))
    xnTob = [sB.enter_context(nc.sbuf_tensor(f"xnTo{i}", [128, 8, TCB + 1], BF16)) for i in range(2)]
    dxxB = sB.enter_context(nc.sbuf_tensor("dxxB", [128, TCB], BF16))
    xrT = sB.enter_context(nc.sbuf_tensor("xrT", [128, 8, TCB], BF16))
    srT = sB.enter_context(nc.sbuf_tensor("srT", [128, 8, TCB], BF16))
    yo = sB.enter_context(nc.sbuf_tensor("yo", [128, 8, TCB], BF16))
    sgb = sB.enter_context(nc.sbuf_tensor("sgb", [128, TCB], BF16))
    ubb = sB.enter_context(nc.sbuf_tensor("ubb", [128, TCB], BF16))
    hact = sB.enter_context(nc.sbuf_tensor("hact", [128, 24, TCB], BF16))
    xo = sB.enter_context(nc.sbuf_tensor("xo", [128, 1024], F32))
    outf = sB.enter_context(nc.sbuf_tensor("outf", [128, 1024], F32))
    outt = sB.enter_context(nc.sbuf_tensor("outt", [128, 1024], F32))

    with nc.Block() as blk:
        @blk.gpsimd
        def _(g):
            cx.wait(g, "A_v_done", "v")
            cx.wait(g, "A_s_done", "s")
            cx.wait(g, "A_t_done", "t")
            cx.bump(g.dma_start(out=wgTb[:], in_=wgT.rearrange("(cc p) h -> p cc h", p=128)), "ldffnw", 16)
            cx.bump(g.dma_start(out=wuTb[:], in_=wuT.rearrange("(cc p) h -> p cc h", p=128)), "ldffnw", 16)
            cx.bump(g.dma_start(out=wdb[:], in_=wdT.rearrange("(hb p) d -> p hb d", p=128)), "ldffnw", 16)
            cx.mark("ldffnw", "ldffnw")

        @blk.sync
        def _(sync):
            cx.wait(sync, "A_v_done", "v")
            cx.wait(sync, "A_s_done", "s")
            cx.wait(sync, "A_t_done", "t")
            cx.wait(sync, "A_sty_done", "sty")
            cx.wait(sync, "A_stx_done", "stx")

            HW_ = OWN // 2
            QC = HW_ // TCA
            cv = cout_all.rearrange("(c p) t -> p c t", p=C)

            def yreloc(q, cond, h):
                c0_ = h * (T // TCA // 2) + q * QC
                dst = own_y[:, q * HW_:(q + 1) * HW_].rearrange("p (c t) -> p c t", t=TCA)
                return sync.dma_start(out=dst, in_=cv[:, c0_:c0_ + QC, :], cond=cond)

            cx.wait(sync, "A_cl_done", "cl")
            if cx.pass1:
                cx.cnt["ldos"] += 128
            else:
                pid = sync.partition_id()
                hval = sync.compute_val(pid % 2)
                c0 = sync.compute_val(hval == 0)
                c1 = sync.compute_val(hval == 1)
                cx.bump(sync.dma_start(out=own_xn[:, 0:HW_ + 1], in_=xnT_hbm[:, 0:HW_ + 1], cond=c0), "ldos", 16)
                cx.bump(sync.dma_start(out=own_xn[:, 0:HW_ + 1], in_=xnT_hbm[:, OWN:OWN + HW_ + 1], cond=c1), "ldos", 16)
                cx.bump(yreloc(0, c0, 0), "ldos", 16)
                cx.bump(yreloc(0, c1, 1), "ldos", 16)
                cx.bump(sync.dma_start(out=own_xn[:, HW_ + 1:OWN + 1], in_=xnT_hbm[:, HW_ + 1:OWN + 1], cond=c0), "ldos", 16)
                cx.bump(sync.dma_start(out=own_xn[:, HW_ + 1:OWN + 1], in_=xnT_hbm[:, OWN + HW_ + 1:T + 1], cond=c1), "ldos", 16)
                cx.bump(yreloc(1, c0, 0), "ldos", 16)
                cx.bump(yreloc(1, c1, 1), "ldos", 16)
            cx.mark("ownstage", "ldos")

    with nc.Block() as blk:
        @blk.sync
        def _(sync):
            for s_ in range(NCB):
                if s_ == 0:
                    if not cx.pass1:
                        sync.wait_ge(cx.sem["ldos"], 64)
                elif s_ == NCB // 2:
                    cx.wait(sync, "ownstage", "ldos")
                elif s_ >= 2:
                    cx.wait(sync, f"bmix{s_-2}", "v")
                    cx.wait(sync, f"gu{s_-2}_{NHB-1}", "t")
                if DEBUG_TAPS and s_ == 1:
                    cx.wait(sync, "tapsB_done", "std")
                t0 = s_ * TCB
                src = own_xn[:, t0:t0 + TCB + 1].rearrange("(cc p) t -> p cc t", p=128)
                cx.bump(sync.dma_start(out=xnTob[s_ % 2][:], in_=src), f"ldxn{s_ % 2}", 16)
                cx.mark(f"ldxn{s_}", f"ldxn{s_ % 2}")
                if s_ > 0:
                    cx.wait(sync, f"smul{s_-1}", "v")
                srcy = own_y[:, t0:t0 + TCB].rearrange("(cc p) t -> p cc t", p=128)
                cx.bump(sync.dma_start(out=yo[:], in_=srcy), "ldy", 16)
                cx.mark(f"ldy{s_}", "ldy")
                if s_ > 0:
                    cx.wait(sync, f"oadd{s_-1}_1", "v")
                cx.bump(sync.dma_start(out=xo[:], in_=xown[t0:t0 + 128, :]), "ldxo", 16)
                cx.mark(f"ldxo{s_}_0", "ldxo")
                if s_ > 0:
                    cx.wait(sync, f"out{s_-1}_1_1", "t")
                cx.bump(sync.dma_start(out=wrob[:], in_=wr_hbm.rearrange("(cc p) d -> p cc d", p=128)), "ldwro", 16)
                cx.mark(f"ldwr{s_}", "ldwro")
                cx.wait(sync, f"rmm{s_}_7", "t")
                if DEBUG_TAPS and s_ == 0:
                    cx.wait(sync, "tap_wrob", "std")
                cx.bump(sync.dma_start(out=wrob[:], in_=wo_hbm.rearrange("(cc p) d -> p cc d", p=128)), "ldwro", 16)
                cx.mark(f"ldwo{s_}", "ldwro")
                cx.wait(sync, f"oadd{s_}_0", "v")
                cx.bump(sync.dma_start(out=xo[:], in_=xown[t0 + 128:t0 + 256, :]), "ldxo", 16)
                cx.mark(f"ldxo{s_}_1", "ldxo")

        @blk.vector
        def _(vector):
            for s_ in range(NCB):
                cx.wait(vector, f"ldxn{s_}", f"ldxn{s_ % 2}")
                if s_ > 0:
                    cx.wait(vector, f"rmm{s_-1}_7", "t")
                xnTo = xnTob[s_ % 2]
                for ccN in range(8):
                    vector.tensor_sub(dxxB[:], xnTo[:, ccN, 1:TCB + 1], xnTo[:, ccN, 0:TCB])
                    inst = vector.scalar_tensor_tensor(xrT[:, ccN, :], dxxB[:], tmr[:, ccN:ccN + 1],
                                                       xnTo[:, ccN, 0:TCB], ALU.mult, ALU.add)
                cx.bump(inst, "v", 1)
                cx.mark(f"bmix{s_}", "v")
                cx.wait(vector, f"revac{s_}_7", "s")
                cx.wait(vector, f"ldy{s_}", "ldy")
                if DEBUG_TAPS and s_ == 0:
                    cx.wait(vector, "tapB_sr", "std")
                vector.tensor_scalar(xrT[:], srT[:], 1.0, None, ALU.add)
                inst = vector.tensor_mul(srT[:], yo[:], xrT[:])
                cx.bump(inst, "v", 1)
                cx.mark(f"smul{s_}", "v")
                for hb in range(NHB):
                    cx.wait(vector, f"guevac{s_}_{hb}", "s")
                    if hb == 0 and s_ > 0:
                        cx.wait(vector, f"out{s_-1}_1_1", "t")
                    inst = vector.scalar_tensor_tensor(hact[:, hb, :], sgb[:], 1.0 / 127.0,
                                                       ubb[:], ALU.mult, ALU.mult)
                    cx.bump(inst, "v", 1)
                    cx.mark(f"hm{s_}_{hb}", "v")
                for tb in range(2):
                    cx.wait(vector, f"oevac{s_}_{tb}_1", "s")
                    cx.wait(vector, f"ldxo{s_}_{tb}", "ldxo")
                    prev = (s_, tb - 1) if tb > 0 else (s_ - 1, 1)
                    if prev[0] >= 0:
                        cx.wait(vector, f"owr{prev[0]}_{prev[1]}", f"sto{prev[1]}")
                    inst = vector.tensor_add(outt[:], outf[:], xo[:])
                    cx.bump(inst, "v", 1)
                    cx.mark(f"oadd{s_}_{tb}", "v")

        @blk.tensor
        def _(tensor):
            for s_ in range(NCB):
                cx.wait(tensor, f"bmix{s_}", "v")
                cx.wait(tensor, f"ldwr{s_}", "ldwro")
                for db in range(8):
                    kprev = (f"revac{s_}_{db-2}" if db >= 2 else
                             (f"revac{s_-1}_{6+db}" if s_ > 0 else None))
                    if kprev is not None:
                        cx.wait(tensor, kprev, "s")
                    for ccN in range(8):
                        inst = tensor.matmul(pr_banks[db % 2][:, 0:TCB], wrob[:, ccN, db * 128:(db + 1) * 128],
                                             xrT[:, ccN, :], start=(ccN == 0), stop=(ccN == 7))
                    cx.bump(inst, "t", 1)
                    cx.mark(f"rmm{s_}_{db}", "t")
                xnTo = xnTob[s_ % 2]
                cx.wait(tensor, "ldffnw", "ldffnw")
                for hb in range(NHB):
                    kprev = (f"guevac{s_}_{hb-2}" if hb >= 2 else
                             (f"guevac{s_-1}_{NHB-2+hb}" if s_ > 0 else None))
                    if kprev is not None:
                        cx.wait(tensor, kprev, "s")
                    for ccN in range(8):
                        tensor.matmul(pg_banks[hb % 2][:, 0:TCB], wgTb[:, ccN, hb * 128:(hb + 1) * 128],
                                      xnTo[:, ccN, 1:TCB + 1], start=(ccN == 0), stop=(ccN == 7))
                    for ccN in range(8):
                        inst = tensor.matmul(pu_banks[hb % 2][:, 0:TCB], wuTb[:, ccN, hb * 128:(hb + 1) * 128],
                                             xnTo[:, ccN, 1:TCB + 1], start=(ccN == 0), stop=(ccN == 7))
                    cx.bump(inst, "t", 1)
                    cx.mark(f"gu{s_}_{hb}", "t")
                cx.wait(tensor, f"smul{s_}", "v")
                cx.wait(tensor, f"hm{s_}_{NHB-1}", "v")
                cx.wait(tensor, f"ldwo{s_}", "ldwro")
                pob = po_banks
                for tb in range(2):
                    for dh in range(2):
                        idx = (tb * 2 + dh) % 2
                        if s_ > 0 or tb > 0:
                            kw = (f"oevac{s_}_{tb-1}_{dh}" if tb > 0
                                  else f"oevac{s_-1}_1_{dh}")
                            cx.wait(tensor, kw, "s")
                        for ccN in range(8):
                            tensor.matmul(pob[idx][:], srT[:, ccN, tb * 128:(tb + 1) * 128],
                                          wrob[:, ccN, dh * 512:(dh + 1) * 512],
                                          start=(ccN == 0), stop=False)
                        for hb in range(NHB):
                            inst = tensor.matmul(pob[idx][:], hact[:, hb, tb * 128:(tb + 1) * 128],
                                                 wdb[:, hb, dh * 512:(dh + 1) * 512],
                                                 start=False, stop=(hb == NHB - 1))
                        cx.bump(inst, "t", 1)
                        cx.mark(f"out{s_}_{tb}_{dh}", "t")

        @blk.scalar
        def _(scalar):
            for s_ in range(NCB):
                for db in range(8):
                    cx.wait(scalar, f"rmm{s_}_{db}", "t")
                    if db == 0 and s_ > 0:
                        cx.wait(scalar, f"out{s_-1}_1_1", "t")
                    inst = scalar.activation(srT[:, db, :], pr_banks[db % 2][:, 0:TCB], AF.Tanh)
                    cx.bump(inst, "s", 1)
                    cx.mark(f"revac{s_}_{db}", "s")
                for hb in range(NHB):
                    cx.wait(scalar, f"gu{s_}_{hb}", "t")
                    if hb > 0:
                        cx.wait(scalar, f"hm{s_}_{hb-1}", "v")
                    elif s_ > 0:
                        cx.wait(scalar, f"hm{s_-1}_{NHB-1}", "v")
                    scalar.activation(sgb[:], pg_banks[hb % 2][:, 0:TCB], AF.Silu)
                    inst = scalar.copy(ubb[:], pu_banks[hb % 2][:, 0:TCB])
                    cx.bump(inst, "s", 1)
                    cx.mark(f"guevac{s_}_{hb}", "s")
                for tb in range(2):
                    for dh in range(2):
                        idx = (tb * 2 + dh) % 2
                        cx.wait(scalar, f"out{s_}_{tb}_{dh}", "t")
                        if s_ > 0 or tb > 0:
                            kw = f"oadd{s_}_{tb-1}" if tb > 0 else f"oadd{s_-1}_1"
                            cx.wait(scalar, kw, "v")
                        inst = scalar.copy(outf[:, dh * 512:(dh + 1) * 512], po_banks[idx][:])
                        cx.bump(inst, "s", 1)
                        cx.mark(f"oevac{s_}_{tb}_{dh}", "s")

        @blk.gpsimd
        def _(g):
            for s_ in range(NCB):
                if DEBUG_TAPS and s_ == 0:
                    cx.wait(g, "ldxn0", "ldxn0")
                    cx.tap(g, "xnTo", xnTob[0][:], [128, 8, TCB + 1], BF16)
                    cx.wait(g, "ldffnw", "ldffnw")
                    cx.tap(g, "wgTb", wgTb[:, :, 0:128], [128, 8, 128], BF16)
                    cx.wait(g, "ldwr0", "ldwro")
                    cx.tap(g, "wrob", wrob[:, :, 0:128], [128, 8, 128], BF16)
                    cx.mark("tap_wrob", "std")
                    cx.wait(g, "bmix0", "v")
                    cx.tap(g, "xrT", xrT[:], [128, 8, TCB], BF16)
                    cx.wait(g, "revac0_7", "s")
                    cx.tap(g, "srT", srT[:], [128, 8, TCB], BF16)
                    cx.mark("tapB_sr", "std")
                    cx.wait(g, "smul0", "v")
                    cx.tap(g, "yos", srT[:], [128, 8, TCB], BF16)
                    cx.wait(g, f"hm0_{NHB-1}", "v")
                    cx.tap(g, "hact", hact[:], [128, 24, TCB], BF16)
                    cx.mark("tapsB_done", "std")
                for tb in range(2):
                    cx.wait(g, f"oadd{s_}_{tb}", "v")
                    dst = out_own[s_ * TCB + tb * 128: s_ * TCB + (tb + 1) * 128, :]
                    cx.bump(g.dma_start(out=dst, in_=outt[:]), f"sto{tb}", 16)
                    cx.mark(f"owr{s_}_{tb}", f"sto{tb}")

    sB.close()

    totals = {k: cx.cnt[k] for k in ["sto0", "sto1", "sty", "stx", "stp0", "stp1", "std", "cl"]}
    with nc.Block() as blk:
        @blk.gpsimd
        def _(g):
            for k, v in totals.items():
                if v:
                    g.wait_ge(cx.sem[k], v)

        @blk.sync
        def _(sync):
            for k, v in totals.items():
                if v:
                    sync.wait_ge(cx.sem[k], v)

    return nc, cx.newM


_nc_cache = {}


def _get_program():
    if "nc" not in _nc_cache:
        _, M1 = build({})
        nc, M2 = build(M1)
        assert M1 == M2, "milestone mismatch between emission passes"
        _nc_cache["nc"] = nc
    return _nc_cache["nc"]


def kernel(L0_attn_x, L0_attn_time_decay, L0_attn_time_first,
           L0_attn_time_mix_k, L0_attn_time_mix_v, L0_attn_time_mix_r,
           L0_attn_W_key, L0_attn_W_value, L0_attn_W_receptance, L0_attn_W_output,
           L0_ffn_w_gate, L0_ffn_w_up, L0_ffn_w_down):
    x = np.ascontiguousarray(np.asarray(L0_attn_x, dtype=np.float32))
    wk = np.ascontiguousarray(np.asarray(L0_attn_W_key, np.float32).T)
    wv = np.ascontiguousarray(np.asarray(L0_attn_W_value, np.float32).T)
    wr = np.ascontiguousarray(np.asarray(L0_attn_W_receptance, np.float32).T)
    wo = np.ascontiguousarray(np.asarray(L0_attn_W_output, np.float32).T)
    wg = np.ascontiguousarray(np.asarray(L0_ffn_w_gate, np.int32).T)
    wu = np.ascontiguousarray(np.asarray(L0_ffn_w_up, np.int32).T)
    wd = np.ascontiguousarray(np.asarray(L0_ffn_w_down, np.int32).T)

    def perm(v):
        return np.ascontiguousarray(np.asarray(v, np.float32).reshape(8, 128).T)

    tmk = perm(L0_attn_time_mix_k)
    tmv = perm(L0_attn_time_mix_v)
    tmr = perm(L0_attn_time_mix_r)
    td = perm(L0_attn_time_decay)
    tf = perm(L0_attn_time_first)
    ident = np.eye(128, dtype=ml_dtypes.bfloat16)

    nc = _get_program()
    wk_h = [np.ascontiguousarray(wk[:, h * CH:(h + 1) * CH]) for h in range(2)]
    wv_h = [np.ascontiguousarray(wv[:, h * CH:(h + 1) * CH]) for h in range(2)]
    td_h = [np.ascontiguousarray(td[:, h * NDBL:(h + 1) * NDBL]) for h in range(2)]
    tf_h = [np.ascontiguousarray(tf[:, h * NDBL:(h + 1) * NDBL]) for h in range(2)]
    maps = []
    for core in range(8):
        b, h = core // 2, core % 2
        maps.append({
            "x_all": np.ascontiguousarray(x[b]),
            "x_own": np.ascontiguousarray(x[b, h * OWN:(h + 1) * OWN]),
            "wkT": wk_h[h], "wvT": wv_h[h], "wrT": wr, "woT": wo,
            "wgT": wg, "wuT": wu, "wdT": wd,
            "tmk": tmk, "tmv": tmv, "tmr": tmr, "td": td_h[h], "tf": tf_h[h],
            "ident": ident,
        })
    res = run_bass_kernel_spmd(nc, maps, core_ids=list(range(8)))
    out = np.empty((B, T, C), np.float32)
    for core in range(8):
        b, h = core // 2, core % 2
        out[b, h * OWN:(h + 1) * OWN] = np.asarray(res.results[core]["out_own"])
    return out


# version 12
# speedup vs baseline: 1.2727x; 1.0981x over previous
import numpy as np
import ml_dtypes
import concourse.bass as bass
import concourse.mybir as mybir
from concourse.bass_utils import run_bass_kernel_spmd
from contextlib import ExitStack

F32 = mybir.dt.float32
BF16 = mybir.dt.bfloat16
I32 = mybir.dt.int32
AF = mybir.ActivationFunctionType
ALU = mybir.AluOpType

B, T, C, HID = 4, 8192, 1024, 3072
TCA = 512
NCA = T // TCA
OWN = T // 2
CH = C // 2
NDBL = CH // 128
TCB = 256
NCB = OWN // TCB
NCC = C // 128
NHB = HID // 128
EPS = 1e-5
NEWTON_ITERS = 8
PAIRS = [[0, 1], [2, 3], [4, 5], [6, 7]]
SEMS = ["lds", "ldp0", "ldp1", "ldx0", "ldx1", "ldxn0", "ldxn1", "ldy", "ldxo", "ldwro",
        "ldffnw", "ldos", "stp0", "stp1", "sty", "stx", "sto0", "sto1",
        "std", "cl", "v", "s", "t"]
DEBUG_TAPS = False


class Ctx:
    def __init__(self, nc, M):
        self.nc = nc
        self.M = M
        self.newM = {}
        self.cnt = {k: 0 for k in SEMS}
        self.sem = {}
        self.pass1 = not M

    def mark(self, key, semname):
        self.newM[key] = self.cnt[semname]

    def bump(self, inst, semname, n=1):
        inst.then_inc(self.sem[semname], n)
        self.cnt[semname] += n

    def wait(self, eng, key, semname):
        if self.pass1:
            return
        val = self.M.get(key)
        assert val is not None, f"missing milestone {key}"
        if val > 0:
            eng.wait_ge(self.sem[semname], val)

    def tap(self, g, name, src_ap, shape, dtype):
        t = self.nc.dram_tensor("tap_" + name, shape, dtype, kind="ExternalOutput").ap()
        self.bump(g.dma_start(out=t, in_=src_ap), "std", 16)
        self.mark("tap_" + name, "std")


def build(M):
    nc = bass.Bass("TRN2", target_bir_lowering=False, detect_race_conditions=False)
    nc.cache_partition_id()
    cx = Ctx(nc, M)

    xall = nc.dram_tensor("x_all", [T, C], F32, kind="ExternalInput").ap()
    xown = nc.dram_tensor("x_own", [OWN, C], F32, kind="ExternalInput").ap()
    wkT = nc.dram_tensor("wkT", [C, CH], F32, kind="ExternalInput").ap()
    wvT = nc.dram_tensor("wvT", [C, CH], F32, kind="ExternalInput").ap()
    wrT = nc.dram_tensor("wrT", [C, C], F32, kind="ExternalInput").ap()
    woT = nc.dram_tensor("woT", [C, C], F32, kind="ExternalInput").ap()
    wgT = nc.dram_tensor("wgT", [C, HID], I32, kind="ExternalInput").ap()
    wuT = nc.dram_tensor("wuT", [C, HID], I32, kind="ExternalInput").ap()
    wdT = nc.dram_tensor("wdT", [HID, C], I32, kind="ExternalInput").ap()
    tmk_i = nc.dram_tensor("tmk", [128, 8], F32, kind="ExternalInput").ap()
    tmv_i = nc.dram_tensor("tmv", [128, 8], F32, kind="ExternalInput").ap()
    tmr_i = nc.dram_tensor("tmr", [128, 8], F32, kind="ExternalInput").ap()
    td_i = nc.dram_tensor("td", [128, NDBL], F32, kind="ExternalInput").ap()
    tf_i = nc.dram_tensor("tf", [128, NDBL], F32, kind="ExternalInput").ap()
    ident_i = nc.dram_tensor("ident", [128, 128], BF16, kind="ExternalInput").ap()

    out_own = nc.dram_tensor("out_own", [OWN, C], F32, kind="ExternalOutput").ap()

    xnT_hbm = nc.dram_tensor("xnT_hbm", [C, T + 1], BF16, kind="Internal").ap()
    cin_all = nc.dram_tensor("cin_all", [NCA * CH, TCA], BF16, kind="Internal").ap()
    cout_all = nc.dram_tensor("cout_all", [NCA * C, TCA], BF16, kind="Internal").ap()
    own_xn = nc.dram_tensor("own_xn", [C, OWN + 1], BF16, kind="Internal").ap()
    own_y = nc.dram_tensor("own_y", [C, OWN], BF16, kind="Internal").ap()
    wr_hbm = nc.dram_tensor("wr_hbm", [C, C], BF16, kind="Internal").ap()
    wo_hbm = nc.dram_tensor("wo_hbm", [C, C], BF16, kind="Internal").ap()

    for semname in SEMS:
        cx.sem[semname] = nc.alloc_semaphore("sem_" + semname)

    identb = nc.alloc_sbuf_tensor("identb", [128, 128], BF16)
    tmk = nc.alloc_sbuf_tensor("tmk_t", [128, 8], F32)
    tmv = nc.alloc_sbuf_tensor("tmv_t", [128, 8], F32)
    tmr = nc.alloc_sbuf_tensor("tmr_t", [128, 8], F32)
    tdt = nc.alloc_sbuf_tensor("td_t", [128, NDBL], F32)
    tft = nc.alloc_sbuf_tensor("tf_t", [128, NDBL], F32)
    eu = nc.alloc_sbuf_tensor("eu_t", [128, NDBL], F32)
    ddec = nc.alloc_sbuf_tensor("ddec_t", [128, NDBL], F32)
    e1n = nc.alloc_sbuf_tensor("e1n_t", [128, NDBL], F32)
    c1p5 = nc.alloc_sbuf_tensor("c1p5", [128, 64], F32)


    sA = ExitStack()
    ptr_banks = [sA.enter_context(nc.psum_tensor(f"ptr{i}", [128, 1024], BF16)) for i in range(4)]
    pk_banks = [sA.enter_context(nc.psum_tensor(f"pk{i}", [128, 512], F32)) for i in range(2)]
    pv_banks = [sA.enter_context(nc.psum_tensor(f"pv{i}", [128, 512], F32)) for i in range(2)]
    wkTb = sA.enter_context(nc.sbuf_tensor("wkTb", [128, 8, CH], BF16))
    wvTb = sA.enter_context(nc.sbuf_tensor("wvTb", [128, 8, CH], BF16))
    dfull = sA.enter_context(nc.sbuf_tensor("dfull", [128, NDBL, TCA], F32))
    carryPv = sA.enter_context(nc.sbuf_tensor("carryPv", [128, 8], F32))
    carryQv = sA.enter_context(nc.sbuf_tensor("carryQv", [128, 8], F32))
    wtmpP = sA.enter_context(nc.sbuf_tensor("wtmpP", [128, 256], F32))
    wtmpQ = sA.enter_context(nc.sbuf_tensor("wtmpQ", [128, 256], F32))
    onesrow = sA.enter_context(nc.sbuf_tensor("onesrow", [128, TCA], F32))
    sNs = sA.enter_context(nc.sbuf_tensor("sNs", [128, 64], F32))
    bNs = sA.enter_context(nc.sbuf_tensor("bNs", [128, 64], F32))

    sP = ExitStack()
    wsrcf = [sP.enter_context(nc.sbuf_tensor(f"wsrcf{i}", [128, 4, 1024], F32)) for i in range(2)]
    wcvtb = [sP.enter_context(nc.sbuf_tensor(f"wcvtb{i}", [128, 4, 1024], BF16)) for i in range(2)]


    pieces = []
    for w_src, dst_res, tag in [(wkT, wkTb, "wk"), (wvT, wvTb, "wv")]:
        for ph in range(2):
            src = w_src[ph * 512:(ph + 1) * 512, :].rearrange("(a p) d -> p a d", p=128)
            pieces.append(("f2res", src, (dst_res, ph), 127.0, f"{tag}{ph}"))
    for w_src, w_dst, scale, tag in [(wrT, wr_hbm, 0.5 * 127.0, "wr"), (woT, wo_hbm, 0.5, "wo")]:
        for ph in range(2):
            src = w_src[ph * 512:(ph + 1) * 512, :].rearrange("(a p) d -> p a d", p=128)
            dst = w_dst[ph * 512:(ph + 1) * 512, :].rearrange("(a p) d -> p a d", p=128)
            pieces.append(("f2hbm", src, dst, scale, f"{tag}{ph}"))

    with nc.Block() as blk:
        @blk.sync
        def _(sync):
            for t_in, t_sb in [(tmk_i, tmk), (tmv_i, tmv), (tmr_i, tmr),
                               (td_i, tdt), (tf_i, tft), (ident_i, identb)]:
                cx.bump(sync.dma_start(out=t_sb[:], in_=t_in), "lds", 16)
            cx.mark("ld_small", "lds")
            for i, (kind, src, dst, scale, tag) in enumerate(pieces):
                par = i % 2
                buf = wsrcf[par][:, :, 0:CH] if kind == "f2res" else wsrcf[par][:]
                if i >= 2:
                    cx.wait(sync, f"pcvt{i-2}", "v")
                cx.bump(sync.dma_start(out=buf, in_=src), f"ldp{par}", 16)
                cx.mark(f"pld{i}", f"ldp{par}")

        @blk.scalar
        def _(scalar):
            cx.wait(scalar, "ld_small", "lds")
            cx.bump(scalar.activation(eu[:], tft[:], AF.Exp), "s", 1)
            cx.mark("eu", "s")
            cx.bump(scalar.activation(ddec[:], tdt[:], AF.Exp), "s", 1)
            cx.mark("etd", "s")
            cx.wait(scalar, "e1n", "v")
            cx.bump(scalar.activation(ddec[:], e1n[:], AF.Exp), "s", 1)
            cx.mark("ddec", "s")

        @blk.vector
        def _(vector):
            vector.memset(c1p5[:], 1.5)
            cx.wait(vector, "etd", "s")
            cx.bump(vector.tensor_scalar(e1n[:], ddec[:], -1.0, None, ALU.mult), "v", 1)
            cx.mark("e1n", "v")
            cx.wait(vector, "ddec", "s")
            vector.memset(onesrow[:], 1.0)
            for ccN in range(NDBL):
                inst = vector.tensor_scalar(dfull[:, ccN, :], onesrow[:],
                                            ddec[:, ccN:ccN + 1], None, ALU.mult)
            cx.bump(inst, "v", 1)
            cx.mark("dfull", "v")
            last_cvt_user = {}
            for i, (kind, src, dst, scale, tag) in enumerate(pieces):
                par = i % 2
                cx.wait(vector, f"pld{i}", f"ldp{par}")
                if kind == "f2res":
                    dst_res, ph = dst
                    inst = vector.tensor_scalar(dst_res[:, ph * 4:(ph + 1) * 4, :],
                                                wsrcf[par][:, :, 0:CH],
                                                scale, None, ALU.mult)
                else:
                    cbuf = ("wcvtb", par)
                    prev = last_cvt_user.get(cbuf)
                    if prev is not None:
                        cx.wait(vector, f"pst{prev}", f"stp{prev % 2}")
                    last_cvt_user[cbuf] = i
                    inst = vector.tensor_scalar(wcvtb[par][:], wsrcf[par][:], scale, None, ALU.mult)
                cx.bump(inst, "v", 1)
                cx.mark(f"pcvt{i}", "v")
            cx.mark("prep_v_done", "v")

        @blk.gpsimd
        def _(g):
            for i, (kind, src, dst, scale, tag) in enumerate(pieces):
                if kind == "f2res":
                    continue
                par = i % 2
                cx.wait(g, f"pcvt{i}", "v")
                cx.bump(g.dma_start(out=dst, in_=wcvtb[par][:]), f"stp{par}", 16)
                cx.mark(f"pst{i}", f"stp{par}")
            cx.mark("prep_st0_done", "stp0")
            cx.mark("prep_st1_done", "stp1")

    sP.close()

    xf2 = [sA.enter_context(nc.sbuf_tensor(f"xf{i}", [128, 4, 1024], F32)) for i in range(2)]
    sqscr = sA.enter_context(nc.sbuf_tensor("sqscr", [128, 1024], BF16))
    ssq = sA.enter_context(nc.sbuf_tensor("ssq", [128, 64], F32))
    sm = sA.enter_context(nc.sbuf_tensor("sm", [128, 64], F32))
    meanT = sA.enter_context(nc.sbuf_tensor("meanT", [128, 64], F32))
    varT = sA.enter_context(nc.sbuf_tensor("varT", [128, 64], F32))
    sNb = [sA.enter_context(nc.sbuf_tensor(f"sN{i}", [128, 64], F32)) for i in range(2)]
    bN = sA.enter_context(nc.sbuf_tensor("bN", [128, 64], F32))
    nt1 = sA.enter_context(nc.sbuf_tensor("nt1", [128, 64], F32))
    nt2 = sA.enter_context(nc.sbuf_tensor("nt2", [128, 64], F32))
    vtmp = sA.enter_context(nc.sbuf_tensor("vtmp", [128, 64], F32))
    xn_tok = sA.enter_context(nc.sbuf_tensor("xn_tok", [128, 4, 1024], BF16))
    xnT = sA.enter_context(nc.sbuf_tensor("xnT", [128, 8, TCA + 1], BF16))
    dxx = sA.enter_context(nc.sbuf_tensor("dxx", [128, TCA], BF16))
    xkTb2 = [sA.enter_context(nc.sbuf_tensor(f"xkT{i}", [128, 8, TCA], BF16)) for i in range(2)]
    xvTb2 = [sA.enter_context(nc.sbuf_tensor(f"xvT{i}", [128, 8, TCA], BF16)) for i in range(2)]
    ekf = sA.enter_context(nc.sbuf_tensor("ekf", [128, 4, TCA + 2], F32))
    vvf = sA.enter_context(nc.sbuf_tensor("vvf", [128, 4, TCA], F32))
    ekvf = sA.enter_context(nc.sbuf_tensor("ekvf", [128, 4, TCA + 2], F32))
    Pt = sA.enter_context(nc.sbuf_tensor("Pt", [128, 4, TCA], F32))
    Qt = sA.enter_context(nc.sbuf_tensor("Qt", [128, 4, TCA], F32))
    numt = sA.enter_context(nc.sbuf_tensor("numt", [128, 4, TCA], F32))
    dent = sA.enter_context(nc.sbuf_tensor("dent", [128, 4, TCA], F32))
    yb = sA.enter_context(nc.sbuf_tensor("yb", [128, NDBL, TCA], BF16))

    with nc.Block() as blk:
        @blk.sync
        def _(sync):
            for j in range(NCA):
                if j == 0:
                    cx.wait(sync, "prep_v_done", "v")
                    cx.wait(sync, "prep_st0_done", "stp0")
                    cx.wait(sync, "prep_st1_done", "stp1")
                elif j >= 2:
                    cx.wait(sync, f"lnap{j-2}", "v")
                    cx.wait(sync, f"sq{j-2}", "s")
                    if DEBUG_TAPS and j == 2:
                        cx.wait(sync, "tapsA_done", "std")
                src = xall[j * TCA:(j + 1) * TCA, :].rearrange("(a p) c -> p a c", p=128)
                cx.bump(sync.dma_start(out=xf2[j % 2][:], in_=src), f"ldx{j % 2}", 16)
                cx.mark(f"ldx{j}", f"ldx{j % 2}")

        @blk.scalar
        def _(scalar):
            def emit_sq_relay(j):
                cx.wait(scalar, f"ldx{j}", f"ldx{j % 2}")
                cx.wait(scalar, f"stats{j-1}" if j > 0 else "statsinit", "v")
                for a in range(4):
                    scalar.activation(sqscr[:], xf2[j % 2][:, a, :], AF.Square,
                                      accum_out=ssq[:, a:a + 1])
                    inst = scalar.activation(sqscr[:], xf2[j % 2][:, a, :], AF.Copy,
                                             accum_out=sm[:, a:a + 1])
                cx.bump(inst, "s", 1)
                cx.mark(f"sq{j}", "s")

            def emit_relay(j):
                cx.wait(scalar, f"stats{j}", "v")
                scalar.copy(sNs[:], sNb[(NEWTON_ITERS + 1) % 2][:])
                inst = scalar.copy(bNs[:], bN[:])
                cx.bump(inst, "s", 1)
                cx.mark(f"lnrelay{j}", "s")

            def emit_trevac(J):
                for ccN in range(8):
                    cx.wait(scalar, f"tr{J}_{ccN}", "t")
                    cx.wait(scalar, f"ghost{J}", "v")
                    if J > 0 and ccN == 0:
                        cx.wait(scalar, f"xnwr{J-1}", "stx")
                    inst = scalar.copy(xnT[:, ccN, 1:TCA + 1], ptr_banks[ccN % 4][:, 0:TCA])
                    cx.bump(inst, "s", 1)
                    cx.mark(f"trevac{J}_{ccN}", "s")

            emit_sq_relay(0)
            emit_relay(0)
            emit_sq_relay(1)
            emit_trevac(0)
            for j in range(NCA):
                for db in range(NDBL):
                    cx.wait(scalar, f"kv{j}_{db}", "t")
                    if db == 0 and j > 0:
                        cx.wait(scalar, f"scan{j-1}_0", "v")
                    scalar.activation(ekf[:, db, 1:257], pk_banks[db % 2][:, 0:256], AF.Exp)
                    cx.bump(scalar.activation(ekf[:, db, 258:TCA + 2], pk_banks[db % 2][:, 256:TCA], AF.Exp), "s", 1)
                    cx.bump(scalar.copy(vvf[:, db, :], pv_banks[db % 2][:]), "s", 1)
                    cx.mark(f"ekevac{j}_{db}", "s")
                    if db == 1 and j + 1 < NCA:
                        emit_relay(j + 1)
                if j + 1 < NCA:
                    emit_trevac(j + 1)
                cx.wait(scalar, f"den{j}_0", "v")
                for db in range(NDBL):
                    inst = scalar.activation(vvf[:, db, :], dent[:, db, :], AF.Ln)
                cx.bump(inst, "s", 1)
                cx.mark(f"lnd{j}_0", "s")
                cx.wait(scalar, f"negd{j}_0", "v")
                for db in range(NDBL):
                    inst = scalar.activation(vvf[:, db, :], dent[:, db, :], AF.Exp)
                cx.bump(inst, "s", 1)
                cx.mark(f"expd{j}_0", "s")
                if j + 2 < NCA:
                    emit_sq_relay(j + 2)

        @blk.vector
        def _(vector):
            def emit_stats_lnap(j):
                cx.wait(vector, f"sq{j}", "s")
                vector.tensor_scalar(meanT[:], sm[:], 1.0 / C, None, ALU.mult)
                vector.tensor_scalar(vtmp[:], ssq[:], 1.0 / C, None, ALU.mult)
                vector.tensor_mul(nt1[:], meanT[:], meanT[:])
                vector.tensor_sub(nt2[:], vtmp[:], nt1[:])
                vector.tensor_scalar(varT[:], nt2[:], EPS, None, ALU.add)
                vector.memset(sNb[0][:], 1.0)
                cur = 0
                for _ in range(NEWTON_ITERS):
                    vector.tensor_mul(nt1[:], sNb[cur][:], sNb[cur][:])
                    vector.tensor_mul(nt2[:], nt1[:], varT[:])
                    vector.scalar_tensor_tensor(nt1[:], nt2[:], -0.5, c1p5[:], ALU.mult, ALU.add)
                    vector.tensor_mul(sNb[1 - cur][:], sNb[cur][:], nt1[:])
                    cur = 1 - cur
                vector.tensor_scalar(sNb[1 - cur][:], sNb[cur][:], 1.0 / 127.0, None, ALU.mult)
                sN = sNb[1 - cur]
                inst = vector.tensor_mul(bN[:], meanT[:], sN[:])
                cx.bump(inst, "v", 1)
                cx.mark(f"stats{j}", "v")
                cx.wait(vector, f"lnrelay{j}", "s")
                if j > 0:
                    cx.wait(vector, f"tr{j-1}_7", "t")
                for a in range(4):
                    inst = vector.tensor_scalar(xn_tok[:, a, :], xf2[j % 2][:, a, :],
                                                sNs[:, a:a + 1], bNs[:, a:a + 1],
                                                ALU.mult, ALU.subtract)
                cx.bump(inst, "v", 1)
                cx.mark(f"lnap{j}", "v")

            for j in range(NCA):
                if j == 0:
                    cx.wait(vector, "prep_st0_done", "stp0")
                    cx.wait(vector, "prep_st1_done", "stp1")
                    vector.memset(sm[:], 1.0)
                    vector.memset(carryPv[:], 0.0)
                    vector.memset(carryQv[:], 0.0)
                    inst = vector.memset(ssq[:], 1.0)
                    cx.bump(inst, "v", 1)
                    cx.mark("statsinit", "v")
                    emit_stats_lnap(0)
                    for ccN in range(8):
                        inst = vector.memset(xnT[:, ccN, 0:1], 0.0)
                    cx.bump(inst, "v", 1)
                    cx.mark("ghost0", "v")
                cx.wait(vector, f"trevac{j}_7", "s")
                if j > 1:
                    cx.wait(vector, f"kv{j-2}_{NDBL-1}", "t")
                xkT, xvT = xkTb2[j % 2], xvTb2[j % 2]
                for ccN in range(8):
                    vector.tensor_sub(dxx[:], xnT[:, ccN, 1:TCA + 1], xnT[:, ccN, 0:TCA])
                    vector.scalar_tensor_tensor(xkT[:, ccN, :], dxx[:], tmk[:, ccN:ccN + 1],
                                                xnT[:, ccN, 0:TCA], ALU.mult, ALU.add)
                    inst = vector.scalar_tensor_tensor(xvT[:, ccN, :], dxx[:], tmv[:, ccN:ccN + 1],
                                                       xnT[:, ccN, 0:TCA], ALU.mult, ALU.add)
                cx.bump(inst, "v", 1)
                cx.mark(f"mix{j}", "v")
                if j + 1 < NCA:
                    emit_stats_lnap(j + 1)
                    for ccN in range(8):
                        inst = vector.tensor_copy(xnT[:, ccN, 0:1], xnT[:, ccN, TCA:TCA + 1])
                    cx.bump(inst, "v", 1)
                    cx.mark(f"ghost{j+1}", "v")
                for db in range(NDBL):
                    cx.wait(vector, f"ekevac{j}_{db}", "s")
                if j > 0:
                    cx.wait(vector, f"ywr{j-1}", "sty")
                vector.tensor_copy(ekvf[:, 0:4, 0:1], carryPv[:, 0:4])
                vector.tensor_copy(ekf[:, 0:4, 0:1], carryQv[:, 0:4])
                for db in range(NDBL):
                    vector.tensor_mul(ekvf[:, db, 1:257], ekf[:, db, 1:257], vvf[:, db, 0:256])
                    vector.tensor_mul(ekvf[:, db, 258:TCA + 2], ekf[:, db, 258:TCA + 2], vvf[:, db, 256:TCA])
                for db in range(NDBL):
                    vector.tensor_tensor_scan(Pt[:, db, 0:256], dfull[:, db, 0:256],
                                              ekvf[:, db, 0:256], 0.0, ALU.mult, ALU.add)
                    vector.tensor_tensor_scan(Qt[:, db, 0:256], dfull[:, db, 0:256],
                                              ekf[:, db, 0:256], 0.0, ALU.mult, ALU.add)
                for db in range(NDBL):
                    vector.tensor_mul(wtmpP[:, db * 64:(db + 1) * 64], Pt[:, db, 192:256], dfull[:, db, 0:64])
                    vector.tensor_mul(wtmpQ[:, db * 64:(db + 1) * 64], Qt[:, db, 192:256], dfull[:, db, 0:64])
                for db in range(NDBL):
                    vector.tensor_add(ekvf[:, db, 257:258], wtmpP[:, db * 64 + 63:db * 64 + 64], ekvf[:, db, 256:257])
                    vector.tensor_add(ekf[:, db, 257:258], wtmpQ[:, db * 64 + 63:db * 64 + 64], ekf[:, db, 256:257])
                for db in range(NDBL):
                    vector.tensor_tensor_scan(Pt[:, db, 256:TCA], dfull[:, db, 0:256],
                                              ekvf[:, db, 257:TCA + 1], 0.0, ALU.mult, ALU.add)
                    vector.tensor_tensor_scan(Qt[:, db, 256:TCA], dfull[:, db, 0:256],
                                              ekf[:, db, 257:TCA + 1], 0.0, ALU.mult, ALU.add)
                for db in range(NDBL):
                    vector.scalar_tensor_tensor(numt[:, db, 0:256], ekvf[:, db, 1:257],
                                                eu[:, db:db + 1], Pt[:, db, 0:256],
                                                ALU.mult, ALU.add)
                    vector.scalar_tensor_tensor(numt[:, db, 256:TCA], ekvf[:, db, 258:TCA + 2],
                                                eu[:, db:db + 1], Pt[:, db, 256:TCA],
                                                ALU.mult, ALU.add)
                    vector.scalar_tensor_tensor(dent[:, db, 0:256], ekf[:, db, 1:257],
                                                eu[:, db:db + 1], Qt[:, db, 0:256],
                                                ALU.mult, ALU.add)
                    cx.last_den_inst = vector.scalar_tensor_tensor(
                        dent[:, db, 256:TCA], ekf[:, db, 258:TCA + 2],
                        eu[:, db:db + 1], Qt[:, db, 256:TCA],
                        ALU.mult, ALU.add)
                cx.bump(cx.last_den_inst, "v", 1)
                cx.mark(f"den{j}_0", "v")
                for db in range(NDBL):
                    vector.tensor_mul(wtmpP[:, db * 64:(db + 1) * 64], Pt[:, db, 448:TCA], dfull[:, db, 0:64])
                    vector.tensor_mul(wtmpQ[:, db * 64:(db + 1) * 64], Qt[:, db, 448:TCA], dfull[:, db, 0:64])
                for db in range(NDBL):
                    vector.tensor_add(carryPv[:, db:db + 1], wtmpP[:, db * 64 + 63:db * 64 + 64], ekvf[:, db, TCA + 1:TCA + 2])
                    vector.tensor_add(carryQv[:, db:db + 1], wtmpQ[:, db * 64 + 63:db * 64 + 64], ekf[:, db, TCA + 1:TCA + 2])
                cx.wait(vector, f"lnd{j}_0", "s")
                for db in range(NDBL):
                    inst = vector.tensor_scalar(dent[:, db, :], vvf[:, db, :], -1.0, None, ALU.mult)
                cx.bump(inst, "v", 1)
                cx.mark(f"negd{j}_0", "v")
                cx.wait(vector, f"expd{j}_0", "s")
                for db in range(NDBL):
                    inst = vector.tensor_mul(yb[:, db, :], numt[:, db, :], vvf[:, db, :])
                cx.bump(inst, "v", 1)
                cx.mark(f"scan{j}_0", "v")

        @blk.tensor
        def _(tensor):
            def emit_tr(j, ccs):
                for ccN in ccs:
                    if j > 0 or ccN >= 4:
                        jw, cw = (j, ccN - 4) if ccN >= 4 else (j - 1, ccN + 4)
                        cx.wait(tensor, f"trevac{jw}_{cw}", "s")
                    for a in range(4):
                        inst = tensor.transpose(ptr_banks[ccN % 4][:, a * 128:(a + 1) * 128],
                                                xn_tok[:, a, ccN * 128:(ccN + 1) * 128], identb[:])
                    cx.bump(inst, "t", 1)
                    cx.mark(f"tr{j}_{ccN}", "t")

            def emit_kv(j, dbs):
                xkT, xvT = xkTb2[j % 2], xvTb2[j % 2]
                for db in dbs:
                    if j > 0 or db >= 2:
                        jw, dw = (j, db - 2) if db >= 2 else (j - 1, db + 2)
                        cx.wait(tensor, f"ekevac{jw}_{dw}", "s")
                    for ccN in range(8):
                        tensor.matmul(pk_banks[db % 2][:], wkTb[:, ccN, db * 128:(db + 1) * 128],
                                      xkT[:, ccN, :], start=(ccN == 0), stop=(ccN == 7))
                    for ccN in range(8):
                        inst = tensor.matmul(pv_banks[db % 2][:], wvTb[:, ccN, db * 128:(db + 1) * 128],
                                             xvT[:, ccN, :], start=(ccN == 0), stop=(ccN == 7))
                    cx.bump(inst, "t", 1)
                    cx.mark(f"kv{j}_{db}", "t")

            cx.wait(tensor, "lnap0", "v")
            emit_tr(0, range(8))
            for j in range(NCA):
                cx.wait(tensor, f"mix{j}", "v")
                emit_kv(j, range(0, 2))
                if j + 1 < NCA:
                    cx.wait(tensor, f"lnap{j+1}", "v")
                    emit_tr(j + 1, range(0, 4))
                emit_kv(j, range(2, 4))
                if j + 1 < NCA:
                    emit_tr(j + 1, range(4, 8))

        @blk.gpsimd
        def _(g):
            for j in range(NCA):
                if DEBUG_TAPS and j == 0:
                    cx.wait(g, "scan0_0", "v")
                    cx.tap(g, "xf", xf2[0][:], [128, 4, 1024], F32)
                    cx.tap(g, "sm", sm[:], [128, 64], F32)
                    cx.tap(g, "ssq", ssq[:], [128, 64], F32)
                    cx.tap(g, "varT", varT[:], [128, 64], F32)
                    cx.tap(g, "meanT", meanT[:], [128, 64], F32)
                    cx.tap(g, "vtmp", vtmp[:], [128, 64], F32)
                    cx.tap(g, "nt1", nt1[:], [128, 64], F32)
                    cx.tap(g, "nt2", nt2[:], [128, 64], F32)
                    cx.tap(g, "xn_tok", xn_tok[:], [128, 4, 1024], BF16)
                    cx.tap(g, "xkT", xkTb2[0][:], [128, 8, TCA], BF16)
                    cx.tap(g, "xvT", xvTb2[0][:], [128, 8, TCA], BF16)
                    cx.tap(g, "sN", sNb[(NEWTON_ITERS + 1) % 2][:], [128, 64], F32)
                    cx.tap(g, "bN", bN[:], [128, 64], F32)
                    cx.tap(g, "ekf", ekf[:], [128, 4, TCA + 2], F32)
                    cx.tap(g, "ekvf", ekvf[:], [128, 4, TCA + 2], F32)
                    cx.tap(g, "Pt", Pt[:], [128, 4, TCA], F32)
                    cx.tap(g, "Qt", Qt[:], [128, 4, TCA], F32)
                    cx.tap(g, "numt", numt[:], [128, 4, TCA], F32)
                    cx.tap(g, "dent", dent[:], [128, 4, TCA], F32)
                    cx.mark("tapsA_half0", "std")
                cx.wait(g, f"trevac{j}_7", "s")
                if j == 0:
                    cx.wait(g, "ghost0", "v")
                    dst2 = xnT_hbm[:, 0:TCA + 1].rearrange("(cc p) t -> p cc t", p=128)
                    cx.bump(g.dma_start(out=dst2, in_=xnT[:]), "stx", 16)
                else:
                    dst2 = xnT_hbm[:, 1 + j * TCA:1 + (j + 1) * TCA].rearrange("(cc p) t -> p cc t", p=128)
                    cx.bump(g.dma_start(out=dst2, in_=xnT[:, :, 1:TCA + 1]), "stx", 16)
                cx.mark(f"xnwr{j}", "stx")
                cx.wait(g, f"scan{j}_0", "v")
                if DEBUG_TAPS and j == 0:
                    cx.tap(g, "yb", yb[:], [128, NDBL, TCA], BF16)
                dst = cin_all[j * CH:(j + 1) * CH, :].rearrange("(cc p) t -> p cc t", p=128)
                cx.bump(g.dma_start(out=dst, in_=yb[:]), "sty", 16)
                cx.mark(f"ywr{j}", "sty")
                cx.wait(g, f"ywr{j}", "sty")
                cx.bump(g.collective_compute(
                    "AllGather", ALU.bypass,
                    ins=[cin_all[j * CH:(j + 1) * CH, :]],
                    outs=[cout_all[j * C:(j + 1) * C, :]],
                    replica_groups=PAIRS), "cl", 1)
                cx.mark(f"cl{j}", "cl")
                if DEBUG_TAPS and j == 0:
                    cx.tap(g, "xnT", xnT[:], [128, 8, TCA + 1], BF16)
                    cx.mark("tapsA_done", "std")

    cx.newM["A_v_done"] = cx.cnt["v"]
    cx.newM["A_s_done"] = cx.cnt["s"]
    cx.newM["A_t_done"] = cx.cnt["t"]
    cx.newM["A_sty_done"] = cx.cnt["sty"]
    cx.newM["A_stx_done"] = cx.cnt["stx"]
    cx.newM["A_cl_done"] = cx.cnt["cl"]
    sA.close()

    sB = ExitStack()
    pr_banks = [sB.enter_context(nc.psum_tensor(f"pr{i}", [128, 512], F32)) for i in range(2)]
    pg_banks = [sB.enter_context(nc.psum_tensor(f"pg{i}", [128, 512], F32)) for i in range(2)]
    pu_banks = [sB.enter_context(nc.psum_tensor(f"pu{i}", [128, 512], F32)) for i in range(2)]
    po_banks = [sB.enter_context(nc.psum_tensor(f"po{i}", [128, 512], F32)) for i in range(2)]
    wgTb = sB.enter_context(nc.sbuf_tensor("wgTb", [128, 8, HID], BF16))
    wuTb = sB.enter_context(nc.sbuf_tensor("wuTb", [128, 8, HID], BF16))
    wdb = sB.enter_context(nc.sbuf_tensor("wdb", [128, 24, 1024], BF16))
    wrob = sB.enter_context(nc.sbuf_tensor("wrob", [128, 8, 1024], BF16))
    xnTob = [sB.enter_context(nc.sbuf_tensor(f"xnTo{i}", [128, 8, TCB + 1], BF16)) for i in range(2)]
    dxxB = sB.enter_context(nc.sbuf_tensor("dxxB", [128, TCB], BF16))
    xrT = sB.enter_context(nc.sbuf_tensor("xrT", [128, 8, TCB], BF16))
    srT = sB.enter_context(nc.sbuf_tensor("srT", [128, 8, TCB], BF16))
    yo = sB.enter_context(nc.sbuf_tensor("yo", [128, 8, TCB], BF16))
    sgb = sB.enter_context(nc.sbuf_tensor("sgb", [128, TCB], BF16))
    ubb = sB.enter_context(nc.sbuf_tensor("ubb", [128, TCB], BF16))
    hact = sB.enter_context(nc.sbuf_tensor("hact", [128, 24, TCB], BF16))
    xo = sB.enter_context(nc.sbuf_tensor("xo", [128, 1024], F32))
    outf = sB.enter_context(nc.sbuf_tensor("outf", [128, 1024], F32))
    outt = sB.enter_context(nc.sbuf_tensor("outt", [128, 1024], F32))

    with nc.Block() as blk:
        @blk.gpsimd
        def _(g):
            cx.wait(g, "A_v_done", "v")
            cx.wait(g, "A_s_done", "s")
            cx.wait(g, "A_t_done", "t")
            cx.bump(g.dma_start(out=wgTb[:], in_=wgT.rearrange("(cc p) h -> p cc h", p=128)), "ldffnw", 16)
            cx.bump(g.dma_start(out=wuTb[:], in_=wuT.rearrange("(cc p) h -> p cc h", p=128)), "ldffnw", 16)
            cx.bump(g.dma_start(out=wdb[:], in_=wdT.rearrange("(hb p) d -> p hb d", p=128)), "ldffnw", 16)
            cx.mark("ldffnw", "ldffnw")

        @blk.sync
        def _(sync):
            cx.wait(sync, "A_v_done", "v")
            cx.wait(sync, "A_s_done", "s")
            cx.wait(sync, "A_t_done", "t")
            cx.wait(sync, "A_sty_done", "sty")
            cx.wait(sync, "A_stx_done", "stx")

            HW_ = OWN // 2
            QC = HW_ // TCA
            cv = cout_all.rearrange("(c p) t -> p c t", p=C)

            def yreloc(q, cond, h):
                c0_ = h * (T // TCA // 2) + q * QC
                dst = own_y[:, q * HW_:(q + 1) * HW_].rearrange("p (c t) -> p c t", t=TCA)
                return sync.dma_start(out=dst, in_=cv[:, c0_:c0_ + QC, :], cond=cond)

            cx.wait(sync, "A_cl_done", "cl")
            if cx.pass1:
                cx.cnt["ldos"] += 128
            else:
                pid = sync.partition_id()
                hval = sync.compute_val(pid % 2)
                c0 = sync.compute_val(hval == 0)
                c1 = sync.compute_val(hval == 1)
                cx.bump(sync.dma_start(out=own_xn[:, 0:HW_ + 1], in_=xnT_hbm[:, 0:HW_ + 1], cond=c0), "ldos", 16)
                cx.bump(sync.dma_start(out=own_xn[:, 0:HW_ + 1], in_=xnT_hbm[:, OWN:OWN + HW_ + 1], cond=c1), "ldos", 16)
                cx.bump(yreloc(0, c0, 0), "ldos", 16)
                cx.bump(yreloc(0, c1, 1), "ldos", 16)
                cx.bump(sync.dma_start(out=own_xn[:, HW_ + 1:OWN + 1], in_=xnT_hbm[:, HW_ + 1:OWN + 1], cond=c0), "ldos", 16)
                cx.bump(sync.dma_start(out=own_xn[:, HW_ + 1:OWN + 1], in_=xnT_hbm[:, OWN + HW_ + 1:T + 1], cond=c1), "ldos", 16)
                cx.bump(yreloc(1, c0, 0), "ldos", 16)
                cx.bump(yreloc(1, c1, 1), "ldos", 16)
            cx.mark("ownstage", "ldos")

    with nc.Block() as blk:
        @blk.sync
        def _(sync):
            for s_ in range(NCB):
                if s_ == 0:
                    if not cx.pass1:
                        sync.wait_ge(cx.sem["ldos"], 64)
                elif s_ == NCB // 2:
                    cx.wait(sync, "ownstage", "ldos")
                elif s_ >= 2:
                    cx.wait(sync, f"bmix{s_-2}", "v")
                    cx.wait(sync, f"gu{s_-2}_{NHB-1}", "t")
                if DEBUG_TAPS and s_ == 1:
                    cx.wait(sync, "tapsB_done", "std")
                t0 = s_ * TCB
                src = own_xn[:, t0:t0 + TCB + 1].rearrange("(cc p) t -> p cc t", p=128)
                cx.bump(sync.dma_start(out=xnTob[s_ % 2][:], in_=src), f"ldxn{s_ % 2}", 16)
                cx.mark(f"ldxn{s_}", f"ldxn{s_ % 2}")
                if s_ > 0:
                    cx.wait(sync, f"smul{s_-1}", "v")
                srcy = own_y[:, t0:t0 + TCB].rearrange("(cc p) t -> p cc t", p=128)
                cx.bump(sync.dma_start(out=yo[:], in_=srcy), "ldy", 16)
                cx.mark(f"ldy{s_}", "ldy")
                if s_ > 0:
                    cx.wait(sync, f"oadd{s_-1}_1", "v")
                cx.bump(sync.dma_start(out=xo[:], in_=xown[t0:t0 + 128, :]), "ldxo", 16)
                cx.mark(f"ldxo{s_}_0", "ldxo")
                if s_ > 0:
                    cx.wait(sync, f"out{s_-1}_1_1", "t")
                cx.bump(sync.dma_start(out=wrob[:], in_=wr_hbm.rearrange("(cc p) d -> p cc d", p=128)), "ldwro", 16)
                cx.mark(f"ldwr{s_}", "ldwro")
                cx.wait(sync, f"rmm{s_}_7", "t")
                if DEBUG_TAPS and s_ == 0:
                    cx.wait(sync, "tap_wrob", "std")
                cx.bump(sync.dma_start(out=wrob[:], in_=wo_hbm.rearrange("(cc p) d -> p cc d", p=128)), "ldwro", 16)
                cx.mark(f"ldwo{s_}", "ldwro")
                cx.wait(sync, f"oadd{s_}_0", "v")
                cx.bump(sync.dma_start(out=xo[:], in_=xown[t0 + 128:t0 + 256, :]), "ldxo", 16)
                cx.mark(f"ldxo{s_}_1", "ldxo")

        @blk.vector
        def _(vector):
            for s_ in range(NCB):
                cx.wait(vector, f"ldxn{s_}", f"ldxn{s_ % 2}")
                if s_ > 0:
                    cx.wait(vector, f"rmm{s_-1}_7", "t")
                xnTo = xnTob[s_ % 2]
                for ccN in range(8):
                    vector.tensor_sub(dxxB[:], xnTo[:, ccN, 1:TCB + 1], xnTo[:, ccN, 0:TCB])
                    inst = vector.scalar_tensor_tensor(xrT[:, ccN, :], dxxB[:], tmr[:, ccN:ccN + 1],
                                                       xnTo[:, ccN, 0:TCB], ALU.mult, ALU.add)
                cx.bump(inst, "v", 1)
                cx.mark(f"bmix{s_}", "v")
                cx.wait(vector, f"revac{s_}_7", "s")
                cx.wait(vector, f"ldy{s_}", "ldy")
                if DEBUG_TAPS and s_ == 0:
                    cx.wait(vector, "tapB_sr", "std")
                vector.tensor_scalar(xrT[:], srT[:], 1.0, None, ALU.add)
                inst = vector.tensor_mul(srT[:], yo[:], xrT[:])
                cx.bump(inst, "v", 1)
                cx.mark(f"smul{s_}", "v")
                for hb in range(NHB):
                    cx.wait(vector, f"guevac{s_}_{hb}", "s")
                    if hb == 0 and s_ > 0:
                        cx.wait(vector, f"out{s_-1}_1_1", "t")
                    inst = vector.scalar_tensor_tensor(hact[:, hb, :], sgb[:], 1.0 / 127.0,
                                                       ubb[:], ALU.mult, ALU.mult)
                    cx.bump(inst, "v", 1)
                    cx.mark(f"hm{s_}_{hb}", "v")
                for tb in range(2):
                    cx.wait(vector, f"oevac{s_}_{tb}_1", "s")
                    cx.wait(vector, f"ldxo{s_}_{tb}", "ldxo")
                    prev = (s_, tb - 1) if tb > 0 else (s_ - 1, 1)
                    if prev[0] >= 0:
                        cx.wait(vector, f"owr{prev[0]}_{prev[1]}", f"sto{prev[1]}")
                    inst = vector.tensor_add(outt[:], outf[:], xo[:])
                    cx.bump(inst, "v", 1)
                    cx.mark(f"oadd{s_}_{tb}", "v")

        @blk.tensor
        def _(tensor):
            for s_ in range(NCB):
                cx.wait(tensor, f"bmix{s_}", "v")
                cx.wait(tensor, f"ldwr{s_}", "ldwro")
                for db in range(8):
                    kprev = (f"revac{s_}_{db-2}" if db >= 2 else
                             (f"revac{s_-1}_{6+db}" if s_ > 0 else None))
                    if kprev is not None:
                        cx.wait(tensor, kprev, "s")
                    for ccN in range(8):
                        inst = tensor.matmul(pr_banks[db % 2][:, 0:TCB], wrob[:, ccN, db * 128:(db + 1) * 128],
                                             xrT[:, ccN, :], start=(ccN == 0), stop=(ccN == 7))
                    cx.bump(inst, "t", 1)
                    cx.mark(f"rmm{s_}_{db}", "t")
                xnTo = xnTob[s_ % 2]
                cx.wait(tensor, "ldffnw", "ldffnw")
                for hb in range(NHB):
                    kprev = (f"guevac{s_}_{hb-2}" if hb >= 2 else
                             (f"guevac{s_-1}_{NHB-2+hb}" if s_ > 0 else None))
                    if kprev is not None:
                        cx.wait(tensor, kprev, "s")
                    for ccN in range(8):
                        tensor.matmul(pg_banks[hb % 2][:, 0:TCB], wgTb[:, ccN, hb * 128:(hb + 1) * 128],
                                      xnTo[:, ccN, 1:TCB + 1], start=(ccN == 0), stop=(ccN == 7))
                    for ccN in range(8):
                        inst = tensor.matmul(pu_banks[hb % 2][:, 0:TCB], wuTb[:, ccN, hb * 128:(hb + 1) * 128],
                                             xnTo[:, ccN, 1:TCB + 1], start=(ccN == 0), stop=(ccN == 7))
                    cx.bump(inst, "t", 1)
                    cx.mark(f"gu{s_}_{hb}", "t")
                cx.wait(tensor, f"smul{s_}", "v")
                cx.wait(tensor, f"hm{s_}_{NHB-1}", "v")
                cx.wait(tensor, f"ldwo{s_}", "ldwro")
                pob = po_banks
                for tb in range(2):
                    for dh in range(2):
                        idx = (tb * 2 + dh) % 2
                        if s_ > 0 or tb > 0:
                            kw = (f"oevac{s_}_{tb-1}_{dh}" if tb > 0
                                  else f"oevac{s_-1}_1_{dh}")
                            cx.wait(tensor, kw, "s")
                        for ccN in range(8):
                            tensor.matmul(pob[idx][:], srT[:, ccN, tb * 128:(tb + 1) * 128],
                                          wrob[:, ccN, dh * 512:(dh + 1) * 512],
                                          start=(ccN == 0), stop=False)
                        for hb in range(NHB):
                            inst = tensor.matmul(pob[idx][:], hact[:, hb, tb * 128:(tb + 1) * 128],
                                                 wdb[:, hb, dh * 512:(dh + 1) * 512],
                                                 start=False, stop=(hb == NHB - 1))
                        cx.bump(inst, "t", 1)
                        cx.mark(f"out{s_}_{tb}_{dh}", "t")

        @blk.scalar
        def _(scalar):
            for s_ in range(NCB):
                for db in range(8):
                    cx.wait(scalar, f"rmm{s_}_{db}", "t")
                    if db == 0 and s_ > 0:
                        cx.wait(scalar, f"out{s_-1}_1_1", "t")
                    inst = scalar.activation(srT[:, db, :], pr_banks[db % 2][:, 0:TCB], AF.Tanh)
                    cx.bump(inst, "s", 1)
                    cx.mark(f"revac{s_}_{db}", "s")
                for hb in range(NHB):
                    cx.wait(scalar, f"gu{s_}_{hb}", "t")
                    if hb > 0:
                        cx.wait(scalar, f"hm{s_}_{hb-1}", "v")
                    elif s_ > 0:
                        cx.wait(scalar, f"hm{s_-1}_{NHB-1}", "v")
                    scalar.activation(sgb[:], pg_banks[hb % 2][:, 0:TCB], AF.Silu)
                    inst = scalar.copy(ubb[:], pu_banks[hb % 2][:, 0:TCB])
                    cx.bump(inst, "s", 1)
                    cx.mark(f"guevac{s_}_{hb}", "s")
                for tb in range(2):
                    for dh in range(2):
                        idx = (tb * 2 + dh) % 2
                        cx.wait(scalar, f"out{s_}_{tb}_{dh}", "t")
                        if s_ > 0 or tb > 0:
                            kw = f"oadd{s_}_{tb-1}" if tb > 0 else f"oadd{s_-1}_1"
                            cx.wait(scalar, kw, "v")
                        inst = scalar.copy(outf[:, dh * 512:(dh + 1) * 512], po_banks[idx][:])
                        cx.bump(inst, "s", 1)
                        cx.mark(f"oevac{s_}_{tb}_{dh}", "s")

        @blk.gpsimd
        def _(g):
            for s_ in range(NCB):
                if DEBUG_TAPS and s_ == 0:
                    cx.wait(g, "ldxn0", "ldxn0")
                    cx.tap(g, "xnTo", xnTob[0][:], [128, 8, TCB + 1], BF16)
                    cx.wait(g, "ldffnw", "ldffnw")
                    cx.tap(g, "wgTb", wgTb[:, :, 0:128], [128, 8, 128], BF16)
                    cx.wait(g, "ldwr0", "ldwro")
                    cx.tap(g, "wrob", wrob[:, :, 0:128], [128, 8, 128], BF16)
                    cx.mark("tap_wrob", "std")
                    cx.wait(g, "bmix0", "v")
                    cx.tap(g, "xrT", xrT[:], [128, 8, TCB], BF16)
                    cx.wait(g, "revac0_7", "s")
                    cx.tap(g, "srT", srT[:], [128, 8, TCB], BF16)
                    cx.mark("tapB_sr", "std")
                    cx.wait(g, "smul0", "v")
                    cx.tap(g, "yos", srT[:], [128, 8, TCB], BF16)
                    cx.wait(g, f"hm0_{NHB-1}", "v")
                    cx.tap(g, "hact", hact[:], [128, 24, TCB], BF16)
                    cx.mark("tapsB_done", "std")
                for tb in range(2):
                    cx.wait(g, f"oadd{s_}_{tb}", "v")
                    dst = out_own[s_ * TCB + tb * 128: s_ * TCB + (tb + 1) * 128, :]
                    cx.bump(g.dma_start(out=dst, in_=outt[:]), f"sto{tb}", 16)
                    cx.mark(f"owr{s_}_{tb}", f"sto{tb}")

    sB.close()

    totals = {k: cx.cnt[k] for k in ["sto0", "sto1", "sty", "stx", "stp0", "stp1", "std", "cl"]}
    with nc.Block() as blk:
        @blk.gpsimd
        def _(g):
            for k, v in totals.items():
                if v:
                    g.wait_ge(cx.sem[k], v)

        @blk.sync
        def _(sync):
            for k, v in totals.items():
                if v:
                    sync.wait_ge(cx.sem[k], v)

    return nc, cx.newM


_nc_cache = {}


def _get_program():
    if "nc" not in _nc_cache:
        _, M1 = build({})
        nc, M2 = build(M1)
        assert M1 == M2, "milestone mismatch between emission passes"
        _nc_cache["nc"] = nc
    return _nc_cache["nc"]


def kernel(L0_attn_x, L0_attn_time_decay, L0_attn_time_first,
           L0_attn_time_mix_k, L0_attn_time_mix_v, L0_attn_time_mix_r,
           L0_attn_W_key, L0_attn_W_value, L0_attn_W_receptance, L0_attn_W_output,
           L0_ffn_w_gate, L0_ffn_w_up, L0_ffn_w_down):
    x = np.ascontiguousarray(np.asarray(L0_attn_x, dtype=np.float32))
    wk = np.ascontiguousarray(np.asarray(L0_attn_W_key, np.float32).T)
    wv = np.ascontiguousarray(np.asarray(L0_attn_W_value, np.float32).T)
    wr = np.ascontiguousarray(np.asarray(L0_attn_W_receptance, np.float32).T)
    wo = np.ascontiguousarray(np.asarray(L0_attn_W_output, np.float32).T)
    wg = np.ascontiguousarray(np.asarray(L0_ffn_w_gate, np.int32).T)
    wu = np.ascontiguousarray(np.asarray(L0_ffn_w_up, np.int32).T)
    wd = np.ascontiguousarray(np.asarray(L0_ffn_w_down, np.int32).T)

    def perm(v):
        return np.ascontiguousarray(np.asarray(v, np.float32).reshape(8, 128).T)

    tmk = perm(L0_attn_time_mix_k)
    tmv = perm(L0_attn_time_mix_v)
    tmr = perm(L0_attn_time_mix_r)
    td = perm(L0_attn_time_decay)
    tf = perm(L0_attn_time_first)
    ident = np.eye(128, dtype=ml_dtypes.bfloat16)

    nc = _get_program()
    wk_h = [np.ascontiguousarray(wk[:, h * CH:(h + 1) * CH]) for h in range(2)]
    wv_h = [np.ascontiguousarray(wv[:, h * CH:(h + 1) * CH]) for h in range(2)]
    td_h = [np.ascontiguousarray(td[:, h * NDBL:(h + 1) * NDBL]) for h in range(2)]
    tf_h = [np.ascontiguousarray(tf[:, h * NDBL:(h + 1) * NDBL]) for h in range(2)]
    maps = []
    for core in range(8):
        b, h = core // 2, core % 2
        maps.append({
            "x_all": np.ascontiguousarray(x[b]),
            "x_own": np.ascontiguousarray(x[b, h * OWN:(h + 1) * OWN]),
            "wkT": wk_h[h], "wvT": wv_h[h], "wrT": wr, "woT": wo,
            "wgT": wg, "wuT": wu, "wdT": wd,
            "tmk": tmk, "tmv": tmv, "tmr": tmr, "td": td_h[h], "tf": tf_h[h],
            "ident": ident,
        })
    res = run_bass_kernel_spmd(nc, maps, core_ids=list(range(8)))
    out = np.empty((B, T, C), np.float32)
    for core in range(8):
        b, h = core // 2, core % 2
        out[b, h * OWN:(h + 1) * OWN] = np.asarray(res.results[core]["out_own"])
    return out


# version 15
# speedup vs baseline: 1.4174x; 1.1137x over previous
import numpy as np
import ml_dtypes
import concourse.bass as bass
import concourse.mybir as mybir
from concourse.bass_utils import run_bass_kernel_spmd
from contextlib import ExitStack

F32 = mybir.dt.float32
BF16 = mybir.dt.bfloat16
I32 = mybir.dt.int32
AF = mybir.ActivationFunctionType
ALU = mybir.AluOpType

B, T, C, HID = 4, 8192, 1024, 3072
TCA = 512
NCA = T // TCA
OWN = T // 2
CH = C // 2
NDBL = CH // 128
TCB = 256
NCB = OWN // TCB
NCC = C // 128
NHB = HID // 128
EPS = 1e-5
NEWTON_ITERS = 8
PAIRS = [[0, 1], [2, 3], [4, 5], [6, 7]]
SEMS = ["lds", "ldp0", "ldp1", "ldx0", "ldx1", "ldxn0", "ldxn1", "ldy", "ldxo", "ldwro",
        "ldffnw", "castw", "ldos", "stp0", "stp1", "sty", "stx", "sto0", "sto1",
        "std", "cl", "v", "s", "t"]
DEBUG_TAPS = False


class Ctx:
    def __init__(self, nc, M):
        self.nc = nc
        self.M = M
        self.newM = {}
        self.cnt = {k: 0 for k in SEMS}
        self.sem = {}
        self.pass1 = not M

    def mark(self, key, semname):
        self.newM[key] = self.cnt[semname]

    def bump(self, inst, semname, n=1):
        inst.then_inc(self.sem[semname], n)
        self.cnt[semname] += n

    def wait(self, eng, key, semname):
        if self.pass1:
            return
        val = self.M.get(key)
        assert val is not None, f"missing milestone {key}"
        if val > 0:
            eng.wait_ge(self.sem[semname], val)

    def tap(self, g, name, src_ap, shape, dtype):
        t = self.nc.dram_tensor("tap_" + name, shape, dtype, kind="ExternalOutput").ap()
        self.bump(g.dma_start(out=t, in_=src_ap), "std", 16)
        self.mark("tap_" + name, "std")


def build(M):
    nc = bass.Bass("TRN2", target_bir_lowering=False, detect_race_conditions=False)
    nc.cache_partition_id()
    cx = Ctx(nc, M)

    xall = nc.dram_tensor("x_all", [T, C], F32, kind="ExternalInput").ap()
    xown = nc.dram_tensor("x_own", [OWN, C], F32, kind="ExternalInput").ap()
    wkT = nc.dram_tensor("wkT", [C, CH], F32, kind="ExternalInput").ap()
    wvT = nc.dram_tensor("wvT", [C, CH], F32, kind="ExternalInput").ap()
    wrT = nc.dram_tensor("wrT", [C, C], F32, kind="ExternalInput").ap()
    woT = nc.dram_tensor("woT", [C, C], F32, kind="ExternalInput").ap()
    wgT = nc.dram_tensor("wgT", [C, HID], I32, kind="ExternalInput").ap()
    wuT = nc.dram_tensor("wuT", [C, HID], I32, kind="ExternalInput").ap()
    wdT = nc.dram_tensor("wdT", [HID, C], I32, kind="ExternalInput").ap()
    tmk_i = nc.dram_tensor("tmk", [128, 8], F32, kind="ExternalInput").ap()
    tmv_i = nc.dram_tensor("tmv", [128, 8], F32, kind="ExternalInput").ap()
    tmr_i = nc.dram_tensor("tmr", [128, 8], F32, kind="ExternalInput").ap()
    td_i = nc.dram_tensor("td", [128, NDBL], F32, kind="ExternalInput").ap()
    tf_i = nc.dram_tensor("tf", [128, NDBL], F32, kind="ExternalInput").ap()
    ident_i = nc.dram_tensor("ident", [128, 128], BF16, kind="ExternalInput").ap()

    out_own = nc.dram_tensor("out_own", [OWN, C], F32, kind="ExternalOutput").ap()

    xnT_hbm = nc.dram_tensor("xnT_hbm", [C, T + 1], BF16, kind="Internal").ap()
    cin_all = nc.dram_tensor("cin_all", [NCA * CH, TCA], BF16, kind="Internal").ap()
    cout_all = nc.dram_tensor("cout_all", [NCA * C, TCA], BF16, kind="Internal").ap()
    own_xn = nc.dram_tensor("own_xn", [C, OWN + 1], BF16, kind="Internal").ap()
    own_y = nc.dram_tensor("own_y", [C, OWN], BF16, kind="Internal").ap()
    wr_hbm = nc.dram_tensor("wr_hbm", [C, C], BF16, kind="Internal").ap()
    wo_hbm = nc.dram_tensor("wo_hbm", [C, C], BF16, kind="Internal").ap()
    wg_hbm = nc.dram_tensor("wg_hbm", [C, HID], BF16, kind="Internal").ap()
    wu_hbm = nc.dram_tensor("wu_hbm", [C, HID], BF16, kind="Internal").ap()
    wd_hbm = nc.dram_tensor("wd_hbm", [HID, C], BF16, kind="Internal").ap()

    for semname in SEMS:
        cx.sem[semname] = nc.alloc_semaphore("sem_" + semname)

    identb = nc.alloc_sbuf_tensor("identb", [128, 128], BF16)
    tmk = nc.alloc_sbuf_tensor("tmk_t", [128, 8], F32)
    tmv = nc.alloc_sbuf_tensor("tmv_t", [128, 8], F32)
    tmr = nc.alloc_sbuf_tensor("tmr_t", [128, 8], F32)
    tdt = nc.alloc_sbuf_tensor("td_t", [128, NDBL], F32)
    tft = nc.alloc_sbuf_tensor("tf_t", [128, NDBL], F32)
    eu = nc.alloc_sbuf_tensor("eu_t", [128, NDBL], F32)
    ddec = nc.alloc_sbuf_tensor("ddec_t", [128, NDBL], F32)
    e1n = nc.alloc_sbuf_tensor("e1n_t", [128, NDBL], F32)
    c1p5 = nc.alloc_sbuf_tensor("c1p5", [128, 64], F32)


    sA = ExitStack()
    ptr_banks = [sA.enter_context(nc.psum_tensor(f"ptr{i}", [128, 1024], BF16)) for i in range(4)]
    pk_banks = [sA.enter_context(nc.psum_tensor(f"pk{i}", [128, 512], F32)) for i in range(2)]
    pv_banks = [sA.enter_context(nc.psum_tensor(f"pv{i}", [128, 512], F32)) for i in range(2)]
    wkTb = sA.enter_context(nc.sbuf_tensor("wkTb", [128, 8, CH], BF16))
    wvTb = sA.enter_context(nc.sbuf_tensor("wvTb", [128, 8, CH], BF16))
    dfull = sA.enter_context(nc.sbuf_tensor("dfull", [128, NDBL, TCA], F32))
    carryPv = sA.enter_context(nc.sbuf_tensor("carryPv", [128, 8], F32))
    carryQv = sA.enter_context(nc.sbuf_tensor("carryQv", [128, 8], F32))
    wtmpP = sA.enter_context(nc.sbuf_tensor("wtmpP", [128, 256], F32))
    wtmpQ = sA.enter_context(nc.sbuf_tensor("wtmpQ", [128, 256], F32))
    onesrow = sA.enter_context(nc.sbuf_tensor("onesrow", [128, TCA], F32))
    sNs = sA.enter_context(nc.sbuf_tensor("sNs", [128, 64], F32))
    bNs = sA.enter_context(nc.sbuf_tensor("bNs", [128, 64], F32))

    sP = ExitStack()
    wsrcf = [sP.enter_context(nc.sbuf_tensor(f"wsrcf{i}", [128, 4, 1024], F32)) for i in range(2)]
    wcvtb = [sP.enter_context(nc.sbuf_tensor(f"wcvtb{i}", [128, 4, 1024], BF16)) for i in range(2)]


    pieces = []
    for w_src, dst_res, tag in [(wkT, wkTb, "wk"), (wvT, wvTb, "wv")]:
        for ph in range(2):
            src = w_src[ph * 512:(ph + 1) * 512, :].rearrange("(a p) d -> p a d", p=128)
            pieces.append(("f2res", src, (dst_res, ph), 127.0, f"{tag}{ph}"))
    for w_src, w_dst, scale, tag in [(wrT, wr_hbm, 0.5 * 127.0, "wr"), (woT, wo_hbm, 0.5, "wo")]:
        for ph in range(2):
            src = w_src[ph * 512:(ph + 1) * 512, :].rearrange("(a p) d -> p a d", p=128)
            dst = w_dst[ph * 512:(ph + 1) * 512, :].rearrange("(a p) d -> p a d", p=128)
            pieces.append(("f2hbm", src, dst, scale, f"{tag}{ph}"))

    with nc.Block() as blk:
        @blk.sync
        def _(sync):
            for t_in, t_sb in [(tmk_i, tmk), (tmv_i, tmv), (tmr_i, tmr),
                               (td_i, tdt), (tf_i, tft), (ident_i, identb)]:
                cx.bump(sync.dma_start(out=t_sb[:], in_=t_in), "lds", 16)
            cx.mark("ld_small", "lds")
            for i, (kind, src, dst, scale, tag) in enumerate(pieces):
                par = i % 2
                buf = wsrcf[par][:, :, 0:CH] if kind == "f2res" else wsrcf[par][:]
                if i >= 2:
                    cx.wait(sync, f"pcvt{i-2}", "v")
                cx.bump(sync.dma_start(out=buf, in_=src), f"ldp{par}", 16)
                cx.mark(f"pld{i}", f"ldp{par}")

        @blk.scalar
        def _(scalar):
            cx.wait(scalar, "ld_small", "lds")
            cx.bump(scalar.activation(eu[:], tft[:], AF.Exp), "s", 1)
            cx.mark("eu", "s")
            cx.bump(scalar.activation(ddec[:], tdt[:], AF.Exp), "s", 1)
            cx.mark("etd", "s")
            cx.wait(scalar, "e1n", "v")
            cx.bump(scalar.activation(ddec[:], e1n[:], AF.Exp), "s", 1)
            cx.mark("ddec", "s")

        @blk.vector
        def _(vector):
            vector.memset(c1p5[:], 1.5)
            cx.wait(vector, "etd", "s")
            cx.bump(vector.tensor_scalar(e1n[:], ddec[:], -1.0, None, ALU.mult), "v", 1)
            cx.mark("e1n", "v")
            cx.wait(vector, "ddec", "s")
            vector.memset(onesrow[:], 1.0)
            for ccN in range(NDBL):
                inst = vector.tensor_scalar(dfull[:, ccN, :], onesrow[:],
                                            ddec[:, ccN:ccN + 1], None, ALU.mult)
            cx.bump(inst, "v", 1)
            cx.mark("dfull", "v")
            last_cvt_user = {}
            for i, (kind, src, dst, scale, tag) in enumerate(pieces):
                par = i % 2
                cx.wait(vector, f"pld{i}", f"ldp{par}")
                if kind == "f2res":
                    dst_res, ph = dst
                    inst = vector.tensor_scalar(dst_res[:, ph * 4:(ph + 1) * 4, :],
                                                wsrcf[par][:, :, 0:CH],
                                                scale, None, ALU.mult)
                else:
                    cbuf = ("wcvtb", par)
                    prev = last_cvt_user.get(cbuf)
                    if prev is not None:
                        cx.wait(vector, f"pst{prev}", f"stp{prev % 2}")
                    last_cvt_user[cbuf] = i
                    inst = vector.tensor_scalar(wcvtb[par][:], wsrcf[par][:], scale, None, ALU.mult)
                cx.bump(inst, "v", 1)
                cx.mark(f"pcvt{i}", "v")
            cx.mark("prep_v_done", "v")

        @blk.gpsimd
        def _(g):
            for i, (kind, src, dst, scale, tag) in enumerate(pieces):
                if kind == "f2res":
                    continue
                par = i % 2
                cx.wait(g, f"pcvt{i}", "v")
                cx.bump(g.dma_start(out=dst, in_=wcvtb[par][:]), f"stp{par}", 16)
                cx.mark(f"pst{i}", f"stp{par}")
            cx.mark("prep_st0_done", "stp0")
            cx.mark("prep_st1_done", "stp1")



    sP.close()

    xf2 = [sA.enter_context(nc.sbuf_tensor(f"xf{i}", [128, 4, 1024], F32)) for i in range(2)]
    sqscr = sA.enter_context(nc.sbuf_tensor("sqscr", [128, 1024], BF16))
    ssq = sA.enter_context(nc.sbuf_tensor("ssq", [128, 64], F32))
    sm = sA.enter_context(nc.sbuf_tensor("sm", [128, 64], F32))
    meanT = sA.enter_context(nc.sbuf_tensor("meanT", [128, 64], F32))
    varT = sA.enter_context(nc.sbuf_tensor("varT", [128, 64], F32))
    sNb = [sA.enter_context(nc.sbuf_tensor(f"sN{i}", [128, 64], F32)) for i in range(2)]
    bN = sA.enter_context(nc.sbuf_tensor("bN", [128, 64], F32))
    nt1 = sA.enter_context(nc.sbuf_tensor("nt1", [128, 64], F32))
    nt2 = sA.enter_context(nc.sbuf_tensor("nt2", [128, 64], F32))
    vtmp = sA.enter_context(nc.sbuf_tensor("vtmp", [128, 64], F32))
    xn_tok = sA.enter_context(nc.sbuf_tensor("xn_tok", [128, 4, 1024], BF16))
    xnT = sA.enter_context(nc.sbuf_tensor("xnT", [128, 8, TCA + 1], BF16))
    dxx = sA.enter_context(nc.sbuf_tensor("dxx", [128, TCA], BF16))
    xkTb2 = [sA.enter_context(nc.sbuf_tensor(f"xkT{i}", [128, 8, TCA], BF16)) for i in range(2)]
    xvTb2 = [sA.enter_context(nc.sbuf_tensor(f"xvT{i}", [128, 8, TCA], BF16)) for i in range(2)]
    ekf = sA.enter_context(nc.sbuf_tensor("ekf", [128, 4, TCA + 2], F32))
    vvf = sA.enter_context(nc.sbuf_tensor("vvf", [128, 4, TCA], F32))
    ekvf = sA.enter_context(nc.sbuf_tensor("ekvf", [128, 4, TCA + 2], F32))
    Pt = sA.enter_context(nc.sbuf_tensor("Pt", [128, 4, TCA], F32))
    Qt = sA.enter_context(nc.sbuf_tensor("Qt", [128, 4, TCA], F32))
    numt = sA.enter_context(nc.sbuf_tensor("numt", [128, 4, TCA], F32))
    dent = sA.enter_context(nc.sbuf_tensor("dent", [128, 4, TCA], F32))
    yb = sA.enter_context(nc.sbuf_tensor("yb", [128, NDBL, TCA], BF16))

    CASTS = []
    for w_src, w_dst, nr in [(wgT, wg_hbm, C), (wuT, wu_hbm, C), (wdT, wd_hbm, HID)]:
        step = nr // 8
        for p in range(8):
            CASTS.append((w_src, w_dst, (p * step, (p + 1) * step)))

    with nc.Block() as blk:
        @blk.sync
        def _(sync):
            HW_ = OWN // 2
            cv = cout_all.rearrange("(c p) t -> p c t", p=C)
            xn_need = [3, 7, 11, 15]
            cl_need = [4, 8, 12, 16]
            conds = [None, None]

            def emit_reloc(q8):
                h, q = q8 // 2, q8 % 2
                cx.wait(sync, f"xnwr{xn_need[q8]}", "stx")
                if cx.pass1:
                    cx.cnt["ldos"] += 32
                    return
                sync.wait_ge(cx.sem["cl"], cl_need[q8])
                cond = conds[h]
                so = h * OWN
                if q == 0:
                    xdst = own_xn[:, 0:HW_ + 1]
                    xsrc = xnT_hbm[:, so:so + HW_ + 1]
                else:
                    xdst = own_xn[:, HW_ + 1:OWN + 1]
                    xsrc = xnT_hbm[:, so + HW_ + 1:so + OWN + 1]
                cx.bump(sync.dma_start(out=xdst, in_=xsrc, cond=cond), "ldos", 16)
                ydst = own_y[:, q * HW_:(q + 1) * HW_].rearrange("p (c t) -> p c t", t=TCA)
                c0_ = h * (NCA // 2) + q * (HW_ // TCA)
                cx.bump(sync.dma_start(out=ydst, in_=cv[:, c0_:c0_ + HW_ // TCA, :],
                                       cond=cond), "ldos", 16)

            for j in range(NCA):
                if j == 0:
                    cx.wait(sync, "prep_v_done", "v")
                    cx.wait(sync, "prep_st0_done", "stp0")
                    cx.wait(sync, "prep_st1_done", "stp1")
                    if not cx.pass1:
                        pid = sync.partition_id()
                        hval = sync.compute_val(pid % 2)
                        conds[0] = sync.compute_val(hval == 0)
                        conds[1] = sync.compute_val(hval == 1)
                elif j >= 2:
                    cx.wait(sync, f"lnap{j-2}", "v")
                    cx.wait(sync, f"sq{j-2}", "s")
                    if DEBUG_TAPS and j == 2:
                        cx.wait(sync, "tapsA_done", "std")
                src = xall[j * TCA:(j + 1) * TCA, :].rearrange("(a p) c -> p a c", p=128)
                cx.bump(sync.dma_start(out=xf2[j % 2][:], in_=src), f"ldx{j % 2}", 16)
                cx.mark(f"ldx{j}", f"ldx{j % 2}")
                if j in (6, 9, 13):
                    emit_reloc({6: 0, 9: 1, 13: 2}[j])
            emit_reloc(3)
            cx.mark("ownstage", "ldos")

        @blk.scalar
        def _(scalar):
            def emit_sq_relay(j):
                cx.wait(scalar, f"ldx{j}", f"ldx{j % 2}")
                cx.wait(scalar, f"stats{j-1}" if j > 0 else "statsinit", "v")
                for a in range(4):
                    scalar.activation(sqscr[:], xf2[j % 2][:, a, :], AF.Square,
                                      accum_out=ssq[:, a:a + 1])
                    inst = scalar.activation(sqscr[:], xf2[j % 2][:, a, :], AF.Copy,
                                             accum_out=sm[:, a:a + 1])
                cx.bump(inst, "s", 1)
                cx.mark(f"sq{j}", "s")

            def emit_relay(j):
                cx.wait(scalar, f"stats{j}", "v")
                scalar.copy(sNs[:], sNb[(NEWTON_ITERS + 1) % 2][:])
                inst = scalar.copy(bNs[:], bN[:])
                cx.bump(inst, "s", 1)
                cx.mark(f"lnrelay{j}", "s")

            def emit_trevac(J):
                for ccN in range(8):
                    cx.wait(scalar, f"tr{J}_{ccN}", "t")
                    cx.wait(scalar, f"ghost{J}", "v")
                    if J > 0 and ccN == 0:
                        cx.wait(scalar, f"xnwr{J-1}", "stx")
                    inst = scalar.copy(xnT[:, ccN, 1:TCA + 1], ptr_banks[ccN % 4][:, 0:TCA])
                    cx.bump(inst, "s", 1)
                    cx.mark(f"trevac{J}_{ccN}", "s")

            emit_sq_relay(0)
            emit_relay(0)
            emit_sq_relay(1)
            emit_trevac(0)
            for j in range(NCA):
                for db in range(NDBL):
                    cx.wait(scalar, f"kv{j}_{db}", "t")
                    if db == 0 and j > 0:
                        cx.wait(scalar, f"scan{j-1}_0", "v")
                    scalar.activation(ekf[:, db, 1:257], pk_banks[db % 2][:, 0:256], AF.Exp)
                    cx.bump(scalar.activation(ekf[:, db, 258:TCA + 2], pk_banks[db % 2][:, 256:TCA], AF.Exp), "s", 1)
                    cx.bump(scalar.copy(vvf[:, db, :], pv_banks[db % 2][:]), "s", 1)
                    cx.mark(f"ekevac{j}_{db}", "s")
                    if db == 0 and j + 1 < NCA:
                        emit_relay(j + 1)
                if j + 1 < NCA:
                    emit_trevac(j + 1)
                cx.wait(scalar, f"den{j}_0", "v")
                for db in range(NDBL):
                    inst = scalar.activation(vvf[:, db, :], dent[:, db, :], AF.Ln)
                cx.bump(inst, "s", 1)
                cx.mark(f"lnd{j}_0", "s")
                cx.wait(scalar, f"negd{j}_0", "v")
                for db in range(NDBL):
                    inst = scalar.activation(vvf[:, db, :], dent[:, db, :], AF.Exp)
                cx.bump(inst, "s", 1)
                cx.mark(f"expd{j}_0", "s")
                if j + 2 < NCA:
                    emit_sq_relay(j + 2)

        @blk.vector
        def _(vector):
            def emit_stats_lnap(j):
                cx.wait(vector, f"sq{j}", "s")
                vector.tensor_scalar(meanT[:], sm[:], 1.0 / C, None, ALU.mult)
                vector.tensor_scalar(vtmp[:], ssq[:], 1.0 / C, None, ALU.mult)
                vector.tensor_mul(nt1[:], meanT[:], meanT[:])
                vector.tensor_sub(nt2[:], vtmp[:], nt1[:])
                vector.tensor_scalar(varT[:], nt2[:], EPS, None, ALU.add)
                vector.memset(sNb[0][:], 1.0)
                cur = 0
                for _ in range(NEWTON_ITERS):
                    vector.tensor_mul(nt1[:], sNb[cur][:], sNb[cur][:])
                    vector.tensor_mul(nt2[:], nt1[:], varT[:])
                    vector.scalar_tensor_tensor(nt1[:], nt2[:], -0.5, c1p5[:], ALU.mult, ALU.add)
                    vector.tensor_mul(sNb[1 - cur][:], sNb[cur][:], nt1[:])
                    cur = 1 - cur
                vector.tensor_scalar(sNb[1 - cur][:], sNb[cur][:], 1.0 / 127.0, None, ALU.mult)
                sN = sNb[1 - cur]
                inst = vector.tensor_mul(bN[:], meanT[:], sN[:])
                cx.bump(inst, "v", 1)
                cx.mark(f"stats{j}", "v")
                cx.wait(vector, f"lnrelay{j}", "s")
                if j > 0:
                    cx.wait(vector, f"tr{j-1}_7", "t")
                for a in range(4):
                    inst = vector.tensor_scalar(xn_tok[:, a, :], xf2[j % 2][:, a, :],
                                                sNs[:, a:a + 1], bNs[:, a:a + 1],
                                                ALU.mult, ALU.subtract)
                cx.bump(inst, "v", 1)
                cx.mark(f"lnap{j}", "v")

            for j in range(NCA):
                if j == 0:
                    cx.wait(vector, "prep_st0_done", "stp0")
                    cx.wait(vector, "prep_st1_done", "stp1")
                    vector.memset(sm[:], 1.0)
                    vector.memset(carryPv[:], 0.0)
                    vector.memset(carryQv[:], 0.0)
                    inst = vector.memset(ssq[:], 1.0)
                    cx.bump(inst, "v", 1)
                    cx.mark("statsinit", "v")
                    emit_stats_lnap(0)
                    for ccN in range(8):
                        inst = vector.memset(xnT[:, ccN, 0:1], 0.0)
                    cx.bump(inst, "v", 1)
                    cx.mark("ghost0", "v")
                cx.wait(vector, f"trevac{j}_7", "s")
                if j > 1:
                    cx.wait(vector, f"kv{j-2}_{NDBL-1}", "t")
                xkT, xvT = xkTb2[j % 2], xvTb2[j % 2]
                for ccN in range(8):
                    vector.tensor_sub(dxx[:], xnT[:, ccN, 1:TCA + 1], xnT[:, ccN, 0:TCA])
                    vector.scalar_tensor_tensor(xkT[:, ccN, :], dxx[:], tmk[:, ccN:ccN + 1],
                                                xnT[:, ccN, 0:TCA], ALU.mult, ALU.add)
                    inst = vector.scalar_tensor_tensor(xvT[:, ccN, :], dxx[:], tmv[:, ccN:ccN + 1],
                                                       xnT[:, ccN, 0:TCA], ALU.mult, ALU.add)
                cx.bump(inst, "v", 1)
                cx.mark(f"mix{j}", "v")
                if j + 1 < NCA:
                    emit_stats_lnap(j + 1)
                    for ccN in range(8):
                        inst = vector.tensor_copy(xnT[:, ccN, 0:1], xnT[:, ccN, TCA:TCA + 1])
                    cx.bump(inst, "v", 1)
                    cx.mark(f"ghost{j+1}", "v")
                for db in range(NDBL):
                    cx.wait(vector, f"ekevac{j}_{db}", "s")
                if j > 0:
                    cx.wait(vector, f"ywr{j-1}", "sty")
                vector.tensor_copy(ekvf[:, 0:4, 0:1], carryPv[:, 0:4])
                vector.tensor_copy(ekf[:, 0:4, 0:1], carryQv[:, 0:4])
                for db in range(NDBL):
                    vector.tensor_mul(ekvf[:, db, 1:257], ekf[:, db, 1:257], vvf[:, db, 0:256])
                    vector.tensor_mul(ekvf[:, db, 258:TCA + 2], ekf[:, db, 258:TCA + 2], vvf[:, db, 256:TCA])
                for db in range(NDBL):
                    vector.tensor_tensor_scan(Pt[:, db, 0:256], dfull[:, db, 0:256],
                                              ekvf[:, db, 0:256], 0.0, ALU.mult, ALU.add)
                    vector.tensor_tensor_scan(Qt[:, db, 0:256], dfull[:, db, 0:256],
                                              ekf[:, db, 0:256], 0.0, ALU.mult, ALU.add)
                for db in range(NDBL):
                    vector.tensor_mul(wtmpP[:, db * 64:(db + 1) * 64], Pt[:, db, 192:256], dfull[:, db, 0:64])
                    vector.tensor_mul(wtmpQ[:, db * 64:(db + 1) * 64], Qt[:, db, 192:256], dfull[:, db, 0:64])
                for db in range(NDBL):
                    vector.tensor_add(ekvf[:, db, 257:258], wtmpP[:, db * 64 + 63:db * 64 + 64], ekvf[:, db, 256:257])
                    vector.tensor_add(ekf[:, db, 257:258], wtmpQ[:, db * 64 + 63:db * 64 + 64], ekf[:, db, 256:257])
                for db in range(NDBL):
                    vector.tensor_tensor_scan(Pt[:, db, 256:TCA], dfull[:, db, 0:256],
                                              ekvf[:, db, 257:TCA + 1], 0.0, ALU.mult, ALU.add)
                    vector.tensor_tensor_scan(Qt[:, db, 256:TCA], dfull[:, db, 0:256],
                                              ekf[:, db, 257:TCA + 1], 0.0, ALU.mult, ALU.add)
                for db in range(NDBL):
                    vector.scalar_tensor_tensor(numt[:, db, 0:256], ekvf[:, db, 1:257],
                                                eu[:, db:db + 1], Pt[:, db, 0:256],
                                                ALU.mult, ALU.add)
                    vector.scalar_tensor_tensor(numt[:, db, 256:TCA], ekvf[:, db, 258:TCA + 2],
                                                eu[:, db:db + 1], Pt[:, db, 256:TCA],
                                                ALU.mult, ALU.add)
                    vector.scalar_tensor_tensor(dent[:, db, 0:256], ekf[:, db, 1:257],
                                                eu[:, db:db + 1], Qt[:, db, 0:256],
                                                ALU.mult, ALU.add)
                    cx.last_den_inst = vector.scalar_tensor_tensor(
                        dent[:, db, 256:TCA], ekf[:, db, 258:TCA + 2],
                        eu[:, db:db + 1], Qt[:, db, 256:TCA],
                        ALU.mult, ALU.add)
                cx.bump(cx.last_den_inst, "v", 1)
                cx.mark(f"den{j}_0", "v")
                for db in range(NDBL):
                    vector.tensor_mul(wtmpP[:, db * 64:(db + 1) * 64], Pt[:, db, 448:TCA], dfull[:, db, 0:64])
                    vector.tensor_mul(wtmpQ[:, db * 64:(db + 1) * 64], Qt[:, db, 448:TCA], dfull[:, db, 0:64])
                for db in range(NDBL):
                    vector.tensor_add(carryPv[:, db:db + 1], wtmpP[:, db * 64 + 63:db * 64 + 64], ekvf[:, db, TCA + 1:TCA + 2])
                    vector.tensor_add(carryQv[:, db:db + 1], wtmpQ[:, db * 64 + 63:db * 64 + 64], ekf[:, db, TCA + 1:TCA + 2])
                cx.wait(vector, f"lnd{j}_0", "s")
                for db in range(NDBL):
                    inst = vector.tensor_scalar(dent[:, db, :], vvf[:, db, :], -1.0, None, ALU.mult)
                cx.bump(inst, "v", 1)
                cx.mark(f"negd{j}_0", "v")
                cx.wait(vector, f"expd{j}_0", "s")
                for db in range(NDBL):
                    inst = vector.tensor_mul(yb[:, db, :], numt[:, db, :], vvf[:, db, :])
                cx.bump(inst, "v", 1)
                cx.mark(f"scan{j}_0", "v")

        @blk.tensor
        def _(tensor):
            def emit_tr(j, ccs):
                for ccN in ccs:
                    if j > 0 or ccN >= 4:
                        jw, cw = (j, ccN - 4) if ccN >= 4 else (j - 1, ccN + 4)
                        cx.wait(tensor, f"trevac{jw}_{cw}", "s")
                    for a in range(4):
                        inst = tensor.transpose(ptr_banks[ccN % 4][:, a * 128:(a + 1) * 128],
                                                xn_tok[:, a, ccN * 128:(ccN + 1) * 128], identb[:])
                    cx.bump(inst, "t", 1)
                    cx.mark(f"tr{j}_{ccN}", "t")

            def emit_kv(j, dbs):
                xkT, xvT = xkTb2[j % 2], xvTb2[j % 2]
                for db in dbs:
                    if j > 0 or db >= 2:
                        jw, dw = (j, db - 2) if db >= 2 else (j - 1, db + 2)
                        cx.wait(tensor, f"ekevac{jw}_{dw}", "s")
                    for ccN in range(8):
                        tensor.matmul(pk_banks[db % 2][:], wkTb[:, ccN, db * 128:(db + 1) * 128],
                                      xkT[:, ccN, :], start=(ccN == 0), stop=(ccN == 7))
                    for ccN in range(8):
                        inst = tensor.matmul(pv_banks[db % 2][:], wvTb[:, ccN, db * 128:(db + 1) * 128],
                                             xvT[:, ccN, :], start=(ccN == 0), stop=(ccN == 7))
                    cx.bump(inst, "t", 1)
                    cx.mark(f"kv{j}_{db}", "t")

            cx.wait(tensor, "lnap0", "v")
            emit_tr(0, range(8))
            for j in range(NCA):
                cx.wait(tensor, f"mix{j}", "v")
                emit_kv(j, range(0, 2))
                if j + 1 < NCA:
                    cx.wait(tensor, f"lnap{j+1}", "v")
                    emit_tr(j + 1, range(0, 4))
                emit_kv(j, range(2, 4))
                if j + 1 < NCA:
                    emit_tr(j + 1, range(4, 8))

        @blk.gpsimd
        def _(g):
            for j in range(NCA):
                if DEBUG_TAPS and j == 0:
                    cx.wait(g, "scan0_0", "v")
                    cx.tap(g, "xf", xf2[0][:], [128, 4, 1024], F32)
                    cx.tap(g, "sm", sm[:], [128, 64], F32)
                    cx.tap(g, "ssq", ssq[:], [128, 64], F32)
                    cx.tap(g, "varT", varT[:], [128, 64], F32)
                    cx.tap(g, "meanT", meanT[:], [128, 64], F32)
                    cx.tap(g, "vtmp", vtmp[:], [128, 64], F32)
                    cx.tap(g, "nt1", nt1[:], [128, 64], F32)
                    cx.tap(g, "nt2", nt2[:], [128, 64], F32)
                    cx.tap(g, "xn_tok", xn_tok[:], [128, 4, 1024], BF16)
                    cx.tap(g, "xkT", xkTb2[0][:], [128, 8, TCA], BF16)
                    cx.tap(g, "xvT", xvTb2[0][:], [128, 8, TCA], BF16)
                    cx.tap(g, "sN", sNb[(NEWTON_ITERS + 1) % 2][:], [128, 64], F32)
                    cx.tap(g, "bN", bN[:], [128, 64], F32)
                    cx.tap(g, "ekf", ekf[:], [128, 4, TCA + 2], F32)
                    cx.tap(g, "ekvf", ekvf[:], [128, 4, TCA + 2], F32)
                    cx.tap(g, "Pt", Pt[:], [128, 4, TCA], F32)
                    cx.tap(g, "Qt", Qt[:], [128, 4, TCA], F32)
                    cx.tap(g, "numt", numt[:], [128, 4, TCA], F32)
                    cx.tap(g, "dent", dent[:], [128, 4, TCA], F32)
                    cx.mark("tapsA_half0", "std")
                cx.wait(g, f"trevac{j}_7", "s")
                if j == 0:
                    cx.wait(g, "ghost0", "v")
                    dst2 = xnT_hbm[:, 0:TCA + 1].rearrange("(cc p) t -> p cc t", p=128)
                    cx.bump(g.dma_start(out=dst2, in_=xnT[:]), "stx", 16)
                else:
                    dst2 = xnT_hbm[:, 1 + j * TCA:1 + (j + 1) * TCA].rearrange("(cc p) t -> p cc t", p=128)
                    cx.bump(g.dma_start(out=dst2, in_=xnT[:, :, 1:TCA + 1]), "stx", 16)
                cx.mark(f"xnwr{j}", "stx")
                cx.wait(g, f"scan{j}_0", "v")
                if DEBUG_TAPS and j == 0:
                    cx.tap(g, "yb", yb[:], [128, NDBL, TCA], BF16)
                dst = cin_all[j * CH:(j + 1) * CH, :].rearrange("(cc p) t -> p cc t", p=128)
                cx.bump(g.dma_start(out=dst, in_=yb[:]), "sty", 16)
                cx.mark(f"ywr{j}", "sty")
                cx.wait(g, f"ywr{j}", "sty")
                cx.bump(g.collective_compute(
                    "AllGather", ALU.bypass,
                    ins=[cin_all[j * CH:(j + 1) * CH, :]],
                    outs=[cout_all[j * C:(j + 1) * C, :]],
                    replica_groups=PAIRS), "cl", 1)
                cx.mark(f"cl{j}", "cl")
                if 2 <= j <= 13:
                    for pc in range(2):
                        w_src, w_dst, nrow = CASTS[(j - 2) * 2 + pc]
                        r0 = nrow
                        cx.bump(g.dma_start(out=w_dst[r0[0]:r0[1], :],
                                            in_=w_src[r0[0]:r0[1], :]), "castw", 16)
                    if j == 13:
                        cx.mark("castffn", "castw")
                if DEBUG_TAPS and j == 0:
                    cx.tap(g, "xnT", xnT[:], [128, 8, TCA + 1], BF16)
                    cx.mark("tapsA_done", "std")

    cx.newM["A_v_done"] = cx.cnt["v"]
    cx.newM["A_s_done"] = cx.cnt["s"]
    cx.newM["A_t_done"] = cx.cnt["t"]
    cx.newM["A_sty_done"] = cx.cnt["sty"]
    cx.newM["A_stx_done"] = cx.cnt["stx"]
    cx.newM["A_cl_done"] = cx.cnt["cl"]
    sA.close()

    sB = ExitStack()
    pr_banks = [sB.enter_context(nc.psum_tensor(f"pr{i}", [128, 512], F32)) for i in range(2)]
    pg_banks = [sB.enter_context(nc.psum_tensor(f"pg{i}", [128, 512], F32)) for i in range(2)]
    pu_banks = [sB.enter_context(nc.psum_tensor(f"pu{i}", [128, 512], F32)) for i in range(2)]
    po_banks = [sB.enter_context(nc.psum_tensor(f"po{i}", [128, 512], F32)) for i in range(2)]
    wgTb = sB.enter_context(nc.sbuf_tensor("wgTb", [128, 8, HID], BF16))
    wuTb = sB.enter_context(nc.sbuf_tensor("wuTb", [128, 8, HID], BF16))
    wdb = sB.enter_context(nc.sbuf_tensor("wdb", [128, 24, 1024], BF16))
    wrob = sB.enter_context(nc.sbuf_tensor("wrob", [128, 8, 1024], BF16))
    xnTob = [sB.enter_context(nc.sbuf_tensor(f"xnTo{i}", [128, 8, TCB + 1], BF16)) for i in range(2)]
    dxxB = sB.enter_context(nc.sbuf_tensor("dxxB", [128, TCB], BF16))
    xrT = sB.enter_context(nc.sbuf_tensor("xrT", [128, 8, TCB], BF16))
    srT = sB.enter_context(nc.sbuf_tensor("srT", [128, 8, TCB], BF16))
    yo = sB.enter_context(nc.sbuf_tensor("yo", [128, 8, TCB], BF16))
    sgb = sB.enter_context(nc.sbuf_tensor("sgb", [128, TCB], BF16))
    hact = sB.enter_context(nc.sbuf_tensor("hact", [128, 24, TCB], BF16))
    xo4 = sB.enter_context(nc.sbuf_tensor("xo4", [128, 2, 512], F32))
    outt = sB.enter_context(nc.sbuf_tensor("outt", [128, 2, 512], F32))

    RUNS = [(0, 0), (1, 0), (0, 1), (1, 1)]

    with nc.Block() as blk:
        @blk.gpsimd
        def _(g):
            cx.wait(g, "A_v_done", "v")
            cx.wait(g, "A_s_done", "s")
            cx.wait(g, "A_t_done", "t")
            cx.wait(g, "castffn", "castw")
            cx.bump(g.dma_start(out=wgTb[:], in_=wg_hbm.rearrange("(cc p) h -> p cc h", p=128)), "castw", 16)
            cx.mark("ldffnw_g", "castw")

        @blk.scalar
        def _(scalar):
            cx.wait(scalar, "A_v_done", "v")
            cx.wait(scalar, "A_s_done", "s")
            cx.wait(scalar, "A_t_done", "t")
            cx.wait(scalar, "castffn", "castw")
            cx.bump(scalar.dma_start(out=wdb[:], in_=wd_hbm.rearrange("(hb p) d -> p hb d", p=128)), "ldffnw", 16)
            cx.mark("ldffnw_s", "ldffnw")

        @blk.sync
        def _(sync):
            cx.wait(sync, "A_v_done", "v")
            cx.wait(sync, "A_s_done", "s")
            cx.wait(sync, "A_t_done", "t")
            cx.wait(sync, "A_sty_done", "sty")
            cx.wait(sync, "A_stx_done", "stx")
            cx.wait(sync, "castffn", "castw")
            cx.bump(sync.dma_start(out=wuTb[:], in_=wu_hbm.rearrange("(cc p) h -> p cc h", p=128)), "ldffnw", 16)
            cx.mark("ldffnw", "ldffnw")


    with nc.Block() as blk:
        @blk.sync
        def _(sync):
            def emit_ldxn(J):
                if J == 0:
                    if not cx.pass1:
                        sync.wait_ge(cx.sem["ldos"], 96)
                if J == NCB // 2:
                    cx.wait(sync, "ownstage", "ldos")
                if J >= 2:
                    cx.wait(sync, f"bmix{J-2}", "v")
                    cx.wait(sync, f"gu{J-2}_{NHB-1}", "t")
                src = own_xn[:, J * TCB:J * TCB + TCB + 1].rearrange("(cc p) t -> p cc t", p=128)
                cx.bump(sync.dma_start(out=xnTob[J % 2][:], in_=src), f"ldxn{J % 2}", 16)
                cx.mark(f"ldxn{J}", f"ldxn{J % 2}")

            def ldxo(s_, r):
                if s_ > 0 or r >= 2:
                    kw = f"oadd{s_}_{r-2}" if r >= 2 else f"oadd{s_-1}_{r+2}"
                    cx.wait(sync, kw, "v")
                tb, dh = RUNS[r]
                t0 = s_ * TCB
                src = xown[t0 + tb * 128:t0 + (tb + 1) * 128, dh * 512:(dh + 1) * 512]
                cx.bump(sync.dma_start(out=xo4[:, r % 2, :], in_=src), "ldxo", 16)
                cx.mark(f"ldxo{s_}_{r}", "ldxo")

            emit_ldxn(0)
            for s_ in range(NCB):
                if s_ + 1 < NCB:
                    emit_ldxn(s_ + 1)
                if DEBUG_TAPS and s_ == 1:
                    cx.wait(sync, "tapsB_done", "std")
                t0 = s_ * TCB
                if s_ > 0:
                    cx.wait(sync, f"smul{s_-1}", "v")
                srcy = own_y[:, t0:t0 + TCB].rearrange("(cc p) t -> p cc t", p=128)
                cx.bump(sync.dma_start(out=yo[:], in_=srcy), "ldy", 16)
                cx.mark(f"ldy{s_}", "ldy")
                if s_ > 0:
                    cx.wait(sync, f"out{s_-1}_3", "t")
                cx.bump(sync.dma_start(out=wrob[:], in_=wr_hbm.rearrange("(cc p) d -> p cc d", p=128)), "ldwro", 16)
                cx.mark(f"ldwr{s_}", "ldwro")
                if DEBUG_TAPS and s_ == 0:
                    cx.wait(sync, "tap_wrob", "std")
                ldxo(s_, 0)
                ldxo(s_, 1)
                cx.wait(sync, f"rmm{s_}_3", "t")
                cx.bump(sync.dma_start(out=wrob[:, :, 0:512],
                                       in_=wo_hbm[:, 0:512].rearrange("(cc p) d -> p cc d", p=128)),
                        "ldwro", 16)
                cx.mark(f"ldwo{s_}_0", "ldwro")
                ldxo(s_, 2)
                cx.wait(sync, f"rmm{s_}_7", "t")
                cx.bump(sync.dma_start(out=wrob[:, :, 512:1024],
                                       in_=wo_hbm[:, 512:1024].rearrange("(cc p) d -> p cc d", p=128)),
                        "ldwro", 16)
                cx.mark(f"ldwo{s_}_1", "ldwro")
                ldxo(s_, 3)

        @blk.vector
        def _(vector):
            def emit_bmix(s_):
                cx.wait(vector, f"ldxn{s_}", f"ldxn{s_ % 2}")
                if s_ > 0:
                    cx.wait(vector, f"rmm{s_-1}_7", "t")
                xnTo = xnTob[s_ % 2]
                for ccN in range(8):
                    vector.tensor_sub(dxxB[:], xnTo[:, ccN, 1:TCB + 1], xnTo[:, ccN, 0:TCB])
                    inst = vector.scalar_tensor_tensor(xrT[:, ccN, :], dxxB[:], tmr[:, ccN:ccN + 1],
                                                       xnTo[:, ccN, 0:TCB], ALU.mult, ALU.add)
                cx.bump(inst, "v", 1)
                cx.mark(f"bmix{s_}", "v")

            emit_bmix(0)
            for s_ in range(NCB):
                for hb in range(NHB):
                    cx.wait(vector, f"guevac{s_}_{hb}", "s")
                    if hb == 0 and s_ > 0:
                        cx.wait(vector, f"out{s_-1}_3", "t")
                    inst = vector.scalar_tensor_tensor(hact[:, hb, :], sgb[:], 1.0 / 127.0,
                                                       pu_banks[hb % 2][:, 0:TCB], ALU.mult, ALU.mult)
                    cx.bump(inst, "v", 1)
                    cx.mark(f"hm{s_}_{hb}", "v")
                cx.wait(vector, f"revac{s_}_7", "s")
                cx.wait(vector, f"ldy{s_}", "ldy")
                if DEBUG_TAPS and s_ == 0:
                    cx.wait(vector, "tapB_sr", "std")
                vector.tensor_scalar(xrT[:], srT[:], 1.0, None, ALU.add)
                inst = vector.tensor_mul(srT[:], yo[:], xrT[:])
                cx.bump(inst, "v", 1)
                cx.mark(f"smul{s_}", "v")
                if s_ + 1 < NCB:
                    emit_bmix(s_ + 1)
                for r in range(4):
                    cx.wait(vector, f"out{s_}_{r}", "t")
                    cx.wait(vector, f"ldxo{s_}_{r}", "ldxo")
                    prev = (s_, r - 2) if r >= 2 else (s_ - 1, r + 2)
                    if prev[0] >= 0:
                        cx.wait(vector, f"owr{prev[0]}_{prev[1]}", f"sto{r % 2}")
                    inst = vector.tensor_add(outt[:, r % 2, :], po_banks[r % 2][:], xo4[:, r % 2, :])
                    cx.bump(inst, "v", 1)
                    cx.mark(f"oadd{s_}_{r}", "v")

        @blk.tensor
        def _(tensor):
            for s_ in range(NCB):
                xnTo = xnTob[s_ % 2]
                if s_ == 0:
                    cx.wait(tensor, "ldffnw", "ldffnw")
                    cx.wait(tensor, "ldffnw_g", "castw")
                    cx.wait(tensor, "ldffnw_s", "ldffnw")
                cx.wait(tensor, f"ldxn{s_}", f"ldxn{s_ % 2}")
                for hb in range(NHB):
                    for kprev, sem in (
                            (f"guevac{s_}_{hb-2}" if hb >= 2 else
                             (f"guevac{s_-1}_{NHB-2+hb}" if s_ > 0 else None), "s"),
                            (f"hm{s_}_{hb-2}" if hb >= 2 else
                             (f"hm{s_-1}_{NHB-2+hb}" if s_ > 0 else None), "v")):
                        if kprev is not None:
                            cx.wait(tensor, kprev, sem)
                    for ccN in range(8):
                        tensor.matmul(pg_banks[hb % 2][:, 0:TCB], wgTb[:, ccN, hb * 128:(hb + 1) * 128],
                                      xnTo[:, ccN, 1:TCB + 1], start=(ccN == 0), stop=(ccN == 7))
                    for ccN in range(8):
                        inst = tensor.matmul(pu_banks[hb % 2][:, 0:TCB], wuTb[:, ccN, hb * 128:(hb + 1) * 128],
                                             xnTo[:, ccN, 1:TCB + 1], start=(ccN == 0), stop=(ccN == 7))
                    cx.bump(inst, "t", 1)
                    cx.mark(f"gu{s_}_{hb}", "t")
                cx.wait(tensor, f"bmix{s_}", "v")
                cx.wait(tensor, f"ldwr{s_}", "ldwro")
                for db in range(8):
                    kprev = (f"revac{s_}_{db-2}" if db >= 2 else
                             (f"revac{s_-1}_{6+db}" if s_ > 0 else None))
                    if kprev is not None:
                        cx.wait(tensor, kprev, "s")
                    for ccN in range(8):
                        inst = tensor.matmul(pr_banks[db % 2][:, 0:TCB], wrob[:, ccN, db * 128:(db + 1) * 128],
                                             xrT[:, ccN, :], start=(ccN == 0), stop=(ccN == 7))
                    cx.bump(inst, "t", 1)
                    cx.mark(f"rmm{s_}_{db}", "t")
                cx.wait(tensor, f"smul{s_}", "v")
                cx.wait(tensor, f"hm{s_}_{NHB-1}", "v")
                for r in range(4):
                    tb, dh = RUNS[r]
                    if r in (0, 2):
                        cx.wait(tensor, f"ldwo{s_}_{dh}", "ldwro")
                    if s_ > 0 or r >= 2:
                        kw = f"oadd{s_}_{r-2}" if r >= 2 else f"oadd{s_-1}_{r+2}"
                        cx.wait(tensor, kw, "v")
                    for ccN in range(8):
                        tensor.matmul(po_banks[r % 2][:], srT[:, ccN, tb * 128:(tb + 1) * 128],
                                      wrob[:, ccN, dh * 512:(dh + 1) * 512],
                                      start=(ccN == 0), stop=False)
                    for hb in range(NHB):
                        inst = tensor.matmul(po_banks[r % 2][:], hact[:, hb, tb * 128:(tb + 1) * 128],
                                             wdb[:, hb, dh * 512:(dh + 1) * 512],
                                             start=False, stop=(hb == NHB - 1))
                    cx.bump(inst, "t", 1)
                    cx.mark(f"out{s_}_{r}", "t")

        @blk.scalar
        def _(scalar):
            for s_ in range(NCB):
                for hb in range(NHB):
                    cx.wait(scalar, f"gu{s_}_{hb}", "t")
                    if hb > 0:
                        cx.wait(scalar, f"hm{s_}_{hb-1}", "v")
                    elif s_ > 0:
                        cx.wait(scalar, f"hm{s_-1}_{NHB-1}", "v")
                    inst = scalar.activation(sgb[:], pg_banks[hb % 2][:, 0:TCB], AF.Silu)
                    cx.bump(inst, "s", 1)
                    cx.mark(f"guevac{s_}_{hb}", "s")
                for db in range(8):
                    cx.wait(scalar, f"rmm{s_}_{db}", "t")
                    inst = scalar.activation(srT[:, db, :], pr_banks[db % 2][:, 0:TCB], AF.Tanh)
                    cx.bump(inst, "s", 1)
                    cx.mark(f"revac{s_}_{db}", "s")

        @blk.gpsimd
        def _(g):
            for s_ in range(NCB):
                if DEBUG_TAPS and s_ == 0:
                    cx.wait(g, "ldxn0", "ldxn0")
                    cx.tap(g, "xnTo", xnTob[0][:], [128, 8, TCB + 1], BF16)
                    cx.wait(g, "ldffnw", "ldffnw")
                    cx.tap(g, "wgTb", wgTb[:, :, 0:128], [128, 8, 128], BF16)
                    cx.wait(g, "ldwr0", "ldwro")
                    cx.tap(g, "wrob", wrob[:, :, 0:128], [128, 8, 128], BF16)
                    cx.mark("tap_wrob", "std")
                    cx.wait(g, "bmix0", "v")
                    cx.tap(g, "xrT", xrT[:], [128, 8, TCB], BF16)
                    cx.wait(g, "revac0_7", "s")
                    cx.tap(g, "srT", srT[:], [128, 8, TCB], BF16)
                    cx.mark("tapB_sr", "std")
                    cx.wait(g, "smul0", "v")
                    cx.tap(g, "yos", srT[:], [128, 8, TCB], BF16)
                    cx.wait(g, f"hm0_{NHB-1}", "v")
                    cx.tap(g, "hact", hact[:], [128, 24, TCB], BF16)
                    cx.mark("tapsB_done", "std")
                for r in range(4):
                    tb, dh = RUNS[r]
                    cx.wait(g, f"oadd{s_}_{r}", "v")
                    t0 = s_ * TCB
                    dst = out_own[t0 + tb * 128:t0 + (tb + 1) * 128, dh * 512:(dh + 1) * 512]
                    cx.bump(g.dma_start(out=dst, in_=outt[:, r % 2, :]), f"sto{r % 2}", 16)
                    cx.mark(f"owr{s_}_{r}", f"sto{r % 2}")

    sB.close()

    totals = {k: cx.cnt[k] for k in ["sto0", "sto1", "sty", "stx", "stp0", "stp1", "std", "cl"]}
    with nc.Block() as blk:
        @blk.gpsimd
        def _(g):
            for k, v in totals.items():
                if v:
                    g.wait_ge(cx.sem[k], v)

        @blk.sync
        def _(sync):
            for k, v in totals.items():
                if v:
                    sync.wait_ge(cx.sem[k], v)

    return nc, cx.newM


_nc_cache = {}


def _get_program():
    if "nc" not in _nc_cache:
        _, M1 = build({})
        nc, M2 = build(M1)
        assert M1 == M2, "milestone mismatch between emission passes"
        _nc_cache["nc"] = nc
    return _nc_cache["nc"]


def kernel(L0_attn_x, L0_attn_time_decay, L0_attn_time_first,
           L0_attn_time_mix_k, L0_attn_time_mix_v, L0_attn_time_mix_r,
           L0_attn_W_key, L0_attn_W_value, L0_attn_W_receptance, L0_attn_W_output,
           L0_ffn_w_gate, L0_ffn_w_up, L0_ffn_w_down):
    x = np.ascontiguousarray(np.asarray(L0_attn_x, dtype=np.float32))
    wk = np.ascontiguousarray(np.asarray(L0_attn_W_key, np.float32).T)
    wv = np.ascontiguousarray(np.asarray(L0_attn_W_value, np.float32).T)
    wr = np.ascontiguousarray(np.asarray(L0_attn_W_receptance, np.float32).T)
    wo = np.ascontiguousarray(np.asarray(L0_attn_W_output, np.float32).T)
    wg = np.ascontiguousarray(np.asarray(L0_ffn_w_gate, np.int32).T)
    wu = np.ascontiguousarray(np.asarray(L0_ffn_w_up, np.int32).T)
    wd = np.ascontiguousarray(np.asarray(L0_ffn_w_down, np.int32).T)

    def perm(v):
        return np.ascontiguousarray(np.asarray(v, np.float32).reshape(8, 128).T)

    tmk = perm(L0_attn_time_mix_k)
    tmv = perm(L0_attn_time_mix_v)
    tmr = perm(L0_attn_time_mix_r)
    td = perm(L0_attn_time_decay)
    tf = perm(L0_attn_time_first)
    ident = np.eye(128, dtype=ml_dtypes.bfloat16)

    nc = _get_program()
    wk_h = [np.ascontiguousarray(wk[:, h * CH:(h + 1) * CH]) for h in range(2)]
    wv_h = [np.ascontiguousarray(wv[:, h * CH:(h + 1) * CH]) for h in range(2)]
    td_h = [np.ascontiguousarray(td[:, h * NDBL:(h + 1) * NDBL]) for h in range(2)]
    tf_h = [np.ascontiguousarray(tf[:, h * NDBL:(h + 1) * NDBL]) for h in range(2)]
    maps = []
    for core in range(8):
        b, h = core // 2, core % 2
        maps.append({
            "x_all": np.ascontiguousarray(x[b]),
            "x_own": np.ascontiguousarray(x[b, h * OWN:(h + 1) * OWN]),
            "wkT": wk_h[h], "wvT": wv_h[h], "wrT": wr, "woT": wo,
            "wgT": wg, "wuT": wu, "wdT": wd,
            "tmk": tmk, "tmv": tmv, "tmr": tmr, "td": td_h[h], "tf": tf_h[h],
            "ident": ident,
        })
    res = run_bass_kernel_spmd(nc, maps, core_ids=list(range(8)))
    out = np.empty((B, T, C), np.float32)
    for core in range(8):
        b, h = core // 2, core % 2
        out[b, h * OWN:(h + 1) * OWN] = np.asarray(res.results[core]["out_own"])
    return out
